# version 03
# baseline (speedup 1.0000x reference)
import contextlib
import numpy as np
import ml_dtypes

import concourse.bass as bass
import concourse.mybir as mybir
from concourse.bass_utils import run_bass_kernel_spmd

F32 = mybir.dt.float32
BF16 = mybir.dt.bfloat16
AF = mybir.ActivationFunctionType
ALU = mybir.AluOpType
AX = mybir.AxisListType

N_CORES = 8
EPS = 1e-5


ENGINES = ("sync", "tensor", "vector", "scalar", "gpsimd")


class Chan:

    __slots__ = ("sem", "count", "name")

    def __init__(self, sem, name):
        self.sem = sem
        self.count = 0
        self.name = name


class _Rec:
    __slots__ = ("method", "args", "kwargs", "incs")

    def __init__(self, method, args, kwargs):
        self.method = method
        self.args = args
        self.kwargs = kwargs
        self.incs = []


class _RecHandle:
    __slots__ = ("rec", "_chans")

    def __init__(self, rec):
        self.rec = rec

    def inc(self, ch, n=1):
        self.rec.incs.append((ch.sem, n))
        ch.count += n
        return self


_NOSERIAL = {"wait_ge", "dma_start", "engine_nop", "nop", "indirect_dma_start",
             "collective_compute", "partition_id", "reg_save", "dma_start_transpose"}
_SERIAL_ENGINES = {"vector", "scalar", "gpsimd"}


class _EngRec:
    def __init__(self, prog, name):
        self._prog = prog
        self._name = name

    def __getattr__(self, method):
        prog, name = self._prog, self._name

        def call(*args, **kwargs):
            sd = kwargs.pop("_sd", 2)
            serial = name in _SERIAL_ENGINES and method not in _NOSERIAL
            if serial and sd > 0:
                sc = prog.self_chan[name]
                v = sc.count - (sd - 1)
                if v > 0:
                    prog.ops[name].append(_Rec("wait_ge", (sc.sem, v), {}))
            rec = _Rec(method, args, kwargs)
            prog.ops[name].append(rec)
            h = _RecHandle(rec)
            if serial:
                h.inc(prog.self_chan[name])
            return h

        return call

    def wait(self, ch, value=None):
        v = ch.count if value is None else value
        if v <= 0:
            return
        rec = _Rec("wait_ge", (ch.sem, v), {})
        self._prog.ops[self._name].append(rec)


class Prog:

    def __init__(self, nc, es):
        self.nc = nc
        self.es = es
        self.alloc_es = es
        self.prefix = ""
        self.ops = {e: [] for e in ENGINES}
        self.chans = []
        self.self_chan = {}
        for e in sorted(_SERIAL_ENGINES):
            self.self_chan[e] = self.chan(f"self_{e}")
        self.q = _EngRec(self, "sync")
        self.t = _EngRec(self, "tensor")
        self.v = _EngRec(self, "vector")
        self.s = _EngRec(self, "scalar")
        self.g = _EngRec(self, "gpsimd")
        self.cv = self.self_chan["vector"]
        self.cs = self.self_chan["scalar"]
        self.cg = self.self_chan["gpsimd"]

    def chan(self, name):
        sem = self.es.enter_context(self.nc.semaphore(name))
        c = Chan(sem, name)
        self.chans.append(c)
        return c

    def sbuf(self, name, shape, dtype):
        return self.alloc_es.enter_context(
            self.nc.sbuf_tensor(self.prefix + name, shape, dtype))

    def psum(self, name, shape, dtype):
        return self.alloc_es.enter_context(
            self.nc.psum_tensor(self.prefix + name, shape, dtype))

    def barrier(self, engines="tqvsg"):
        m = {"q": self.q, "t": self.t, "v": self.v, "s": self.s, "g": self.g}
        for e in engines:
            for c in self.chans:
                m[e].wait(c)

    def finalize(self):
        block = self.es.enter_context(self.nc.Block())
        for eng_name in ENGINES:
            ops = self.ops[eng_name]
            if not ops:
                continue

            def mk(ops):
                def body(eng):
                    for rec in ops:
                        instr = getattr(eng, rec.method)(*rec.args, **rec.kwargs)
                        for sem, n in rec.incs:
                            instr.then_inc(sem, n)

                return body

            getattr(block, eng_name)(mk(ops))



def emit_ln_stats(p, x_ap, width, sc, col0, scr, eps=EPS):
    msum = sc[:, col0 + 0: col0 + 1]
    nmu = sc[:, col0 + 1: col0 + 2]
    vsum = sc[:, col0 + 2: col0 + 3]
    rcol = sc[:, col0 + 3: col0 + 4]
    nmr = sc[:, col0 + 4: col0 + 5]
    p.s.activation(out=scr[:, 0:width], in_=x_ap, func=AF.Identity,
                   accum_out=msum)
    p.v.wait(p.cs)
    p.v.tensor_scalar(out=nmu, in0=msum, scalar1=-1.0 / width, scalar2=None,
                      op0=ALU.mult)
    p.s.wait(p.cv)
    p.s.activation(out=scr[:, 0:width], in_=x_ap, func=AF.Square, bias=nmu,
                   scale=1.0, accum_out=vsum)
    p.v.wait(p.cs)
    p.v.tensor_scalar(out=vsum, in0=vsum, scalar1=1.0 / width, scalar2=eps,
                      op0=ALU.mult, op1=ALU.add)
    p.s.wait(p.cv)
    p.s.activation(out=rcol, in_=vsum, func=AF.Sqrt)
    p.v.wait(p.cs)
    p.v.reciprocal(out=rcol, in_=rcol)
    p.v.scalar_tensor_tensor(out=nmr, in0=nmu, scalar=1.0, in1=rcol,
                             op0=ALU.mult, op1=ALU.mult, _sd=1)
    return nmu, rcol, nmr




def dram3(x2d, p=128):
    return x2d.rearrange("(g p) w -> p g w", p=p)


def sb3(t, g, w):
    return t[:, :].rearrange("p (g w) -> p g w", g=g)


def build_l0ffn(n_steps=32, rows=256):
    D = 1024
    E3 = 3072
    KCH = D // 128
    RT = rows // 128

    nc = bass.Bass("TRN2", target_bir_lowering=False)
    xs = nc.dram_tensor("xs", [n_steps, rows, D], F32, kind="ExternalInput").ap()
    wtx = nc.dram_tensor("wtx", [D, E3], F32, kind="ExternalInput").ap()
    ut = nc.dram_tensor("ut", [D, E3], F32, kind="ExternalInput").ap()
    b3 = nc.dram_tensor("b3", [1, E3], F32, kind="ExternalInput").ap()
    wot = nc.dram_tensor("wot", [D, D], F32, kind="ExternalInput").ap()
    uot = nc.dram_tensor("uot", [D, D], F32, kind="ExternalInput").ap()
    bo = nc.dram_tensor("bo", [1, D], F32, kind="ExternalInput").ap()
    in_c = nc.dram_tensor("in_c", [1, D], F32, kind="ExternalInput").ap()
    ident_in = nc.dram_tensor("ident", [128, 128], BF16, kind="ExternalInput").ap()
    xs2 = nc.dram_tensor("xs2", [n_steps, rows, D], F32, kind="ExternalOutput").ap()

    xnfm = nc.dram_tensor("xnfm", [n_steps, D, rows], BF16, kind="Internal").ap()
    hbuf = nc.dram_tensor("hbuf", [n_steps, rows, D], F32, kind="Internal").ap()

    es = contextlib.ExitStack()
    with es:
        p = Prog(nc, es)
        wbig = p.sbuf("wbig", [128, 2 * KCH * E3], BF16)
        b3r = p.sbuf("b3r", [1, E3], BF16)
        bor = p.sbuf("bor", [1, D], BF16)
        onesr = p.sbuf("onesr", [1, rows], BF16)
        ident = p.sbuf("identS", [128, 128], BF16)
        incol = p.sbuf("incol", [128, KCH], F32)
        onesT = p.sbuf("onesT", [128, rows], F32)
        cfm = p.sbuf("cfm", [128, KCH * rows], F32)
        cbf = p.sbuf("cbf", [128, KCH * rows], BF16)
        fsb = p.sbuf("fsb", [128, KCH * rows], F32)
        isb = p.sbuf("isb", [128, KCH * rows], F32)
        gsb = p.sbuf("gsb", [128, KCH * rows], F32)
        tmp1 = p.sbuf("tmp1", [128, rows], F32)
        tmp2 = p.sbuf("tmp2", [128, rows], F32)
        xraw = [p.sbuf(f"xraw{i}", [128, RT * D], F32) for i in range(2)]
        xtm = p.sbuf("xtm", [128, RT * D], BF16)
        xfm = [p.sbuf(f"xfm{i}", [128, KCH * rows], BF16) for i in range(2)]
        scr = p.sbuf("scr", [128, D], BF16)
        sc = p.sbuf("sc", [128, 16], F32)
        cuo = p.sbuf("cuo", [128, RT * D], F32)
        htile = p.sbuf("htile", [128, RT * D], F32)
        ps = [p.psum(f"ps{i}", [128, 512], F32) for i in range(6)]
        pt = [p.psum(f"pt{i}", [128, 128], BF16) for i in range(2)]

        chq = p.chan("chq")
        chg = p.chan("chg")
        chmm = p.chan("chmm")
        chst = p.chan("chst")

        p.g.dma_start(out=ident[:, :], in_=ident_in[:, :]).inc(chg, 16)
        p.g.dma_start(out=b3r[0:1, :], in_=b3[:, :]).inc(chg, 16)
        p.g.dma_start(out=bor[0:1, :], in_=bo[:, :]).inc(chg, 16)
        p.g.dma_start(out=incol[:, 0:KCH],
                      in_=in_c.rearrange("one (k p) -> (one p) k", p=128),
                      allow_slow_non_contiguous=True).inc(chg, 16)
        for k in range(KCH):
            p.g.dma_start(out=wbig[:, k * E3:(k + 1) * E3],
                          in_=wtx[k * 128:(k + 1) * 128, :]).inc(chg, 16)
            p.g.dma_start(out=wbig[:, (KCH + k) * E3:(KCH + k + 1) * E3],
                          in_=ut[k * 128:(k + 1) * 128, :]).inc(chg, 16)
        p.v.memset(onesr[0:1, :], 1.0)
        p.v.memset(onesT[:, :], 1.0)
        p.s.wait(chg)
        p.s.wait(p.cv)
        for k in range(KCH):
            p.s.activation(out=cfm[:, k * rows:(k + 1) * rows], in_=onesT[:, 0:rows],
                           func=AF.Identity, scale=incol[:, k:k + 1])
        p.v.wait(p.cs)
        p.v.tensor_copy(out=cbf[:, :], in_=cfm[:, :])
        cbf_ready = p.cv.count
        p.t.wait(chg)
        p.t.wait(p.cv)

        tr_slot_ev = [0, 0]
        for t in range(n_steps):
            buf = xraw[t % 2]
            p.q.wait(p.cs)
            p.q.wait(p.cv)
            p.q.dma_start(out=sb3(buf, RT, D), in_=dram3(xs[t])).inc(chq, 16)
            p.s.wait(chq)
            p.v.wait(chq)
            p.v.wait(chmm)
            for rt in range(RT):
                src0 = buf[:, rt * D:(rt + 1) * D]
                nmu, rcol, nmr = emit_ln_stats(p, src0, D, sc, 0 if rt == 0 else 8, scr)
                p.v.tensor_scalar(out=xtm[:, rt * D:(rt + 1) * D], in0=src0,
                                  scalar1=rcol, scalar2=nmr, op0=ALU.mult,
                                  op1=ALU.add, _sd=1)
            ln_done = p.cv.count
            fbuf = xfm[t % 2]
            for k in range(KCH):
                for rt in range(RT):
                    slot = (k * RT + rt) % 2
                    bank = pt[slot]
                    p.t.wait(p.cv, ln_done)
                    p.t.wait(p.cs, tr_slot_ev[slot])
                    p.t.transpose(bank[:, :],
                                  xtm[:, rt * D + k * 128: rt * D + (k + 1) * 128],
                                  ident[:, :]).inc(chmm)
                    p.s.wait(chmm)
                    p.s.activation(out=fbuf[:, k * rows + rt * 128: k * rows + (rt + 1) * 128],
                                   in_=bank[:, :], func=AF.Identity)
                    tr_slot_ev[slot] = p.cs.count
            p.g.wait(p.cs)
            p.g.dma_start(out=dram3(xnfm[t]),
                          in_=sb3(fbuf, KCH, rows)).inc(chg, 16)

        p.barrier("tqvsg")

        slot_ev = [0] * 6
        for t in range(n_steps):
            fbuf = xfm[t % 2]
            p.q.wait(chmm)
            p.q.dma_start(out=sb3(fbuf, KCH, rows), in_=dram3(xnfm[t])).inc(chq, 16)
            xload = chq.count
            cupd = p.cv.count
            for pair in range(12):
                slot = pair % 6
                bank = ps[slot]
                p.t.wait(p.cs, slot_ev[slot])
                if pair == 0:
                    p.t.wait(chq, xload)
                    p.t.wait(p.cv, cupd if t > 0 else cbf_ready)
                for half in range(2):
                    ec = pair * 2 + half
                    outap = bank[:, half * rows:(half + 1) * rows]
                    p.t.matmul(outap, b3r[0:1, ec * 128:(ec + 1) * 128],
                               onesr[0:1, 0:rows],
                               start=(half == 0), stop=False)
                    for k in range(KCH):
                        p.t.matmul(outap,
                                   wbig[:, k * E3 + ec * 128: k * E3 + (ec + 1) * 128],
                                   fbuf[:, k * rows:(k + 1) * rows], start=False,
                                   stop=False)
                    for k in range(KCH):
                        h = p.t.matmul(outap,
                                       wbig[:, (KCH + k) * E3 + ec * 128: (KCH + k) * E3 + (ec + 1) * 128],
                                       cbf[:, k * rows:(k + 1) * rows], start=False,
                                       stop=(half == 1 and k == KCH - 1))
                h.inc(chmm)
                p.s.wait(chmm)
                for half in range(2):
                    ec = pair * 2 + half
                    outap = bank[:, half * rows:(half + 1) * rows]
                    func = AF.Sigmoid if ec < 16 else AF.Tanh
                    dst = fsb if ec < 8 else (isb if ec < 16 else gsb)
                    j = ec % 8
                    p.s.activation(out=dst[:, j * rows:(j + 1) * rows], in_=outap,
                                   func=func)
                slot_ev[slot] = p.cs.count
            mm_all = chmm.count
            ev_all = p.cs.count
            p.v.wait(p.cs, ev_all)
            p.v.wait(chmm, mm_all)
            for j in range(KCH):
                sl = slice(j * rows, (j + 1) * rows)
                p.v.tensor_tensor(out=tmp1[:, 0:rows], in0=isb[:, sl], in1=gsb[:, sl],
                                  op=ALU.mult)
                p.v.tensor_tensor(out=tmp2[:, 0:rows], in0=fsb[:, sl], in1=cfm[:, sl],
                                  op=ALU.mult)
                p.v.tensor_tensor(out=cfm[:, sl], in0=tmp2[:, 0:rows],
                                  in1=tmp1[:, 0:rows], op=ALU.add, _sd=1)
                p.v.tensor_copy(out=cbf[:, sl], in_=cfm[:, sl], _sd=1)

        p.barrier("tqvsg")

        for k in range(KCH):
            p.g.dma_start(out=wbig[:, k * D:(k + 1) * D],
                          in_=wot[k * 128:(k + 1) * 128, :]).inc(chg, 16)
            p.g.dma_start(out=wbig[:, (KCH + k) * D:(KCH + k + 1) * D],
                          in_=uot[k * 128:(k + 1) * 128, :]).inc(chg, 16)
        wload = chg.count
        for rt in range(RT):
            for ot in range(2):
                bank = ps[rt * 2 + ot]
                outap = bank[:, 0:512]
                p.t.wait(chg, wload)
                p.t.matmul(outap, onesr[0:1, 0:128], bor[0:1, ot * 512:(ot + 1) * 512],
                           start=True, stop=False)
                for k in range(KCH):
                    h = p.t.matmul(outap,
                                   cbf[:, k * rows + rt * 128: k * rows + (rt + 1) * 128],
                                   wbig[:, (KCH + k) * D + ot * 512: (KCH + k) * D + (ot + 1) * 512],
                                   start=False, stop=(k == KCH - 1))
                h.inc(chmm)
                p.v.wait(chmm)
                p.v.tensor_copy(out=cuo[:, rt * D + ot * 512: rt * D + (ot + 1) * 512],
                                in_=outap)
        cuo_ready = p.cv.count

        hslot_dv = [0] * 4
        for t in range(n_steps):
            fbuf = xfm[t % 2]
            p.q.wait(chmm)
            p.q.dma_start(out=sb3(fbuf, KCH, rows), in_=dram3(xnfm[t])).inc(chq, 16)
            xload = chq.count
            p.v.wait(chg)
            for rt in range(RT):
                for ot in range(2):
                    slot = rt * 2 + ot
                    bank = ps[slot]
                    outap = bank[:, 0:512]
                    p.t.wait(chq, xload)
                    p.t.wait(p.cv, hslot_dv[slot] if t > 0 else cuo_ready)
                    for k in range(KCH):
                        h = p.t.matmul(outap,
                                       fbuf[:, k * rows + rt * 128: k * rows + (rt + 1) * 128],
                                       wbig[:, k * D + ot * 512: k * D + (ot + 1) * 512],
                                       start=(k == 0), stop=(k == KCH - 1))
                    h.inc(chmm)
                    hsl = slice(rt * D + ot * 512, rt * D + (ot + 1) * 512)
                    p.v.wait(chmm)
                    p.v.tensor_tensor(out=htile[:, hsl], in0=outap, in1=cuo[:, hsl],
                                      op=ALU.add)
                    hslot_dv[slot] = p.cv.count
                    p.s.wait(p.cv)
                    p.s.activation(out=htile[:, hsl], in_=htile[:, hsl],
                                   func=AF.Tanh)
            p.g.wait(p.cs)
            p.g.dma_start(out=dram3(hbuf[t]), in_=sb3(htile, RT, D)).inc(chg, 16)

        p.barrier("tqvsg")

        for t in range(n_steps):
            hb = xraw[t % 2]
            p.q.wait(p.cs)
            p.q.wait(p.cv)
            p.q.dma_start(out=sb3(hb, RT, D), in_=dram3(hbuf[t])).inc(chq, 16)
            p.q.dma_start(out=sb3(cuo, RT, D), in_=dram3(xs[t])).inc(chq, 16)
            p.s.wait(chq)
            p.v.wait(chq)
            p.v.wait(chst)
            for rt in range(RT):
                src0 = hb[:, rt * D:(rt + 1) * D]
                nmu, rcol, nmr = emit_ln_stats(p, src0, D, sc, 0 if rt == 0 else 8, scr)
                hsl = slice(rt * D, (rt + 1) * D)
                p.v.tensor_scalar(out=htile[:, hsl], in0=cuo[:, hsl], scalar1=nmr,
                                  scalar2=None, op0=ALU.add, _sd=1)
                p.v.scalar_tensor_tensor(out=htile[:, hsl], in0=src0, scalar=rcol,
                                         in1=htile[:, hsl], op0=ALU.mult,
                                         op1=ALU.add, _sd=1)
            p.g.wait(p.cv)
            p.g.dma_start(out=dram3(xs2[t]), in_=sb3(htile, RT, D)).inc(chst, 16)

        p.g.wait(chst)
        p.finalize()
    return nc



def _mk_chans(p):
    return {
        "chq": p.chan("chq"), "chg": p.chan("chg"), "chw": p.chan("chw"),
        "chws": p.chan("chws"), "chy": p.chan("chy"),
        "cht": [p.chan("cht0"), p.chan("cht1")],
        "chh": [p.chan("chh0"), p.chan("chh1")],
        "chmm": p.chan("chmm"), "chst": p.chan("chst"),
    }



def emit_l1attn(nc, p, chs, xs, wqkv, proj, wout, bout, ident_in, xs3, n_b=4, S=2048):
    D = 1024
    H, HD, FEAT = 16, 64, 256
    KCH = D // 128
    TT = S // 128

    xnfm = p.sbuf("xnfm", [128, KCH * S], BF16)
    kfm = p.sbuf("kfm", [128, KCH * S], BF16)
    vtm = p.sbuf("vtm", [128, TT * D], BF16)
    qfm = p.sbuf("qfm", [128, KCH * S], BF16)
    wsb = p.sbuf("wsb", [128, KCH * D], BF16)
    projsb = p.sbuf("projsb", [128, H * FEAT], BF16)
    boutr = p.sbuf("boutr", [1, D], BF16)
    onesr = p.sbuf("onesr", [1, 128], BF16)
    onescol = p.sbuf("onescol", [128, 1], BF16)
    ident = p.sbuf("identS", [128, 128], BF16)
    kproj = p.sbuf("kproj", [128, TT * FEAT], BF16)
    qproj = p.sbuf("qproj", [128, 2 * S], BF16)
    kvsb = p.sbuf("kvsb", [128, 2 * (HD + 1)], BF16)
    xtile = p.sbuf("xtile", [128, D], F32)
    xtm = p.sbuf("xtm", [128, D], BF16)
    scr = p.sbuf("scr", [128, D], BF16)
    scr1 = p.sbuf("scr1", [128, 512], F32)
    scr2 = p.sbuf("scr2", [128, 512], F32)
    rtile = p.sbuf("rtile", [128, D], F32)
    sc = p.sbuf("sc", [128, 16], F32)
    rc = p.sbuf("rc", [128, 1], F32)
    pf = [p.psum(f"pf{i}", [128, 512], F32) for i in range(4)]
    ptr = [p.psum(f"ptr{i}", [128, 128], BF16) for i in range(2)]

    chq = chs["chq"]
    chg = chs["chg"]
    chmm = chs["chmm"]
    chst = chs["chst"]

    p.g.dma_start(out=ident[:, :], in_=ident_in[:, :]).inc(chg, 16)
    p.g.dma_start(out=boutr[0:1, :], in_=bout[:, :]).inc(chg, 16)
    for h in range(H):
        p.g.dma_start(out=projsb[0:64, h * FEAT:(h + 1) * FEAT],
                      in_=proj[h]).inc(chg, 16)
        p.g.dma_start(out=projsb[64:128, h * FEAT:(h + 1) * FEAT],
                      in_=proj[h]).inc(chg, 16)
    p.v.memset(onesr[0:1, :], 1.0)
    p.v.memset(onescol[:, 0:1], 1.0)
    p.t.wait(chg)
    p.t.wait(p.cv)

    pf_slot = [0] * 4
    ptr_slot = [0] * 2

    def tr_tile(src_bf_ap, dst_ap, i):
        slot = i % 2
        bank = ptr[slot]
        p.t.wait(p.cs, ptr_slot[slot])
        p.t.transpose(bank[:, :], src_bf_ap, ident[:, :]).inc(chmm)
        p.s.wait(chmm)
        p.s.activation(out=dst_ap, in_=bank[:, :], func=AF.Identity)
        ptr_slot[slot] = p.cs.count

    for b in range(n_b):
        for tt in range(TT):
            p.q.wait(p.cs)
            p.q.wait(p.cv)
            p.q.wait(chmm)
            p.q.wait(chst)
            p.q.dma_start(out=xtile[:, :], in_=xs[b, tt * 128:(tt + 1) * 128, :]).inc(chq, 16)
            p.s.wait(chq)
            p.v.wait(chq)
            nmu, rcol, nmr = emit_ln_stats(p, xtile[:, :], D, sc, 0, scr)
            p.v.wait(chmm)
            p.v.tensor_scalar(out=xtm[:, :], in0=xtile[:, :], scalar1=rcol,
                              scalar2=nmr, op0=ALU.mult, op1=ALU.add, _sd=1)
            lnv = p.cv.count
            p.t.wait(p.cv, lnv)
            for k in range(KCH):
                tr_tile(xtm[:, k * 128:(k + 1) * 128],
                        xnfm[:, k * S + tt * 128: k * S + (tt + 1) * 128], k)
        xn_done = p.cs.count

        for (dst, col0, fm) in ((kfm, D, True), (vtm, 2 * D, False), (qfm, 0, True)):
            p.g.wait(chmm)
            for k in range(KCH):
                p.g.dma_start(out=wsb[:, k * D:(k + 1) * D],
                              in_=wqkv[k * 128:(k + 1) * 128, col0:col0 + D]).inc(chg, 16)
            wl = chg.count
            if fm:
                for ec in range(KCH):
                    for t5 in range(S // 512):
                        slot = (ec * (S // 512) + t5) % 4
                        bank = pf[slot]
                        p.t.wait(p.cv, pf_slot[slot])
                        p.t.wait(chg, wl)
                        p.t.wait(p.cs, xn_done)
                        for k in range(KCH):
                            hm = p.t.matmul(bank[:, :],
                                            wsb[:, k * D + ec * 128: k * D + (ec + 1) * 128],
                                            xnfm[:, k * S + t5 * 512: k * S + (t5 + 1) * 512],
                                            start=(k == 0), stop=(k == KCH - 1))
                        hm.inc(chmm)
                        p.v.wait(chmm)
                        p.v.tensor_copy(out=dst[:, ec * S + t5 * 512: ec * S + (t5 + 1) * 512],
                                        in_=bank[:, :])
                        pf_slot[slot] = p.cv.count
            else:
                for tt in range(TT):
                    for nn in range(2):
                        slot = (tt * 2 + nn) % 4
                        bank = pf[slot]
                        p.t.wait(p.cv, pf_slot[slot])
                        p.t.wait(chg, wl)
                        p.t.wait(p.cs, xn_done)
                        for k in range(KCH):
                            hm = p.t.matmul(bank[:, :],
                                            xnfm[:, k * S + tt * 128: k * S + (tt + 1) * 128],
                                            wsb[:, k * D + nn * 512: k * D + (nn + 1) * 512],
                                            start=(k == 0), stop=(k == KCH - 1))
                        hm.inc(chmm)
                        p.v.wait(chmm)
                        p.v.tensor_copy(out=dst[:, tt * D + nn * 512: tt * D + (nn + 1) * 512],
                                        in_=bank[:, :])
                        pf_slot[slot] = p.cv.count
        qkv_done = p.cv.count

        for h in range(H):
            base = (h % 2) * 64
            ch = h // 2
            for tt in range(TT):
                slot = tt % 4
                bank = pf[slot]
                p.t.wait(p.cv, pf_slot[slot])
                p.t.wait(p.cv, qkv_done)
                hm = p.t.matmul(bank[:, 0:FEAT],
                                kfm[base:base + 64, ch * S + tt * 128: ch * S + (tt + 1) * 128],
                                projsb[base:base + 64, h * FEAT:(h + 1) * FEAT],
                                start=True, stop=True)
                hm.inc(chmm)
                p.v.wait(chmm)
                p.v.tensor_scalar(out=scr1[:, 0:FEAT], in0=bank[:, 0:FEAT],
                                  scalar1=0.0, scalar2=None, op0=ALU.min)
                p.s.wait(p.cv)
                p.s.activation(out=scr2[:, 0:FEAT], in_=scr1[:, 0:FEAT], func=AF.Exp)
                p.v.wait(p.cs)
                p.v.scalar_tensor_tensor(out=kproj[:, tt * FEAT:(tt + 1) * FEAT],
                                         in0=bank[:, 0:FEAT], scalar=0.0,
                                         in1=scr2[:, 0:FEAT], op0=ALU.max, op1=ALU.add)
                pf_slot[slot] = p.cv.count
            kp_done = p.cv.count
            for fc in range(2):
                slot_kv = fc % 2
                bank = pf[slot_kv]
                bank2 = pf[2 + slot_kv]
                p.t.wait(p.cv, pf_slot[slot_kv])
                p.t.wait(p.cv, pf_slot[2 + slot_kv])
                p.t.wait(p.cv, kp_done)
                for tt in range(TT):
                    hm1 = p.t.matmul(bank[:, 0:HD],
                                     kproj[:, tt * FEAT + fc * 128: tt * FEAT + (fc + 1) * 128],
                                     vtm[:, tt * D + h * HD: tt * D + (h + 1) * HD],
                                     start=(tt == 0), stop=(tt == TT - 1))
                hm1.inc(chmm)
                for tt in range(TT):
                    hm2 = p.t.matmul(bank2[:, 0:1],
                                     kproj[:, tt * FEAT + fc * 128: tt * FEAT + (fc + 1) * 128],
                                     onescol[:, 0:1],
                                     start=(tt == 0), stop=(tt == TT - 1))
                hm2.inc(chmm)
                p.v.wait(chmm)
                p.v.tensor_copy(out=kvsb[:, fc * (HD + 1): fc * (HD + 1) + HD],
                                in_=bank[:, 0:HD])
                p.v.tensor_copy(out=kvsb[:, fc * (HD + 1) + HD: (fc + 1) * (HD + 1)],
                                in_=bank2[:, 0:1])
                pf_slot[slot_kv] = p.cv.count
                pf_slot[2 + slot_kv] = p.cv.count
            kv_done = p.cv.count

            for t5 in range(S // 512):
                for fc in range(2):
                    slot = (t5 * 2 + fc) % 4
                    bank = pf[slot]
                    p.t.wait(p.cv, pf_slot[slot])
                    p.t.wait(p.cv, qkv_done)
                    hm = p.t.matmul(bank[:, 0:512],
                                    projsb[base:base + 64, h * FEAT + fc * 128: h * FEAT + (fc + 1) * 128],
                                    qfm[base:base + 64, ch * S + t5 * 512: ch * S + (t5 + 1) * 512],
                                    start=True, stop=True)
                    hm.inc(chmm)
                    p.v.wait(chmm)
                    p.v.tensor_scalar(out=scr1[:, 0:512], in0=bank[:, 0:512],
                                      scalar1=0.0, scalar2=None, op0=ALU.min)
                    p.s.wait(p.cv)
                    p.s.activation(out=scr2[:, 0:512], in_=scr1[:, 0:512], func=AF.Exp)
                    p.v.wait(p.cs)
                    p.v.scalar_tensor_tensor(
                        out=qproj[:, fc * S + t5 * 512:fc * S + (t5 + 1) * 512],
                        in0=bank[:, 0:512], scalar=0.0,
                        in1=scr2[:, 0:512], op0=ALU.max, op1=ALU.add)
                    pf_slot[slot] = p.cv.count
            qp_done = p.cv.count
            for tt in range(TT):
                slot = tt % 4
                bank = pf[slot]
                p.t.wait(p.cv, pf_slot[slot])
                p.t.wait(p.cv, qp_done)
                p.t.wait(p.cv, kv_done)
                for fc in range(2):
                    hm = p.t.matmul(bank[:, 0:HD + 1],
                                    qproj[:, fc * S + tt * 128: fc * S + (tt + 1) * 128],
                                    kvsb[:, fc * (HD + 1):(fc + 1) * (HD + 1)],
                                    start=(fc == 0), stop=(fc == 1))
                hm.inc(chmm)
                p.v.wait(chmm)
                p.v.tensor_scalar(out=rc[:, 0:1], in0=bank[:, HD:HD + 1],
                                  scalar1=1e-8, scalar2=None, op0=ALU.add, _sd=1)
                p.v.reciprocal(out=rc[:, 0:1], in_=rc[:, 0:1], _sd=1)
                p.v.tensor_scalar(out=xnfm[:, tt * D + h * HD: tt * D + (h + 1) * HD],
                                  in0=bank[:, 0:HD], scalar1=rc[:, 0:1],
                                  scalar2=None, op0=ALU.mult, _sd=1)
                pf_slot[slot] = p.cv.count
        o_done = p.cv.count

        p.g.wait(chmm)
        for k in range(KCH):
            p.g.dma_start(out=wsb[:, k * D:(k + 1) * D],
                          in_=wout[k * 128:(k + 1) * 128, :]).inc(chg, 16)
        wl = chg.count
        for tt in range(TT):
            p.t.wait(p.cv, o_done)
            for k in range(KCH):
                tr_tile(xnfm[:, tt * D + k * 128: tt * D + (k + 1) * 128],
                        kfm[:, k * 128:(k + 1) * 128], k)
            ofm_done = p.cs.count
            for nn in range(2):
                slot = nn % 2
                bank = pf[slot]
                p.t.wait(p.cv, pf_slot[slot])
                p.t.wait(p.cs, ofm_done)
                p.t.wait(chg, wl)
                p.t.matmul(bank[:, 0:512], onesr[0:1, 0:128],
                           boutr[0:1, nn * 512:(nn + 1) * 512], start=True, stop=False)
                for k in range(KCH):
                    hm = p.t.matmul(bank[:, 0:512],
                                    kfm[:, k * 128:(k + 1) * 128],
                                    wsb[:, k * D + nn * 512: k * D + (nn + 1) * 512],
                                    start=False, stop=(k == KCH - 1))
                hm.inc(chmm)
                p.v.wait(chmm)
                p.v.tensor_copy(out=rtile[:, nn * 512:(nn + 1) * 512], in_=bank[:, 0:512])
                pf_slot[slot] = p.cv.count
            p.q.wait(p.cs)
            p.q.wait(p.cv)
            p.q.wait(chst)
            p.q.dma_start(out=xtile[:, :], in_=xs[b, tt * 128:(tt + 1) * 128, :]).inc(chq, 16)
            p.s.wait(p.cv)
            nmu, rcol, nmr = emit_ln_stats(p, rtile[:, :], D, sc, 8, scr)
            p.v.wait(chq)
            p.v.wait(chst)
            p.v.tensor_scalar(out=scr1[:, 0:512], in0=xtile[:, 0:512], scalar1=nmr,
                              scalar2=None, op0=ALU.add, _sd=1)
            p.v.tensor_scalar(out=scr2[:, 0:512], in0=xtile[:, 512:1024], scalar1=nmr,
                              scalar2=None, op0=ALU.add, _sd=1)
            p.v.scalar_tensor_tensor(out=xtile[:, 0:512], in0=rtile[:, 0:512],
                                     scalar=rcol, in1=scr1[:, 0:512],
                                     op0=ALU.mult, op1=ALU.add)
            p.v.scalar_tensor_tensor(out=xtile[:, 512:1024], in0=rtile[:, 512:1024],
                                     scalar=rcol, in1=scr2[:, 0:512],
                                     op0=ALU.mult, op1=ALU.add)
            p.g.wait(p.cv)
            p.g.dma_start(out=xs3[b, tt * 128:(tt + 1) * 128, :],
                          in_=xtile[:, :]).inc(chst, 16)


def build_l1attn(n_b=4, S=2048):
    D = 1024
    H, HD, FEAT = 16, 64, 256
    nc = bass.Bass("TRN2", target_bir_lowering=False)
    xs = nc.dram_tensor("xs", [n_b, S, D], F32, kind="ExternalInput").ap()
    wqkv = nc.dram_tensor("wqkv", [D, 3 * D], F32, kind="ExternalInput").ap()
    proj = nc.dram_tensor("proj", [H, HD, FEAT], F32, kind="ExternalInput").ap()
    wout = nc.dram_tensor("wout", [D, D], F32, kind="ExternalInput").ap()
    bout = nc.dram_tensor("bout", [1, D], F32, kind="ExternalInput").ap()
    ident_in = nc.dram_tensor("ident", [128, 128], BF16, kind="ExternalInput").ap()
    xs3 = nc.dram_tensor("xs3", [n_b, S, D], F32, kind="ExternalOutput").ap()

    es = contextlib.ExitStack()
    with es:
        p = Prog(nc, es)
        chs = _mk_chans(p)
        emit_l1attn(nc, p, chs, xs, wqkv, proj, wout, bout, ident_in, xs3, n_b, S)
        p.g.wait(chs["chst"])
        p.finalize()
    return nc


def _host_l1attn_inputs(x2_full, P):
    wqkv = np.ascontiguousarray(P["L1_attn_w_qkv"])
    proj = np.ascontiguousarray(P["L1_attn_proj_matrix"])
    wout = np.ascontiguousarray(P["L1_attn_w_out"])
    bout = np.ascontiguousarray(P["L1_attn_b_out"][None, :]).astype(np.float32)
    ident = _ident128()
    in_maps = []
    for c in range(N_CORES):
        in_maps.append({
            "xs": np.ascontiguousarray(x2_full[c * 4:(c + 1) * 4]),
            "wqkv": wqkv, "proj": proj, "wout": wout, "bout": bout, "ident": ident,
        })
    return in_maps


def run_l1attn(x2_full, P, n_b=4):
    nc = build_l1attn(n_b=n_b)
    in_maps = _host_l1attn_inputs(x2_full, P)
    res = run_bass_kernel_spmd(nc, in_maps, core_ids=list(range(N_CORES)))
    x3 = np.empty((N_CORES * n_b, 2048, 1024), np.float32)
    for c in range(N_CORES):
        x3[c * n_b:(c + 1) * n_b] = res.results[c]["xs3"]
    return x3




def emit_l1ffn_a(nc, p, chs, xs, g1, wdown, wqkvc, wup, cosr, sinr, cmask, ident_in, xp, n_b=4, S=2048):
    D = 1024
    L = 512
    H, HD = 16, 32
    KCH = D // 128
    LCH = L // 128
    TT = S // 128
    NP = H // 2
    RS = 0.17677669529663687

    qaug = p.sbuf("qaug", [128, TT * H * 33], BF16)
    kaug = p.sbuf("kaug", [128, TT * H * 33], BF16)
    qfm = [p.sbuf(f"qfm{i}", [128, S], BF16) for i in range(NP)]
    kfm = [p.sbuf(f"kfm{i}", [128, S], BF16) for i in range(NP)]
    vsb = p.sbuf("vsb", [128, TT * H * 33], BF16)
    otm = p.sbuf("otm", [128, TT * L], BF16)
    cossb = p.sbuf("cossb", [128, 256], BF16)
    sinsb = p.sbuf("sinsb", [128, 256], BF16)
    wdsb = p.sbuf("wdsb", [128, KCH * L], BF16)
    wqsb = p.sbuf("wqsb", [128, LCH * 3 * L], BF16)
    wupsb = p.sbuf("wupsb", [128, LCH * D], BF16)
    cmsk = p.sbuf("cmsk", [128, 4 * 512], F32)
    ident = p.sbuf("identS", [128, 128], BF16)
    g1col = p.sbuf("g1col", [128, KCH], F32)
    qn = p.sbuf("qn", [128, TT * H], F32)
    knm = p.sbuf("knm", [128, H], F32)
    knmb = p.sbuf("knmb", [128, H], BF16)
    kmaxb = p.sbuf("kmaxb", [128, H], F32)
    kmrow = p.sbuf("kmrow", [1, 64], BF16)
    kred = p.sbuf("kred", [64, 128], F32)
    kcol = p.sbuf("kcol", [64, 1], F32)
    kcolb = p.sbuf("kcolb", [64, 1], BF16)
    onesr = p.sbuf("onesr", [1, 128], BF16)
    xtile = p.sbuf("xtile", [128, D], F32)
    x2til = p.sbuf("x2til", [128, D], F32)
    t1tm = p.sbuf("t1tm", [128, D], BF16)
    ctm = p.sbuf("ctm", [128, L], BF16)
    cfmt = p.sbuf("cfmt", [128, L], BF16)
    xfmt = p.sbuf("xfmt", [128, D], BF16)
    qtm = p.sbuf("qtm", [128, L], F32)
    ktm = p.sbuf("ktm", [128, L], F32)
    scr = p.sbuf("scr", [128, D], BF16)
    scrf = p.sbuf("scrf", [128, 512], F32)
    ptb = [p.sbuf(f"ptb{i}", [128, 512], BF16) for i in range(2)]
    sc = p.sbuf("sc", [128, 16], F32)
    rc = p.sbuf("rc", [128, 4], F32)
    pf = [p.psum(f"pf{i}", [128, 512], F32) for i in range(2)]
    po = [p.psum(f"po{i}", [128, 512], F32) for i in range(4)]
    ptr = [p.psum(f"ptr{i}", [128, 128], BF16) for i in range(2)]

    chq = chs["chq"]
    chg = chs["chg"]
    chmm = chs["chmm"]
    chst = chs["chst"]
    chws = chs["chws"]

    p.g.dma_start(out=ident[:, :], in_=ident_in[:, :]).inc(chg, 16)
    p.g.dma_start(out=cmsk[:, :], in_=cmask[:, :]).inc(chg, 16)

    p.g.dma_start(out=g1col[:, 0:KCH],
                  in_=g1.rearrange("one (k p) -> (one p) k", p=128),
                  allow_slow_non_contiguous=True).inc(chg, 16)
    for k in range(LCH):
        p.g.dma_start(out=wqsb[:, k * 3 * L:(k + 1) * 3 * L],
                      in_=wqkvc[k * 128:(k + 1) * 128, :]).inc(chg, 16)
        p.g.dma_start(out=wupsb[:, k * D:(k + 1) * D],
                      in_=wup[k * 128:(k + 1) * 128, :]).inc(chg, 16)
    for k in range(KCH):
        p.q.wait(p.cs)
        p.q.dma_start(out=scrf[:, 0:512],
                      in_=wdown[k * 128:(k + 1) * 128, :]).inc(chq, 16)
        p.s.wait(chq)
        p.s.wait(chg)
        p.s.activation(out=wdsb[:, k * L:(k + 1) * L], in_=scrf[:, 0:512],
                       func=AF.Identity, scale=g1col[:, k:k + 1])
    p.v.memset(onesr[0:1, :], 1.0)
    p.v.memset(vsb[:, :].rearrange("p (g w) -> p g w", w=33)[:, :, 32:33], 1.0)
    p.v.memset(kaug[:, :].rearrange("p (g w) -> p g w", w=33)[:, :, 32:33], 1.0)
    p.t.wait(chg)
    p.t.wait(chq)
    p.t.wait(p.cs)
    p.t.wait(p.cv)

    pfs = [(0, 0)] * 2
    pos = [(0, 0)] * 4
    ptrs = [(0, 0)] * 2
    ptbs = [0] * 2

    def wait_slot(eng, mark):
        eng.wait(p.cv, mark[0])
        eng.wait(p.cs, mark[1])

    for b in range(n_b):
        p.v.wait(p.cs)
        p.v.memset(knm[:, :], 0.0)
        for tt in range(TT):
            p.q.wait(p.cs)
            p.q.wait(p.cv)
            p.q.wait(chg)
            p.q.dma_start(out=xtile[:, :], in_=xs[b, tt * 128:(tt + 1) * 128, :]).inc(chq, 16)
            p.q.dma_start(out=cossb[:, :], in_=cosr[tt * 128:(tt + 1) * 128, :]).inc(chq, 16)
            p.q.dma_start(out=sinsb[:, :], in_=sinr[tt * 128:(tt + 1) * 128, :]).inc(chq, 16)
            p.s.wait(chq)
            p.v.wait(chq)
            nmu, rcol, nmr = emit_ln_stats(p, xtile[:, :], D, sc, 0, scr)
            p.v.wait(chst)
            p.v.wait(chg)
            p.v.tensor_scalar(out=x2til[:, :], in0=xtile[:, :], scalar1=rcol,
                              scalar2=nmr, op0=ALU.mult, op1=ALU.add, _sd=1)
            p.s.wait(p.cv)
            p.s.activation(out=scr[:, 0:D], in_=x2til[:, :], func=AF.Square,
                           accum_out=sc[:, 8:9])
            p.v.wait(p.cs)
            p.v.tensor_scalar(out=sc[:, 9:10], in0=sc[:, 8:9], scalar1=1.0 / D,
                              scalar2=EPS, op0=ALU.mult, op1=ALU.add)
            p.s.wait(p.cv)
            p.s.activation(out=sc[:, 10:11], in_=sc[:, 9:10], func=AF.Sqrt)
            p.v.wait(p.cs)
            p.v.reciprocal(out=sc[:, 10:11], in_=sc[:, 10:11])
            p.v.wait(chmm)
            p.v.tensor_scalar(out=t1tm[:, :], in0=x2til[:, :],
                              scalar1=sc[:, 10:11], scalar2=None, op0=ALU.mult, _sd=1)
            t1_done = p.cv.count
            p.g.wait(p.cv, t1_done)
            p.g.dma_start(out=xp[b, tt * 128:(tt + 1) * 128, :],
                          in_=x2til[:, :]).inc(chg, 16)
            for k in range(KCH):
                slot = k % 2
                bank = ptr[slot]
                wait_slot(p.t, ptrs[slot])
                p.t.wait(p.cv, t1_done)
                p.t.transpose(bank[:, :], t1tm[:, k * 128:(k + 1) * 128],
                              ident[:, :]).inc(chmm)
                p.v.wait(chmm)
                p.v.tensor_copy(out=xfmt[:, k * 128:(k + 1) * 128], in_=bank[:, :])
                ptrs[slot] = (p.cv.count, 0)
            xfm_done = p.cv.count
            slot = 0
            bank = pf[slot]
            wait_slot(p.t, pfs[slot])
            p.t.wait(p.cv, xfm_done)
            for k in range(KCH):
                hm = p.t.matmul(bank[:, :], xfmt[:, k * 128:(k + 1) * 128],
                                wdsb[:, k * L:(k + 1) * L],
                                start=(k == 0), stop=(k == KCH - 1))
            hm.inc(chmm)
            p.v.wait(chmm)
            p.v.tensor_copy(out=ctm[:, :], in_=bank[:, :])
            pfs[slot] = (p.cv.count, 0)
            for k in range(LCH):
                slot = k % 2
                bank = ptr[slot]
                wait_slot(p.t, ptrs[slot])
                p.t.wait(p.cv)
                p.t.transpose(bank[:, :], ctm[:, k * 128:(k + 1) * 128],
                              ident[:, :]).inc(chmm)
                p.v.wait(chmm)
                p.v.tensor_copy(out=cfmt[:, k * 128:(k + 1) * 128], in_=bank[:, :])
                ptrs[slot] = (p.cv.count, 0)
            cfm_done = p.cv.count
            for part in range(3):
                slot = part % 2
                bank = pf[slot]
                wait_slot(p.t, pfs[slot])
                p.t.wait(p.cv, cfm_done)
                for k in range(LCH):
                    hm = p.t.matmul(bank[:, :], cfmt[:, k * 128:(k + 1) * 128],
                                    wqsb[:, k * 3 * L + part * L: k * 3 * L + (part + 1) * L],
                                    start=(k == 0), stop=(k == LCH - 1))
                hm.inc(chmm)
                p.v.wait(chmm)
                if part == 0:
                    p.v.tensor_copy(out=qtm[:, :], in_=bank[:, :])
                elif part == 1:
                    p.v.tensor_copy(out=ktm[:, :], in_=bank[:, :])
                else:
                    p.v.tensor_copy(
                        out=vsb[:, :].rearrange("p (g w) -> p g w", w=33)
                            [:, tt * H:(tt + 1) * H, 0:32],
                        in_=bank[:, :].rearrange("p (g w) -> p g w", w=32))
                pfs[slot] = (p.cv.count, 0)
            for (tsrc, dst) in ((qtm, qaug), (ktm, kaug)):
                s3 = tsrc[:, :].rearrange("p (g w) -> p g w", w=32)
                d3 = dst[:, :].rearrange("p (g w) -> p g w", w=33)[:, tt * H:(tt + 1) * H, :]
                c3 = cossb[:, :].rearrange("p (g w) -> p g w", w=16)
                s3s = sinsb[:, :].rearrange("p (g w) -> p g w", w=16)
                p.v.tensor_tensor(out=scrf[:, 0:256].rearrange("p (g w) -> p g w", w=16),
                                  in0=s3[:, :, 0:32:2], in1=c3, op=ALU.mult, _sd=1)
                p.v.tensor_tensor(out=scrf[:, 256:512].rearrange("p (g w) -> p g w", w=16),
                                  in0=s3[:, :, 1:32:2], in1=s3s, op=ALU.mult, _sd=1)
                p.v.tensor_tensor(out=d3[:, :, 0:32:2],
                                  in0=scrf[:, 0:256].rearrange("p (g w) -> p g w", w=16),
                                  in1=scrf[:, 256:512].rearrange("p (g w) -> p g w", w=16),
                                  op=ALU.subtract, _sd=1)
                p.v.tensor_tensor(out=scrf[:, 0:256].rearrange("p (g w) -> p g w", w=16),
                                  in0=s3[:, :, 0:32:2], in1=s3s, op=ALU.mult, _sd=1)
                p.v.tensor_tensor(out=scrf[:, 256:512].rearrange("p (g w) -> p g w", w=16),
                                  in0=s3[:, :, 1:32:2], in1=c3, op=ALU.mult, _sd=1)
                p.v.tensor_tensor(out=d3[:, :, 1:33:2],
                                  in0=scrf[:, 0:256].rearrange("p (g w) -> p g w", w=16),
                                  in1=scrf[:, 256:512].rearrange("p (g w) -> p g w", w=16),
                                  op=ALU.add, _sd=1)
            q3 = qaug[:, :].rearrange("p (g w) -> p g w", w=33)[:, tt * H:(tt + 1) * H, 0:32]
            k3 = kaug[:, :].rearrange("p (g w) -> p g w", w=33)[:, tt * H:(tt + 1) * H, 0:32]
            p.s.wait(p.cv)
            p.s.activation(out=scrf[:, 0:512].rearrange("p (g w) -> p g w", w=32),
                           in_=q3, func=AF.Square)
            p.v.wait(p.cs)
            p.v.tensor_reduce(out=qn[:, tt * H:(tt + 1) * H].rearrange("p (g w) -> p g w", w=1),
                              in_=scrf[:, 0:512].rearrange("p (g w) -> p g w", w=32),
                              axis=AX.X, op=ALU.add)
            p.s.wait(p.cv)
            p.s.activation(out=qn[:, tt * H:(tt + 1) * H],
                           in_=qn[:, tt * H:(tt + 1) * H], func=AF.Sqrt)
            p.s.activation(out=scrf[:, 0:512].rearrange("p (g w) -> p g w", w=32),
                           in_=k3, func=AF.Square)
            p.v.wait(p.cs)
            p.v.tensor_reduce(out=sc[:, 0:16].rearrange("p (g w) -> p g w", w=1),
                              in_=scrf[:, 0:512].rearrange("p (g w) -> p g w", w=32),
                              axis=AX.X, op=ALU.add)
            p.v.tensor_tensor(out=knm[:, :], in0=knm[:, :], in1=sc[:, 0:16],
                              op=ALU.max, _sd=1)
        p.v.tensor_copy(out=knmb[:, :], in_=knm[:, :], _sd=1)
        kn_done = p.cv.count
        slot = 0
        bank = ptr[slot]
        wait_slot(p.t, ptrs[slot])
        p.t.wait(p.cv, kn_done)
        p.t.transpose(bank[0:16, 0:128], knmb[:, 0:16], ident[:, :]).inc(chmm)
        p.v.wait(chmm)
        p.v.tensor_copy(out=kred[0:16, :], in_=bank[0:16, :])
        p.v.tensor_reduce(out=kcol[0:16, 0:1], in_=kred[0:16, :], axis=AX.X,
                          op=ALU.max, _sd=1)
        p.v.tensor_scalar(out=kcol[0:16, 0:1], in0=kcol[0:16, 0:1], scalar1=0.0,
                          scalar2=1e18, op0=ALU.max, op1=ALU.min, _sd=1)
        p.s.wait(p.cv)
        p.s.activation(out=kcol[0:16, 0:1], in_=kcol[0:16, 0:1], func=AF.Sqrt)
        p.v.wait(p.cs)
        p.v.tensor_copy(out=kcolb[0:16, 0:1], in_=kcol[0:16, 0:1], _sd=1)
        ptrs[slot] = (p.cv.count, 0)
        kb_done = p.cv.count
        slot = 1
        bank = ptr[slot]
        wait_slot(p.t, ptrs[slot])
        p.t.wait(p.cv, kb_done)
        p.t.transpose(bank[0:1, 0:16], kcolb[0:16, 0:1], ident[0:16, 0:16]).inc(chmm)
        p.v.wait(chmm)
        p.v.tensor_copy(out=kmrow[0:1, 0:16], in_=bank[0:1, 0:16])
        ptrs[slot] = (p.cv.count, 0)
        km_done = p.cv.count
        slot = 0
        bank = pf[slot]
        wait_slot(p.t, pfs[slot])
        p.t.wait(p.cv, km_done)
        p.t.matmul(bank[:, 0:16], onesr[0:1, 0:128], kmrow[0:1, 0:16],
                   start=True, stop=True).inc(chmm)
        p.v.wait(chmm)
        p.v.tensor_copy(out=kmaxb[:, :], in_=bank[:, 0:16])
        pfs[slot] = (p.cv.count, 0)

        q33 = qaug[:, :].rearrange("p (g w) -> p g w", w=33)
        p.v.wait(p.cs)
        for tt in range(TT):
            for h in range(H):
                p.v.scalar_tensor_tensor(
                    out=q33[:, tt * H + h, 32:33],
                    in0=qn[:, tt * H + h: tt * H + h + 1], scalar=-1.0,
                    in1=kmaxb[:, h:h + 1], op0=ALU.mult, op1=ALU.mult, _sd=1)
        ccol_done = p.cv.count
        for tt in range(TT):
            for h in range(H):
                pair, base = h // 2, 64 * (h % 2)
                for (srcbuf, dstbuf) in ((qaug, qfm[pair]), (kaug, kfm[pair])):
                    slot = (tt * H + h) % 2
                    bank = ptr[slot]
                    wait_slot(p.t, ptrs[slot])
                    p.t.wait(p.cv, ccol_done)
                    p.t.transpose(bank[0:33, :],
                                  srcbuf[:, tt * 528 + h * 33: tt * 528 + (h + 1) * 33],
                                  ident[:, :]).inc(chmm)
                    p.v.wait(chmm)
                    p.v.tensor_copy(out=dstbuf[base:base + 33, tt * 128:(tt + 1) * 128],
                                    in_=bank[0:33, :])
                    ptrs[slot] = (p.cv.count, 0)
        aug_done = p.cv.count

        for h in range(H):
            pair, base = h // 2, 64 * (h % 2)
            for g in range(4):
                nki = 4 * g + 4
                for ki in range(nki):
                    slot = ki % 2
                    bank = pf[slot]
                    wait_slot(p.t, pfs[slot])
                    p.t.wait(p.cv, aug_done)
                    p.t.matmul(bank[:, :],
                               kfm[pair][base:base + 33, ki * 128:(ki + 1) * 128],
                               qfm[pair][base:base + 33, g * 512:(g + 1) * 512],
                               start=True, stop=True).inc(chmm)
                    mmv = chmm.count
                    if ki >= 4 * g:
                        phi = ki - 4 * g
                        p.v.wait(chmm, mmv)
                        p.v.scalar_tensor_tensor(out=bank[:, :], in0=bank[:, :],
                                                 scalar=0.0, in1=cmsk[:, phi * 512:(phi + 1) * 512],
                                                 op0=ALU.add, op1=ALU.add)
                        dvm = p.cv.count
                    else:
                        dvm = 0
                    pslot = ki % 2
                    p.s.wait(chmm, ptbs[pslot])
                    p.s.wait(chmm, mmv)
                    if dvm:
                        p.s.wait(p.cv, dvm)
                    p.s.activation(out=ptb[pslot][:, :], in_=bank[:, :],
                                   func=AF.Exp, scale=RS)
                    pfs[slot] = (p.cv.count, p.cs.count)
                    pt_done = p.cs.count
                    last = None
                    for j in range(4):
                        qt = 4 * g + j
                        if qt < ki:
                            continue
                        ob = po[j]
                        if ki == 0:
                            wait_slot(p.t, pos[j])
                        p.t.wait(p.cs, pt_done)
                        last = p.t.matmul(ob[:, 0:33],
                                          ptb[pslot][:, j * 128:(j + 1) * 128],
                                          vsb[:, ki * 528 + h * 33: ki * 528 + (h + 1) * 33],
                                          start=(ki == 0), stop=(ki == qt))
                    last.inc(chmm)
                    ptbs[pslot] = chmm.count
                p.v.wait(chmm)
                for j in range(4):
                    qt = 4 * g + j
                    ob = po[j]
                    p.v.reciprocal(out=rc[:, j:j + 1], in_=ob[:, 32:33])
                    p.v.tensor_scalar(out=otm[:, qt * 512 + h * 32: qt * 512 + (h + 1) * 32],
                                      in0=ob[:, 0:32], scalar1=rc[:, j:j + 1],
                                      scalar2=None, op0=ALU.mult, _sd=1)
                    pos[j] = (p.cv.count, 0)
        o_done = p.cv.count

        for tt in range(TT):
            p.q.wait(p.cs)
            p.q.wait(p.cv)
            p.q.wait(chst)
            p.q.dma_start(out=xtile[:, :], in_=xp[b, tt * 128:(tt + 1) * 128, :]).inc(chq, 16)
            for k in range(LCH):
                slot = k % 2
                bank = ptr[slot]
                wait_slot(p.t, ptrs[slot])
                p.t.wait(p.cv, o_done)
                p.t.transpose(bank[:, :], otm[:, tt * 512 + k * 128: tt * 512 + (k + 1) * 128],
                              ident[:, :]).inc(chmm)
                p.v.wait(chmm)
                p.v.tensor_copy(out=cfmt[:, k * 128:(k + 1) * 128], in_=bank[:, :])
                ptrs[slot] = (p.cv.count, 0)
            ofm_done = p.cv.count
            for nn in range(2):
                slot = nn % 2
                bank = pf[slot]
                wait_slot(p.t, pfs[slot])
                p.t.wait(p.cv, ofm_done)
                for k in range(LCH):
                    hm = p.t.matmul(bank[:, :], cfmt[:, k * 128:(k + 1) * 128],
                                    wupsb[:, k * D + nn * 512: k * D + (nn + 1) * 512],
                                    start=(k == 0), stop=(k == LCH - 1))
                hm.inc(chmm)
                p.v.wait(chmm)
                p.v.wait(chq)
                p.v.wait(chg)
                p.v.tensor_tensor(out=x2til[:, nn * 512:(nn + 1) * 512],
                                  in0=bank[:, :], in1=xtile[:, nn * 512:(nn + 1) * 512],
                                  op=ALU.add)
                pfs[slot] = (p.cv.count, 0)
            p.g.wait(p.cv)
            p.g.dma_start(out=xp[b, tt * 128:(tt + 1) * 128, :],
                          in_=x2til[:, :]).inc(chst, 16)


def build_l1ffn_a(n_b=4, S=2048):
    D, L = 1024, 512
    nc = bass.Bass("TRN2", target_bir_lowering=False)
    xs = nc.dram_tensor("xs", [n_b, S, D], F32, kind="ExternalInput").ap()
    g1 = nc.dram_tensor("g1", [1, D], F32, kind="ExternalInput").ap()
    wdown = nc.dram_tensor("wdown", [D, L], F32, kind="ExternalInput").ap()
    wqkvc = nc.dram_tensor("wqkvc", [L, 3 * L], F32, kind="ExternalInput").ap()
    wup = nc.dram_tensor("wup", [L, D], F32, kind="ExternalInput").ap()
    cosr = nc.dram_tensor("cosr", [S, 256], BF16, kind="ExternalInput").ap()
    sinr = nc.dram_tensor("sinr", [S, 256], BF16, kind="ExternalInput").ap()
    cmask = nc.dram_tensor("cmask", [128, 4 * 512], F32, kind="ExternalInput").ap()
    ident_in = nc.dram_tensor("ident", [128, 128], BF16, kind="ExternalInput").ap()
    xp = nc.dram_tensor("xp", [n_b, S, D], F32, kind="ExternalOutput").ap()

    es = contextlib.ExitStack()
    with es:
        p = Prog(nc, es)
        chs = _mk_chans(p)
        emit_l1ffn_a(nc, p, chs, xs, g1, wdown, wqkvc, wup, cosr, sinr, cmask, ident_in, xp, n_b, S)
        p.g.wait(chs["chst"])
        p.finalize()
    return nc


def _rope_tables(S=2048, HD=32, H=16):
    inv = 1.0 / (10000.0 ** (np.arange(0, HD, 2, dtype=np.float32) / HD))
    ang = np.arange(S, dtype=np.float32)[:, None] * inv[None, :]
    cos = np.cos(ang).astype(np.float32)
    sin = np.sin(ang).astype(np.float32)
    cosr = np.tile(cos, (1, H))
    sinr = np.tile(sin, (1, H))
    return np.ascontiguousarray(cosr), np.ascontiguousarray(sinr)


def _cmask_phases():
    m = np.zeros((128, 4 * 512), np.float32)
    for phi in range(4):
        k = np.arange(128)[:, None]
        j = np.arange(512)[None, :]
        m[:, phi * 512:(phi + 1) * 512] = np.where(j >= k + 128 * phi, 0.0, -1e30)
    return np.ascontiguousarray(m)


def _host_l1ffna_inputs(x3_full, P):
    wqkvc = np.ascontiguousarray(
        np.concatenate([P["L1_ffn_wq"], P["L1_ffn_wk"], P["L1_ffn_wv"]], axis=1))
    cosr, sinr = _rope_tables()
    cosr = cosr.astype(ml_dtypes.bfloat16)
    sinr = sinr.astype(ml_dtypes.bfloat16)
    in_maps = []
    for c in range(N_CORES):
        in_maps.append({
            "xs": np.ascontiguousarray(x3_full[c * 4:(c + 1) * 4]),
            "g1": np.ascontiguousarray(P["L1_ffn_g1"][None, :]),
            "wdown": np.ascontiguousarray(P["L1_ffn_w_down"]),
            "wqkvc": wqkvc,
            "wup": np.ascontiguousarray(P["L1_ffn_w_up"]),
            "cosr": cosr, "sinr": sinr, "cmask": _cmask_phases(),
            "ident": _ident128(),
        })
    return in_maps




def emit_l1ffn_b(nc, p, chs, xs, xpb, g2, rw, w1, w3, w2, ident_in, xs4, tfm, hidb, ybuf, n_b=4, S=2048):
    D = 1024
    F = 2048
    E = 8
    KCH = D // 128
    FCH = F // 128
    TT = S // 128
    NTU = n_b * S // 512

    w13 = p.sbuf("w13", [128, KCH * 2 * F], BF16)
    w2sb = p.sbuf("w2sb", [128, FCH * D], BF16)
    tfsb = [p.sbuf(f"tfsb{i}", [128, KCH * 512], BF16) for i in range(2)]
    hidw = p.sbuf("hidw", [128, FCH * 512], BF16)
    hidr = [p.sbuf(f"hidr{i}", [128, FCH * 512], BF16) for i in range(2)]
    yacc = p.sbuf("yacc", [128, D], F32)
    ybf = p.sbuf("ybf", [128, D], F32)
    gatesb = p.sbuf("gatesb", [128, n_b * TT * E], F32)
    lgsb = p.sbuf("lgsb", [128, n_b * TT * E], F32)
    rwsb = p.sbuf("rwsb", [128, KCH * E], BF16)
    g2col = p.sbuf("g2col", [128, KCH], F32)
    ident = p.sbuf("identS", [128, 128], BF16)
    xtile = p.sbuf("xtile", [128, D], F32)
    x2til = p.sbuf("x2til", [128, D], F32)
    t1tm = p.sbuf("t1tm", [128, D], BF16)
    xfmt = p.sbuf("xfmt", [128, D], BF16)
    scr = p.sbuf("scr", [128, D], BF16)
    scrf = p.sbuf("scrf", [128, F], F32)
    scrb = p.sbuf("scrb", [128, 512], BF16)
    sc = p.sbuf("sc", [128, 16], F32)
    pf = [p.psum(f"pf{i}", [128, 512], F32) for i in range(6)]
    ptr = [p.psum(f"ptr{i}", [128, 128], BF16) for i in range(2)]

    chq = chs["chq"]
    chg = chs["chg"]
    chw = chs["chw"]
    chy = chs["chy"]
    chws = chs["chws"]
    cht = chs["cht"]
    chh = chs["chh"]
    chmm = chs["chmm"]
    chst = chs["chst"]

    p.g.dma_start(out=ident[:, :], in_=ident_in[:, :]).inc(chw, 16)
    p.g.dma_start(out=g2col[:, 0:KCH],
                  in_=g2.rearrange("one (k p) -> (one p) k", p=128),
                  allow_slow_non_contiguous=True).inc(chw, 16)
    for k in range(KCH):
        p.q.wait(p.cs)
        p.q.dma_start(out=scrf[:, 0:E], in_=rw[k * 128:(k + 1) * 128, :]).inc(chws, 16)
        p.s.wait(chws)
        p.s.wait(chw)
        p.s.activation(out=rwsb[:, k * E:(k + 1) * E], in_=scrf[:, 0:E],
                       func=AF.Identity, scale=g2col[:, k:k + 1])
    p.t.wait(chw)
    p.t.wait(chws)
    p.t.wait(p.cs)

    pfs = [(0, 0)] * 6
    ptrs = [(0, 0)] * 2

    def wait_slot(eng, mark):
        eng.wait(p.cv, mark[0])
        eng.wait(p.cs, mark[1])

    for b in range(n_b):
        for tt in range(TT):
            p.q.wait(p.cs)
            p.q.wait(p.cv)
            p.q.dma_start(out=xtile[:, :], in_=xpb[b, tt * 128:(tt + 1) * 128, :]).inc(chq, 16)
            p.s.wait(chq)
            p.s.activation(out=scr[:, 0:D], in_=xtile[:, :], func=AF.Square,
                           accum_out=sc[:, 0:1])
            p.v.wait(p.cs)
            p.v.tensor_scalar(out=sc[:, 1:2], in0=sc[:, 0:1], scalar1=1.0 / D,
                              scalar2=EPS, op0=ALU.mult, op1=ALU.add)
            p.s.wait(p.cv)
            p.s.activation(out=sc[:, 2:3], in_=sc[:, 1:2], func=AF.Sqrt)
            p.v.wait(p.cs)
            p.v.reciprocal(out=sc[:, 2:3], in_=sc[:, 2:3], _sd=1)
            p.v.wait(chmm)
            p.v.tensor_scalar(out=t1tm[:, :], in0=xtile[:, :],
                              scalar1=sc[:, 2:3], scalar2=None, op0=ALU.mult, _sd=1)
            t1_done = p.cv.count
            for k in range(KCH):
                slot = k % 2
                bank = ptr[slot]
                wait_slot(p.t, ptrs[slot])
                p.t.wait(p.cv, t1_done)
                p.t.transpose(bank[:, :], t1tm[:, k * 128:(k + 1) * 128],
                              ident[:, :]).inc(chmm)
                p.v.wait(chmm)
                p.v.wait(chg)
                p.v.tensor_copy(out=xfmt[:, k * 128:(k + 1) * 128], in_=bank[:, :])
                ptrs[slot] = (p.cv.count, 0)
            xfm_done = p.cv.count
            p.g.wait(p.cv, xfm_done)
            p.g.dma_start(
                out=tfm[b].rearrange("(k p) s -> p k s", p=128)[:, :, tt * 128:(tt + 1) * 128],
                in_=xfmt[:, :].rearrange("p (k s) -> p k s", s=128)).inc(chg, 16)
            slot = 0
            bank = pf[slot]
            wait_slot(p.t, pfs[slot])
            p.t.wait(p.cv, xfm_done)
            for k in range(KCH):
                hm = p.t.matmul(bank[:, 0:E], xfmt[:, k * 128:(k + 1) * 128],
                                rwsb[:, k * E:(k + 1) * E],
                                start=(k == 0), stop=(k == KCH - 1))
            hm.inc(chmm)
            p.v.wait(chmm)
            p.v.tensor_copy(out=lgsb[:, (b * TT + tt) * E:(b * TT + tt + 1) * E],
                            in_=bank[:, 0:E])
            pfs[slot] = (p.cv.count, 0)

    for i in range(n_b * TT):
        lg = lgsb[:, i * E:(i + 1) * E]
        gt = gatesb[:, i * E:(i + 1) * E]
        p.v.tensor_reduce(out=sc[:, 0:1], in_=lg, axis=AX.X, op=ALU.max)
        p.v.tensor_scalar(out=sc[:, 1:2], in0=sc[:, 0:1], scalar1=-1.0,
                          scalar2=None, op0=ALU.mult, _sd=1)
        p.s.wait(p.cv)
        p.s.activation(out=scrf[:, 0:E], in_=lg, func=AF.Exp, bias=sc[:, 1:2])
        p.v.wait(p.cs)
        p.v.tensor_reduce(out=sc[:, 2:3], in_=scrf[:, 0:E], axis=AX.X, op=ALU.max)
        p.v.tensor_scalar(out=scrf[:, 8:16], in0=scrf[:, 0:E], scalar1=sc[:, 2:3],
                          scalar2=None, op0=ALU.is_equal, _sd=1)
        p.v.tensor_tensor(out=scrf[:, 16:24], in0=scrf[:, 0:E], in1=scrf[:, 8:16],
                          op=ALU.mult, _sd=1)
        p.v.tensor_tensor(out=scrf[:, 24:32], in0=scrf[:, 0:E], in1=scrf[:, 16:24],
                          op=ALU.subtract, _sd=1)
        p.v.tensor_reduce(out=sc[:, 3:4], in_=scrf[:, 24:32], axis=AX.X, op=ALU.max, _sd=1)
        p.v.tensor_scalar(out=scrf[:, 32:40], in0=scrf[:, 24:32], scalar1=sc[:, 3:4],
                          scalar2=None, op0=ALU.is_equal, _sd=1)
        p.v.tensor_tensor(out=scrf[:, 40:48], in0=scrf[:, 8:16], in1=scrf[:, 32:40],
                          op=ALU.add, _sd=1)
        p.v.tensor_tensor(out=sc[:, 4:5], in0=sc[:, 2:3], in1=sc[:, 3:4], op=ALU.add)
        p.v.reciprocal(out=sc[:, 4:5], in_=sc[:, 4:5], _sd=1)
        p.v.tensor_tensor(out=scrf[:, 48:56], in0=scrf[:, 0:E], in1=scrf[:, 40:48],
                          op=ALU.mult, _sd=1)
        p.v.tensor_scalar(out=gt, in0=scrf[:, 48:56], scalar1=sc[:, 4:5],
                          scalar2=None, op0=ALU.mult, _sd=1)
    gates_done = p.cv.count

    for e in range(E):
        for k in range(KCH):
            for (wt, off) in ((w1, 0), (w3, F)):
                p.q.wait(p.cs)
                p.q.wait(p.cv)
                p.q.dma_start(out=scrf[:, 0:F],
                              in_=wt[e, k * 128:(k + 1) * 128, :]).inc(chws, 16)
                p.s.wait(chws)
                p.s.wait(chmm)
                p.s.activation(out=w13[:, k * 2 * F + off: k * 2 * F + off + F],
                               in_=scrf[:, 0:F], func=AF.Identity,
                               scale=g2col[:, k:k + 1])
        w_done = p.cs.count
        for tu in range(NTU):
            tb = tfsb[tu % 2]
            p.q.wait(chmm)
            p.q.dma_start(
                out=tb[:, :].rearrange("p (k s) -> p k s", s=512),
                in_=tfm[tu // 4].rearrange("(k p) s -> p k s", p=128)
                    [:, :, (tu % 4) * 512:(tu % 4 + 1) * 512]).inc(cht[tu % 2], 16)
            tchan, tload = cht[tu % 2], cht[tu % 2].count
            for fch in range(FCH):
                sa = fch % 3
                sb_ = 3 + fch % 3
                banka, bankb = pf[sa], pf[sb_]
                wait_slot(p.t, pfs[sa])
                wait_slot(p.t, pfs[sb_])
                p.t.wait(tchan, tload)
                p.t.wait(p.cs, w_done)
                for k in range(KCH):
                    p.t.matmul(banka[:, :],
                               w13[:, k * 2 * F + fch * 128: k * 2 * F + (fch + 1) * 128],
                               tb[:, k * 512:(k + 1) * 512],
                               start=(k == 0), stop=(k == KCH - 1))
                for k in range(KCH):
                    hm = p.t.matmul(bankb[:, :],
                                    w13[:, k * 2 * F + F + fch * 128: k * 2 * F + F + (fch + 1) * 128],
                                    tb[:, k * 512:(k + 1) * 512],
                                    start=(k == 0), stop=(k == KCH - 1))
                hm.inc(chmm)
                p.s.wait(chmm)
                p.s.wait(p.cv)
                p.s.activation(out=scrb[:, 0:512], in_=banka[:, :], func=AF.Silu)
                p.v.wait(p.cs)
                p.v.wait(chg)
                p.v.scalar_tensor_tensor(out=hidw[:, fch * 512:(fch + 1) * 512],
                                         in0=bankb[:, :], scalar=0.0,
                                         in1=scrb[:, 0:512], op0=ALU.add,
                                         op1=ALU.mult)
                pfs[sa] = (0, p.cs.count)
                pfs[sb_] = (p.cv.count, 0)
            p.g.wait(p.cv)
            p.g.dma_start(out=dram3(hidb[e, tu]),
                          in_=hidw[:, :].rearrange("p (k s) -> p k s", s=512)).inc(chg, 16)

    p.q.wait(chg)
    for e in range(E):
        for k in range(FCH):
            p.g.wait(chmm)
            p.g.dma_start(out=w2sb[:, k * D:(k + 1) * D],
                          in_=w2[e, k * 128:(k + 1) * 128, :]).inc(chw, 16)
        w2_done = chw.count
        p.g.wait(chy)
        for tu in range(NTU):
            hb = hidr[tu % 2]
            p.q.wait(chmm)
            p.q.dma_start(out=hb[:, :].rearrange("p (k s) -> p k s", s=512),
                          in_=dram3(hidb[e, tu])).inc(chh[tu % 2], 16)
            hchan, hload = chh[tu % 2], chh[tu % 2].count
            for t4 in range(4):
                gi = tu * 4 + t4
                for nn in range(2):
                    slot = (t4 * 2 + nn) % 6
                    bank = pf[slot]
                    wait_slot(p.t, pfs[slot])
                    p.t.wait(hchan, hload)
                    p.t.wait(chw, w2_done)
                    for k in range(FCH):
                        hm = p.t.matmul(bank[:, :],
                                        hb[:, k * 512 + t4 * 128: k * 512 + (t4 + 1) * 128],
                                        w2sb[:, k * D + nn * 512: k * D + (nn + 1) * 512],
                                        start=(k == 0), stop=(k == FCH - 1))
                    hm.inc(chmm)
                    p.v.wait(chmm)
                    p.v.wait(chy)
                    p.v.tensor_scalar(out=ybf[:, nn * 512:(nn + 1) * 512],
                                      in0=bank[:, :],
                                      scalar1=gatesb[:, gi * E + e: gi * E + e + 1],
                                      scalar2=None, op0=ALU.mult)
                    pfs[slot] = (p.cv.count, 0)
                p.g.wait(p.cv)
                p.g.dma_start(out=ybuf[gi * 128:(gi + 1) * 128, :], in_=ybf[:, :],
                              accum_op=(ALU.bypass if e == 0 else ALU.add)).inc(chy, 16)

    p.q.wait(chy)
    for b in range(n_b):
        for tt in range(TT):
            gi = b * TT + tt
            p.q.wait(p.cs)
            p.q.wait(p.cv)
            p.q.wait(chst)
            p.q.dma_start(out=yacc[:, :], in_=ybuf[gi * 128:(gi + 1) * 128, :]).inc(chq, 16)
            p.q.dma_start(out=xtile[:, :], in_=xpb[b, tt * 128:(tt + 1) * 128, :]).inc(chq, 16)
            p.q.dma_start(out=x2til[:, :], in_=xs[b, tt * 128:(tt + 1) * 128, :]).inc(chq, 16)
            p.v.wait(chq)
            p.v.tensor_tensor(out=yacc[:, :], in0=yacc[:, :], in1=xtile[:, :],
                              op=ALU.add)
            p.s.wait(p.cv)
            nmu, rcol, nmr = emit_ln_stats(p, yacc[:, :], D, sc, 0, scr)
            p.v.tensor_scalar(out=x2til[:, :], in0=x2til[:, :], scalar1=nmr,
                              scalar2=None, op0=ALU.add, _sd=1)
            p.v.scalar_tensor_tensor(out=x2til[:, :], in0=yacc[:, :], scalar=rcol,
                                     in1=x2til[:, :], op0=ALU.mult, op1=ALU.add, _sd=1)
            p.g.wait(p.cv)
            p.g.dma_start(out=xs4[b, tt * 128:(tt + 1) * 128, :],
                          in_=x2til[:, :]).inc(chst, 16)


def build_l1ffn_b(n_b=4, S=2048):
    D, F, E = 1024, 2048, 8
    NTU = n_b * S // 512
    nc = bass.Bass("TRN2", target_bir_lowering=False)
    xs = nc.dram_tensor("xs", [n_b, S, D], F32, kind="ExternalInput").ap()
    xpb = nc.dram_tensor("xpb", [n_b, S, D], F32, kind="ExternalInput").ap()
    g2 = nc.dram_tensor("g2", [1, D], F32, kind="ExternalInput").ap()
    rw = nc.dram_tensor("rw", [D, E], F32, kind="ExternalInput").ap()
    w1 = nc.dram_tensor("w1", [E, D, F], F32, kind="ExternalInput").ap()
    w3 = nc.dram_tensor("w3", [E, D, F], F32, kind="ExternalInput").ap()
    w2 = nc.dram_tensor("w2", [E, F, D], F32, kind="ExternalInput").ap()
    ident_in = nc.dram_tensor("ident", [128, 128], BF16, kind="ExternalInput").ap()
    xs4 = nc.dram_tensor("xs4", [n_b, S, D], F32, kind="ExternalOutput").ap()

    tfm = nc.dram_tensor("tfm", [n_b, D, S], BF16, kind="Internal").ap()
    hidb = nc.dram_tensor("hidb", [E, NTU, F, 512], BF16, kind="Internal").ap()
    ybuf = nc.dram_tensor("ybuf", [n_b * S, D], F32, kind="Internal").ap()

    es = contextlib.ExitStack()
    with es:
        p = Prog(nc, es)
        chs = _mk_chans(p)
        emit_l1ffn_b(nc, p, chs, xs, xpb, g2, rw, w1, w3, w2, ident_in, xs4, tfm, hidb, ybuf, n_b, S)
        p.g.wait(chs["chst"])
        p.finalize()
    return nc


def _host_l1ffnb_inputs(x3_full, xp_full, P):
    in_maps = []
    for c in range(N_CORES):
        in_maps.append({
            "xs": np.ascontiguousarray(x3_full[c * 4:(c + 1) * 4]),
            "xpb": np.ascontiguousarray(xp_full[c * 4:(c + 1) * 4]),
            "g2": np.ascontiguousarray(P["L1_ffn_g2"][None, :]),
            "rw": np.ascontiguousarray(P["L1_ffn_router_w"]),
            "w1": np.ascontiguousarray(P["L1_ffn_w1"]),
            "w3": np.ascontiguousarray(P["L1_ffn_w3"]),
            "w2": np.ascontiguousarray(P["L1_ffn_w2"]),
            "ident": _ident128(),
        })
    return in_maps



def build_l0attn(n_b=4, S=2048):
    D = 1024
    DI = 2048
    CV = 2304
    DIP = 4384
    NH, P_ = 32, 64
    KCH = D // 128
    TT = S // 128
    ZCH = DI // 128
    XCH = CV // 128

    nc = bass.Bass("TRN2", target_bir_lowering=False)
    xs = nc.dram_tensor("xs", [n_b, S, D], F32, kind="ExternalInput").ap()
    wipT = nc.dram_tensor("wipT", [D, DIP], F32, kind="ExternalInput").ap()
    convw = nc.dram_tensor("convw", [CV, 4], F32, kind="ExternalInput").ap()
    convb = nc.dram_tensor("convb", [CV, 1], F32, kind="ExternalInput").ap()
    dtb = nc.dram_tensor("dtb", [NH, 1], F32, kind="ExternalInput").ap()
    alog = nc.dram_tensor("alog", [NH, 1], F32, kind="ExternalInput").ap()
    dskip = nc.dram_tensor("dskip", [1, NH], BF16, kind="ExternalInput").ap()
    dskrep = nc.dram_tensor("dskrep", [1, NH * P_], BF16, kind="ExternalInput").ap()
    normw = nc.dram_tensor("normw", [DI, 1], F32, kind="ExternalInput").ap()
    woutT = nc.dram_tensor("woutT", [DI, D], F32, kind="ExternalInput").ap()
    lng = nc.dram_tensor("lng", [1, D], BF16, kind="ExternalInput").ap()
    lnb = nc.dram_tensor("lnb", [1, D], BF16, kind="ExternalInput").ap()
    ebank = nc.dram_tensor("ebank", [NH, NH * 128], BF16, kind="ExternalInput").ap()
    tril_ = nc.dram_tensor("tril", [128, 128], BF16, kind="ExternalInput").ap()
    trius_ = nc.dram_tensor("trius", [128, 128], BF16, kind="ExternalInput").ap()
    logtri_ = nc.dram_tensor("logtri", [128, 128], F32, kind="ExternalInput").ap()
    ident_in = nc.dram_tensor("ident", [128, 128], BF16, kind="ExternalInput").ap()
    xs1 = nc.dram_tensor("xs1", [n_b, S, D], F32, kind="ExternalOutput").ap()

    abuf = nc.dram_tensor("abuf", [n_b, S, D], F32, kind="Internal").ap()
    zbuf = nc.dram_tensor("zbuf", [n_b, S, DI], BF16, kind="Internal").ap()
    xbcbuf = nc.dram_tensor("xbcbuf", [n_b, CV, S], BF16, kind="Internal").ap()
    cvbuf = nc.dram_tensor("cvbuf", [n_b, CV, S], BF16, kind="Internal").ap()
    ybuf = nc.dram_tensor("ybuf", [n_b, S, DI], F32, kind="Internal").ap()

    es = contextlib.ExitStack()
    with es:
        p = Prog(nc, es)
        wsb = p.sbuf("wsb", [128, KCH * 2304], BF16)
        xnfm = p.sbuf("xnfm", [128, KCH * S], BF16)
        xtile = p.sbuf("xtile", [128, D], F32)
        xtm = p.sbuf("xtm", [128, D], BF16)
        scr = p.sbuf("scr", [128, DI], BF16)
        scrf = p.sbuf("scrf", [128, 1024], F32)
        sc = p.sbuf("sc", [128, 16], F32)
        ident = p.sbuf("identS", [128, 128], BF16)
        cwcol = p.sbuf("cwcol", [128, XCH * 4], F32)
        cbcol = p.sbuf("cbcol", [128, XCH], F32)
        cbufs = [p.sbuf(f"cbuf{i}", [128, 515], BF16) for i in range(2)]
        ebk = p.sbuf("ebk", [NH, NH * 128], BF16)
        trilS = p.sbuf("trilS", [128, 128], BF16)
        triuS = p.sbuf("triuS", [128, 128], BF16)
        logtS = p.sbuf("logtS", [128, 128], F32)
        dtbc = p.sbuf("dtbc", [NH, 1], F32)
        acol = p.sbuf("acol", [NH, 1], F32)
        dskb = p.sbuf("dskb", [128, NH], F32)
        dskx = p.sbuf("dskx", [128, NH * P_], BF16)
        dskrow = p.sbuf("dskrow", [1, NH], BF16)
        dskreprow = p.sbuf("dskreprow", [1, NH * P_], BF16)
        onesr = p.sbuf("onesr", [1, 128], BF16)
        dtmix = p.sbuf("dtmix", [32, 2 * S], F32)
        dtmixb = p.sbuf("dtmixb", [32, 2 * S], BF16)
        dtat = p.sbuf("dtat", [128, 64], F32)
        dtatb = p.sbuf("dtatb", [128, 64], BF16)
        eal = p.sbuf("eal", [128, NH], F32)
        ebe = p.sbuf("ebe", [128, NH], F32)
        dteb = p.sbuf("dteb", [128, NH], F32)
        gcol = p.sbuf("gcol", [128, NH], F32)
        afmb = p.sbuf("afmb", [NH, 128], BF16)
        bfm = p.sbuf("bfm", [128, S], BF16)
        cfmS = p.sbuf("cfmS", [128, S], BF16)
        ldxh = p.sbuf("ldxh", [128, 16 * 128], BF16)
        xhtm = p.sbuf("xhtm", [128, 16 * 128], BF16)
        u2bf = p.sbuf("u2bf", [128, 16 * 128], BF16)
        btm = p.sbuf("btm", [128, 128], BF16)
        dmt = p.sbuf("dmt", [128, 2 * 128], F32)
        ssb = p.sbuf("ssb", [128, 2 * 128], BF16)
        expd = p.sbuf("expd", [128, 2 * 128], F32)
        gsbb = p.sbuf("gsbb", [128, 128], BF16)
        ldtc = p.sbuf("ldtc", [128, NH], F32)
        lcol = p.sbuf("lcol", [128, NH], F32)
        hstate = p.sbuf("hstate", [128, NH * P_], F32)
        hstb = p.sbuf("hstb", [128, NH * P_], BF16)
        t1sb = p.sbuf("t1sb", [128, 2 * P_], F32)
        t2sb = p.sbuf("t2sb", [128, 2 * P_], F32)
        ytm = p.sbuf("ytm", [128, DI], F32)
        nwcol = p.sbuf("nwcol", [128, ZCH], F32)
        lngb = p.sbuf("lngb", [128, 2 * D], BF16)
        ytile = p.sbuf("ytile", [128, DI], F32)
        sztile = p.sbuf("sztile", [128, DI], BF16)
        ygb = p.sbuf("ygb", [128, DI], BF16)
        ygfm = p.sbuf("ygfm", [128, DI], BF16)
        rtile = p.sbuf("rtile", [128, D], F32)
        atile = p.sbuf("atile", [128, D], F32)
        lngrow = p.sbuf("lngrow", [1, D], BF16)
        lnbrow = p.sbuf("lnbrow", [1, D], BF16)

        pfp = [p.psum(f"pf{i}", [128, 512], F32) for i in range(4)]
        ptr = [p.psum(f"ptr{i}", [128, 128], BF16) for i in range(2)]
        pss = [p.psum(f"ps{i}", [128, 128], F32) for i in range(2)]

        chq = p.chan("chq")
        chg = p.chan("chg")
        chw = p.chan("chw")
        chc = [p.chan("chc0"), p.chan("chc1")]
        chx = p.chan("chx")
        chbc = p.chan("chbc")
        chmm = p.chan("chmm")
        chst = p.chan("chst")

        pfs = [(0, 0)] * 4
        ptrs = [(0, 0)] * 2
        psss = [(0, 0)] * 2

        def wait_slot(eng, mark):
            eng.wait(p.cv, mark[0])
            eng.wait(p.cs, mark[1])

        p.g.dma_start(out=ident[:, :], in_=ident_in[:, :]).inc(chw, 16)
        p.g.dma_start(out=ebk[:, :], in_=ebank[:, :]).inc(chw, 16)
        p.g.dma_start(out=trilS[:, :], in_=tril_[:, :]).inc(chw, 16)
        p.g.dma_start(out=triuS[:, :], in_=trius_[:, :]).inc(chw, 16)
        p.g.dma_start(out=logtS[:, :], in_=logtri_[:, :]).inc(chw, 16)
        p.g.dma_start(out=cwcol[:, :], in_=dram3(convw)).inc(chw, 16)
        p.g.dma_start(out=cbcol[:, :].rearrange("p (g w) -> p g w", w=1),
                      in_=dram3(convb), allow_slow_non_contiguous=True).inc(chw, 16)
        p.g.dma_start(out=dtbc[0:NH, 0:1], in_=dtb[:, :]).inc(chw, 16)
        p.g.dma_start(out=acol[0:NH, 0:1], in_=alog[:, :]).inc(chw, 16)
        p.g.dma_start(out=dskrow[0:1, :], in_=dskip[:, :]).inc(chw, 16)
        p.g.dma_start(out=dskreprow[0:1, :], in_=dskrep[:, :]).inc(chw, 16)
        p.g.dma_start(out=nwcol[:, :].rearrange("p (g w) -> p g w", w=1),
                      in_=dram3(normw), allow_slow_non_contiguous=True).inc(chw, 16)
        p.g.dma_start(out=lngrow[0:1, :], in_=lng[:, :]).inc(chw, 16)
        p.g.dma_start(out=lnbrow[0:1, :], in_=lnb[:, :]).inc(chw, 16)
        p.v.memset(onesr[0:1, :], 1.0)
        p.v.wait(chw)
        p.v.memset(hstate[:, :], 0.0)
        p.s.wait(chw)
        p.s.activation(out=acol[0:NH, 0:1], in_=acol[0:NH, 0:1], func=AF.Exp)
        p.v.wait(p.cs)
        p.v.tensor_scalar(out=acol[0:NH, 0:1], in0=acol[0:NH, 0:1], scalar1=-1.0,
                          scalar2=None, op0=ALU.mult)
        cst_done = p.cv.count
        p.t.wait(chw)
        p.t.wait(p.cv, cst_done)
        p.t.matmul(pfp[0][:, 0:NH], onesr[0:1, 0:128], dskrow[0:1, 0:NH],
                   start=True, stop=True)
        for q4 in range(4):
            p.t.wait(p.cv)
            hm4 = p.t.matmul(pfp[3][:, :], onesr[0:1, 0:128],
                             dskreprow[0:1, q4 * 512:(q4 + 1) * 512],
                             start=True, stop=True)
            hm4.inc(chmm)
            p.v.wait(chmm)
            p.v.tensor_copy(out=dskx[:, q4 * 512:(q4 + 1) * 512], in_=pfp[3][:, :])
        p.t.matmul(pfp[1][:, :], onesr[0:1, 0:128], lngrow[0:1, 0:512],
                   start=True, stop=True)
        hm = p.t.matmul(pfp[2][:, :], onesr[0:1, 0:128], lnbrow[0:1, 0:512],
                        start=True, stop=True)
        hm.inc(chmm)
        p.v.wait(chmm)
        p.v.tensor_copy(out=dskb[:, :], in_=pfp[0][:, 0:NH])
        p.v.tensor_copy(out=lngb[:, 0:512], in_=pfp[1][:, :])
        p.v.tensor_copy(out=lngb[:, D:D + 512], in_=pfp[2][:, :])
        bcast_done = p.cv.count
        pfs[0] = (bcast_done, 0)
        pfs[1] = (bcast_done, 0)
        pfs[2] = (bcast_done, 0)
        p.t.wait(p.cv, bcast_done)
        p.t.matmul(pfp[1][:, :], onesr[0:1, 0:128], lngrow[0:1, 512:1024],
                   start=True, stop=True)
        hm = p.t.matmul(pfp[2][:, :], onesr[0:1, 0:128], lnbrow[0:1, 512:1024],
                        start=True, stop=True)
        hm.inc(chmm)
        p.v.wait(chmm)
        p.v.tensor_copy(out=lngb[:, 512:1024], in_=pfp[1][:, :])
        p.v.tensor_copy(out=lngb[:, D + 512:2 * D], in_=pfp[2][:, :])
        bcast_done = p.cv.count
        pfs[1] = (bcast_done, 0)
        pfs[2] = (bcast_done, 0)

        for b in range(n_b):
            for tt in range(TT):
                p.q.wait(p.cs)
                p.q.wait(p.cv)
                p.q.wait(chg)
                p.q.dma_start(out=xtile[:, :], in_=xs[b, tt * 128:(tt + 1) * 128, :]).inc(chq, 16)
                p.s.wait(chq)
                p.v.wait(chq)
                nmu, rcol, nmr = emit_ln_stats(p, xtile[:, :], D, sc, 0, scr)
                p.v.wait(chg)
                p.v.wait(chmm)
                p.v.tensor_scalar(out=atile[:, 0:D], in0=xtile[:, :], scalar1=rcol,
                                  scalar2=nmr, op0=ALU.mult, op1=ALU.add, _sd=1)
                p.v.tensor_copy(out=xtm[:, :], in_=atile[:, 0:D], _sd=1)
                ln_done = p.cv.count
                p.g.wait(p.cv, ln_done)
                p.g.dma_start(out=abuf[b, tt * 128:(tt + 1) * 128, :],
                              in_=atile[:, 0:D]).inc(chg, 16)
                for k in range(KCH):
                    slot = k % 2
                    bank = ptr[slot]
                    wait_slot(p.t, ptrs[slot])
                    p.t.wait(p.cv, ln_done)
                    p.t.transpose(bank[:, :], xtm[:, k * 128:(k + 1) * 128],
                                  ident[:, :]).inc(chmm)
                    p.v.wait(chmm)
                    p.v.tensor_copy(out=xnfm[:, k * S + tt * 128: k * S + (tt + 1) * 128],
                                    in_=bank[:, :])
                    ptrs[slot] = (p.cv.count, 0)
            xn_done = p.cv.count

            p.g.wait(chmm)
            for k in range(KCH):
                p.g.dma_start(out=wsb[:, k * DI:(k + 1) * DI],
                              in_=wipT[k * 128:(k + 1) * 128, 0:DI]).inc(chg, 16)
            wl = chg.count
            for tt in range(TT):
                for nn in range(4):
                    slot = nn % 4
                    bank = pfp[slot]
                    wait_slot(p.t, pfs[slot])
                    p.t.wait(chg, wl)
                    p.t.wait(p.cv, xn_done)
                    for k in range(KCH):
                        hm = p.t.matmul(bank[:, :],
                                        xnfm[:, k * S + tt * 128: k * S + (tt + 1) * 128],
                                        wsb[:, k * DI + nn * 512: k * DI + (nn + 1) * 512],
                                        start=(k == 0), stop=(k == KCH - 1))
                    hm.inc(chmm)
                    p.s.wait(chmm)
                    p.s.wait(chg)
                    p.s.activation(out=sztile[:, nn * 512:(nn + 1) * 512],
                                   in_=bank[:, :], func=AF.Silu)
                    pfs[slot] = (0, p.cs.count)
                p.g.wait(p.cs)
                p.g.dma_start(out=zbuf[b, tt * 128:(tt + 1) * 128, :],
                              in_=sztile[:, :]).inc(chg, 16)
            p.g.wait(chmm)
            for k in range(KCH):
                p.g.dma_start(out=wsb[:, k * 2304:(k + 1) * 2304],
                              in_=wipT[k * 128:(k + 1) * 128, DI:DI + 2304]).inc(chg, 16)
            wl = chg.count
            for ec in range(XCH):
                for t5 in range(4):
                    slot = t5 % 4
                    bank = pfp[slot]
                    wait_slot(p.t, pfs[slot])
                    p.t.wait(chg, wl)
                    p.t.wait(p.cv, xn_done)
                    for k in range(KCH):
                        hm = p.t.matmul(bank[:, :],
                                        wsb[:, k * 2304 + ec * 128: k * 2304 + (ec + 1) * 128],
                                        xnfm[:, k * S + t5 * 512: k * S + (t5 + 1) * 512],
                                        start=(k == 0), stop=(k == KCH - 1))
                    hm.inc(chmm)
                    p.v.wait(chmm)
                    p.v.wait(chg)
                    p.v.tensor_copy(out=scr[:, t5 * 512:(t5 + 1) * 512], in_=bank[:, :])
                    pfs[slot] = (p.cv.count, 0)
                p.g.wait(p.cv)
                p.g.dma_start(out=xbcbuf[b, ec * 128:(ec + 1) * 128, :],
                              in_=scr[:, 0:S]).inc(chg, 16)
            p.g.wait(chmm)
            for k in range(KCH):
                p.g.dma_start(out=wsb[:, k * 32:(k + 1) * 32],
                              in_=wipT[k * 128:(k + 1) * 128, DI + 2304:DIP]).inc(chg, 16)
            wl = chg.count
            for t5 in range(4):
                slot = t5 % 2
                bank = pfp[slot]
                wait_slot(p.t, pfs[slot])
                p.t.wait(chg, wl)
                p.t.wait(p.cv, xn_done)
                for k in range(KCH):
                    hm = p.t.matmul(bank[0:32, 0:512],
                                    wsb[:, k * 32:(k + 1) * 32],
                                    xnfm[:, k * S + t5 * 512: k * S + (t5 + 1) * 512],
                                    start=(k == 0), stop=(k == KCH - 1))
                hm.inc(chmm)
                p.v.wait(chmm)
                p.v.tensor_copy(out=dtmix[0:32, t5 * 512:(t5 + 1) * 512],
                                in_=bank[0:32, 0:512])
                pfs[slot] = (p.cv.count, 0)

            for ec in range(XCH):
                for t5 in range(4):
                    cb = cbufs[t5 % 2]
                    ch_ = chc[t5 % 2]
                    p.q.wait(p.cs)
                    p.q.wait(p.cv)
                    if t5 == 0:
                        p.v.wait(ch_)
                        p.v.memset(cb[:, 0:3], 0.0)
                        pad_done = p.cv.count
                        p.q.wait(p.cv, pad_done)
                        p.q.dma_start(out=cb[:, 3:515],
                                      in_=xbcbuf[b, ec * 128:(ec + 1) * 128, 0:512]).inc(ch_, 16)
                    else:
                        p.q.dma_start(out=cb[:, 0:515],
                                      in_=xbcbuf[b, ec * 128:(ec + 1) * 128,
                                                 t5 * 512 - 3: t5 * 512 + 512]).inc(ch_, 16)
                    p.v.wait(ch_)
                    cw = cwcol[:, ec * 4: ec * 4 + 4]
                    p.v.tensor_scalar(out=scrf[:, 0:512], in0=cb[:, 0:512],
                                      scalar1=cw[:, 0:1], scalar2=None, op0=ALU.mult)
                    p.v.scalar_tensor_tensor(out=scrf[:, 0:512], in0=cb[:, 1:513],
                                             scalar=cw[:, 1:2], in1=scrf[:, 0:512],
                                             op0=ALU.mult, op1=ALU.add, _sd=1)
                    p.v.scalar_tensor_tensor(out=scrf[:, 0:512], in0=cb[:, 2:514],
                                             scalar=cw[:, 2:3], in1=scrf[:, 0:512],
                                             op0=ALU.mult, op1=ALU.add, _sd=1)
                    p.v.scalar_tensor_tensor(out=scrf[:, 0:512], in0=cb[:, 3:515],
                                             scalar=cw[:, 3:4], in1=scrf[:, 0:512],
                                             op0=ALU.mult, op1=ALU.add, _sd=1)
                    p.s.wait(p.cv)
                    p.s.wait(chg)
                    p.s.activation(out=scr[:, t5 * 512:(t5 + 1) * 512], in_=scrf[:, 0:512],
                                   func=AF.Silu, bias=cbcol[:, ec:ec + 1], scale=1.0)
                p.g.wait(p.cs)
                p.g.dma_start(out=cvbuf[b, ec * 128:(ec + 1) * 128, :],
                              in_=scr[:, 0:S]).inc(chg, 16)

            p.v.wait(p.cs)
            p.v.tensor_scalar(out=dtmix[0:32, 0:S], in0=dtmix[0:32, 0:S],
                              scalar1=dtbc[0:32, 0:1], scalar2=None, op0=ALU.add, _sd=1)
            p.s.wait(p.cv)
            p.s.activation(out=dtmix[0:32, S:2 * S], in_=dtmix[0:32, 0:S],
                           func=AF.Abs)
            p.s.activation(out=dtmix[0:32, S:2 * S], in_=dtmix[0:32, S:2 * S],
                           func=AF.Exp, scale=-1.0, _sd=1)
            p.v.wait(p.cs)
            p.v.tensor_scalar(out=dtmix[0:32, S:2 * S], in0=dtmix[0:32, S:2 * S],
                              scalar1=1.0, scalar2=None, op0=ALU.add)
            p.s.wait(p.cv)
            p.s.activation(out=dtmix[0:32, S:2 * S], in_=dtmix[0:32, S:2 * S],
                           func=AF.Ln)
            p.v.wait(p.cs)
            p.v.scalar_tensor_tensor(out=dtmix[0:32, 0:S], in0=dtmix[0:32, 0:S],
                                     scalar=0.0, in1=dtmix[0:32, S:2 * S], op0=ALU.max,
                                     op1=ALU.add, _sd=1)
            p.v.tensor_scalar(out=dtmix[0:32, S:2 * S], in0=dtmix[0:32, 0:S],
                              scalar1=acol[0:32, 0:1], scalar2=None, op0=ALU.mult, _sd=1)
            p.v.tensor_copy(out=dtmixb[:, :], in_=dtmix[:, :], _sd=1)
            dt_done = p.cv.count
            p.q.wait(chmm)
            p.q.wait(chg)
            p.q.dma_start(out=bfm[:, :], in_=cvbuf[b, 2048:2176, :]).inc(chbc, 16)
            p.q.dma_start(out=cfmS[:, :], in_=cvbuf[b, 2176:2304, :]).inc(chbc, 16)
            bc_load = chbc.count
            hb_mark = p.cg.count

            for c in range(16):
                csl = slice(c * 128, (c + 1) * 128)
                wait_slot(p.t, ptrs[0])
                wait_slot(p.t, ptrs[1])
                p.t.wait(p.cv, dt_done)
                p.t.transpose(ptr[0][:, 0:32], dtmixb[0:32, csl],
                              ident[0:32, 0:32]).inc(chmm)
                p.t.transpose(ptr[1][:, 0:32],
                              dtmixb[0:32, S + c * 128: S + (c + 1) * 128],
                              ident[0:32, 0:32]).inc(chmm)
                p.v.wait(chmm)
                p.v.tensor_copy(out=dtat[:, 0:32], in_=ptr[0][:, 0:32])
                p.v.tensor_copy(out=dtat[:, 32:64], in_=ptr[1][:, 0:32])
                p.v.tensor_copy(out=dtatb[:, :], in_=dtat[:, :], _sd=1)
                ptrs[0] = (p.cv.count, 0)
                ptrs[1] = (p.cv.count, 0)
                dtt_done = p.cv.count
                slot = 0
                bank = pss[slot]
                wait_slot(p.t, psss[slot])
                p.t.wait(p.cv, dtt_done)
                p.t.matmul(bank[:, 0:32], trilS[:, :], dtatb[:, 32:64],
                           start=True, stop=False)
                hm = p.t.matmul(bank[:, 32:64], triuS[:, :], dtatb[:, 32:64],
                                start=False, stop=True)
                hm.inc(chmm)
                aslot = 0
                abank = pfp[aslot]
                wait_slot(p.t, pfs[aslot])
                hm = p.t.matmul(abank[0:32, 0:128], dtatb[:, 32:64], trilS[:, :],
                                start=True, stop=True)
                hm.inc(chmm)
                p.v.wait(chmm)
                p.v.tensor_copy(out=dtat[:, 32:64], in_=bank[:, 0:32])
                p.s.wait(chmm)
                p.s.activation(out=eal[:, :], in_=bank[:, 0:32], func=AF.Exp)
                p.s.activation(out=ebe[:, :], in_=bank[:, 32:64], func=AF.Exp)
                p.v.wait(p.cs)
                p.v.tensor_copy(out=afmb[0:32, :], in_=abank[0:32, 0:128])
                pfs[aslot] = (p.cv.count, 0)
                p.v.tensor_tensor(out=dteb[:, :], in0=dtat[:, 0:32], in1=ebe[:, :],
                                  op=ALU.mult)
                p.v.tensor_tensor(out=gcol[:, :], in0=ebe[:, :], in1=eal[:, :],
                                  op=ALU.mult)
                p.s.wait(p.cv)
                p.s.activation(out=ldtc[:, :], in_=dtat[:, 0:32], func=AF.Ln)
                p.v.wait(p.cs)
                p.v.tensor_tensor(out=lcol[:, :], in0=ldtc[:, :], in1=dtat[:, 32:64],
                                  op=ALU.subtract)
                psss[slot] = (p.cv.count, p.cs.count)
                sh_done = p.cv.count
                p.q.wait(chmm)
                p.q.dma_start(out=ldxh[:, :].rearrange("p (g w) -> p g w", w=128),
                              in_=dram3(cvbuf[b, 0:2048, :])[:, :, csl]).inc(chx, 16)
                xh_load = chx.count
                for pr in range(16):
                    slot = pr % 2
                    bank = ptr[slot]
                    wait_slot(p.t, ptrs[slot])
                    p.t.wait(chx, xh_load)
                    p.t.transpose(bank[:, :], ldxh[:, pr * 128:(pr + 1) * 128],
                                  ident[:, :]).inc(chmm)
                    p.v.wait(chmm)
                    p.v.tensor_copy(out=xhtm[:, pr * 128:(pr + 1) * 128], in_=bank[:, :])
                    ptrs[slot] = (p.cv.count, 0)
                    for hh in range(2):
                        h = pr * 2 + hh
                        dsl = slice(pr * 128 + hh * 64, pr * 128 + (hh + 1) * 64)
                        p.s.wait(p.cv)
                        p.s.activation(out=u2bf[:, dsl], in_=xhtm[:, dsl],
                                       func=AF.Identity, scale=dteb[:, h:h + 1])
                u_done = p.cv.count
                u_done_s = p.cs.count
                slot = 0
                bank = ptr[slot]
                wait_slot(p.t, ptrs[slot])
                p.t.wait(chbc, bc_load)
                p.t.transpose(bank[:, :], bfm[:, csl], ident[:, :]).inc(chmm)
                p.v.wait(chmm)
                p.v.tensor_copy(out=btm[:, :], in_=bank[:, :])
                ptrs[slot] = (p.cv.count, 0)
                btm_done = p.cv.count
                slot = 1
                gbank = pss[slot]
                wait_slot(p.t, psss[slot])
                p.t.wait(chbc, bc_load)
                p.t.matmul(gbank[:, :], bfm[:, csl], cfmS[:, csl],
                           start=True, stop=True).inc(chmm)
                gt_mm = chmm.count
                p.v.wait(chmm, gt_mm)
                p.v.tensor_copy(out=gsbb[:, :], in_=gbank[:, :])
                psss[slot] = (p.cv.count, 0)
                p.v.wait(p.cv)
                ssb_rd = [0, 0]
                dmt_rd = [0, 0]
                t1_rd = [0, 0]
                t2_rd = [0, 0]
                for h in range(NH):
                    par = h % 2
                    hp = slice((h // 2) * 128 + (h % 2) * 64, (h // 2) * 128 + (h % 2 + 1) * 64)
                    psl = slice(par * 128, (par + 1) * 128)
                    p6 = slice(par * P_, (par + 1) * P_)
                    slot = 2 + par
                    dbank = pfp[slot]
                    wait_slot(p.t, pfs[slot])
                    p.t.wait(p.cv, sh_done)
                    p.t.matmul(dbank[:, 0:128], ebk[:, h * 128:(h + 1) * 128],
                               afmb[0:32, :], start=True, stop=True).inc(chmm)
                    p.v.wait(chmm)
                    p.v.wait(p.cs, dmt_rd[par])
                    p.v.scalar_tensor_tensor(out=dmt[:, psl], in0=dbank[:, 0:128],
                                             scalar=lcol[:, h:h + 1],
                                             in1=logtS[:, :], op0=ALU.add,
                                             op1=ALU.add, _sd=0)
                    dm_done = p.cv.count
                    pfs[slot] = (dm_done, 0)
                    p.s.wait(p.cv, dm_done)
                    p.s.activation(out=expd[:, psl], in_=dmt[:, psl], func=AF.Exp)
                    dmt_rd[par] = p.cs.count
                    ed_done = p.cs.count
                    p.v.wait(p.cs, ed_done)
                    p.v.wait(chmm, ssb_rd[par])
                    p.v.tensor_tensor(out=ssb[:, psl], in0=expd[:, psl],
                                      in1=gsbb[:, :], op=ALU.mult, _sd=0)
                    st_done = p.cv.count
                    ybank = pfp[par]
                    wait_slot(p.t, pfs[par])
                    p.t.wait(p.cv, st_done)
                    p.t.matmul(ybank[:, 0:P_], ssb[:, psl], xhtm[:, hp],
                               start=True, stop=True).inc(chmm)
                    yi_mm = chmm.count
                    ssb_rd[par] = yi_mm
                    if c > 0:
                        ibank = pss[par]
                        wait_slot(p.t, psss[par])
                        p.t.wait(p.cg, hb_mark)
                        p.t.matmul(ibank[:, 0:P_], cfmS[:, csl],
                                   hstb[:, h * P_:(h + 1) * P_],
                                   start=True, stop=True).inc(chmm)
                        yint_mm = chmm.count
                        p.s.wait(chmm, yint_mm)
                        p.s.wait(p.cg, t1_rd[par])
                        p.s.activation(out=t1sb[:, p6], in_=ibank[:, 0:P_],
                                       func=AF.Identity, scale=eal[:, h:h + 1])
                        psss[par] = (0, p.cs.count)
                        t1_done = p.cs.count
                        p.g.wait(p.cs, t1_done)
                        p.g.wait(p.cv, max(u_done, t2_rd[par]))
                        p.g.tensor_tensor(out=t2sb[:, p6], in0=xhtm[:, hp],
                                          in1=dskx[:, h * P_:(h + 1) * P_],
                                          op=ALU.mult, _sd=0)
                        p.g.tensor_tensor(out=t2sb[:, p6], in0=t2sb[:, p6],
                                          in1=t1sb[:, p6], op=ALU.add, _sd=1)
                        t1_rd[par] = p.cg.count
                        t2_done = p.cg.count
                    else:
                        p.g.wait(p.cv, max(u_done, t2_rd[par]))
                        p.g.tensor_tensor(out=t2sb[:, p6], in0=xhtm[:, hp],
                                          in1=dskx[:, h * P_:(h + 1) * P_],
                                          op=ALU.mult, _sd=0)
                        t2_done = p.cg.count
                    sbank = pfp[2 + par]
                    wait_slot(p.t, pfs[2 + par])
                    p.t.wait(p.cv, btm_done)
                    p.t.wait(p.cs, u_done_s)
                    p.t.matmul(sbank[:, 0:P_], btm[:, :], u2bf[:, hp],
                               start=True, stop=True).inc(chmm)
                    hst_mm = chmm.count
                    p.v.wait(chmm, yi_mm)
                    p.v.wait(p.cg, t2_done)
                    if h == 0:
                        p.v.wait(chg)
                    p.v.tensor_tensor(out=ytm[:, h * P_:(h + 1) * P_],
                                      in0=ybank[:, 0:P_], in1=t2sb[:, p6],
                                      op=ALU.add, _sd=0)
                    t2_rd[par] = p.cv.count
                    pfs[par] = (p.cv.count, 0)
                    p.v.wait(chmm, hst_mm)
                    if c > 0:
                        p.v.scalar_tensor_tensor(out=hstate[:, h * P_:(h + 1) * P_],
                                                 in0=hstate[:, h * P_:(h + 1) * P_],
                                                 scalar=gcol[:, h:h + 1],
                                                 in1=sbank[:, 0:P_], op0=ALU.mult,
                                                 op1=ALU.add, _sd=0)
                    else:
                        p.v.tensor_copy(out=hstate[:, h * P_:(h + 1) * P_],
                                        in_=sbank[:, 0:P_], _sd=0)
                    pfs[2 + par] = (p.cv.count, 0)
                    hs_done = p.cv.count
                    p.g.wait(p.cv, hs_done)
                    p.g.tensor_copy(out=hstb[:, h * P_:(h + 1) * P_],
                                    in_=hstate[:, h * P_:(h + 1) * P_], _sd=0)
                hb_mark = p.cg.count
                p.v.wait(p.cg)
                p.g.wait(p.cv)
                p.g.dma_start(out=ybuf[b, csl, :], in_=ytm[:, :]).inc(chg, 16)
            p.v.wait(chmm)
            p.v.memset(hstate[:, :], 0.0)

        p.g.wait(chmm)
        for k in range(ZCH):
            p.g.dma_start(out=wsb[:, k * D:(k + 1) * D],
                          in_=woutT[k * 128:(k + 1) * 128, :]).inc(chg, 16)
        wl = chg.count
        p.v.wait(chg, wl)
        for k in range(ZCH):
            p.v.tensor_scalar(out=wsb[:, k * D:(k + 1) * D], in0=wsb[:, k * D:(k + 1) * D],
                              scalar1=nwcol[:, k:k + 1], scalar2=None, op0=ALU.mult)
        wout_done = p.cv.count
        for b in range(n_b):
            for tt in range(TT):
                tsl = slice(tt * 128, (tt + 1) * 128)
                p.q.wait(p.cs)
                p.q.wait(p.cv)
                p.q.wait(chst)
                p.q.dma_start(out=ytile[:, :], in_=ybuf[b, tsl, :]).inc(chq, 16)
                p.q.dma_start(out=sztile[:, :], in_=zbuf[b, tsl, :]).inc(chq, 16)
                p.q.dma_start(out=atile[:, :], in_=abuf[b, tsl, :]).inc(chq, 16)
                p.q.dma_start(out=xtile[:, :], in_=xs[b, tsl, :]).inc(chq, 16)
                p.v.wait(chq)
                p.v.wait(chmm)
                p.v.tensor_tensor(out=ytile[:, :], in0=ytile[:, :], in1=sztile[:, :],
                                  op=ALU.mult)
                p.s.wait(p.cv)
                p.s.activation(out=scr[:, 0:DI], in_=ytile[:, :], func=AF.Square,
                               accum_out=sc[:, 8:9])
                p.v.wait(p.cs)
                p.v.tensor_scalar(out=sc[:, 9:10], in0=sc[:, 8:9], scalar1=1.0 / DI,
                                  scalar2=EPS, op0=ALU.mult, op1=ALU.add)
                p.s.wait(p.cv)
                p.s.activation(out=sc[:, 10:11], in_=sc[:, 9:10], func=AF.Sqrt)
                p.v.wait(p.cs)
                p.v.reciprocal(out=sc[:, 10:11], in_=sc[:, 10:11])
                p.v.tensor_scalar(out=ygb[:, :], in0=ytile[:, :], scalar1=sc[:, 10:11],
                                  scalar2=None, op0=ALU.mult, _sd=1)
                yg_done = p.cv.count
                for k in range(ZCH):
                    slot = k % 2
                    bank = ptr[slot]
                    wait_slot(p.t, ptrs[slot])
                    p.t.wait(p.cv, yg_done)
                    p.t.wait(p.cv, wout_done)
                    p.t.transpose(bank[:, :], ygb[:, k * 128:(k + 1) * 128],
                                  ident[:, :]).inc(chmm)
                    p.v.wait(chmm)
                    p.v.tensor_copy(out=ygfm[:, k * 128:(k + 1) * 128], in_=bank[:, :])
                    ptrs[slot] = (p.cv.count, 0)
                ygfm_done = p.cv.count
                for nn in range(2):
                    slot = nn % 2
                    bank = pfp[slot]
                    wait_slot(p.t, pfs[slot])
                    p.t.wait(p.cv, ygfm_done)
                    for k in range(ZCH):
                        hm = p.t.matmul(bank[:, :], ygfm[:, k * 128:(k + 1) * 128],
                                        wsb[:, k * D + nn * 512: k * D + (nn + 1) * 512],
                                        start=(k == 0), stop=(k == ZCH - 1))
                    hm.inc(chmm)
                    p.v.wait(chmm)
                    p.v.tensor_copy(out=rtile[:, nn * 512:(nn + 1) * 512], in_=bank[:, :])
                    pfs[slot] = (p.cv.count, 0)
                p.s.wait(p.cv)
                nmu, rcol, nmr = emit_ln_stats(p, rtile[:, :], D, sc, 0, scr)
                p.v.tensor_scalar(out=rtile[:, :], in0=rtile[:, :], scalar1=rcol,
                                  scalar2=nmr, op0=ALU.mult, op1=ALU.add, _sd=1)
                p.v.tensor_tensor(out=rtile[:, :], in0=rtile[:, :], in1=lngb[:, 0:D],
                                  op=ALU.mult, _sd=1)
                p.v.tensor_tensor(out=rtile[:, :], in0=rtile[:, :], in1=lngb[:, D:2 * D],
                                  op=ALU.add, _sd=1)
                p.v.tensor_tensor(out=rtile[:, :], in0=rtile[:, :], in1=atile[:, :],
                                  op=ALU.add, _sd=1)
                p.s.wait(p.cv)
                nmu, rcol, nmr = emit_ln_stats(p, rtile[:, :], D, sc, 8, scr)
                p.v.tensor_scalar(out=xtile[:, :], in0=xtile[:, :], scalar1=nmr,
                                  scalar2=None, op0=ALU.add, _sd=1)
                p.v.scalar_tensor_tensor(out=xtile[:, :], in0=rtile[:, :], scalar=rcol,
                                         in1=xtile[:, :], op0=ALU.mult, op1=ALU.add, _sd=1)
                p.g.wait(p.cv)
                p.g.dma_start(out=xs1[b, tsl, :], in_=xtile[:, :]).inc(chst, 16)

        p.g.wait(chst)
        p.finalize()
    return nc


def _host_l0attn_inputs(x0_full, P):
    NH = 32
    wipT = np.ascontiguousarray(P["L0_attn_in_proj_w"].T)
    woutT = np.ascontiguousarray(P["L0_attn_out_w"].T)
    ebank = np.zeros((NH, NH * 128), ml_dtypes.bfloat16)
    for h in range(NH):
        ebank[h, h * 128:(h + 1) * 128] = 1.0
    s_ = np.arange(128)[:, None]
    t_ = np.arange(128)[None, :]
    tril = (s_ <= t_).astype(ml_dtypes.bfloat16)
    trius = (s_ > t_).astype(ml_dtypes.bfloat16)
    logtri = np.where(s_ <= t_, 0.0, -1e30).astype(np.float32)
    in_maps = []
    for c in range(N_CORES):
        in_maps.append({
            "xs": np.ascontiguousarray(x0_full[c * 4:(c + 1) * 4]),
            "wipT": wipT,
            "convw": np.ascontiguousarray(P["L0_attn_conv_w"]),
            "convb": np.ascontiguousarray(P["L0_attn_conv_b"][:, None]),
            "dtb": np.ascontiguousarray(P["L0_attn_dt_bias"][:, None]),
            "alog": np.ascontiguousarray(P["L0_attn_A_log"][:, None]),
            "dskip": np.ascontiguousarray(P["L0_attn_D_skip"][None, :]).astype(ml_dtypes.bfloat16),
            "dskrep": np.ascontiguousarray(np.repeat(P["L0_attn_D_skip"], 64)[None, :]).astype(ml_dtypes.bfloat16),
            "normw": np.ascontiguousarray(P["L0_attn_norm_w"][:, None]),
            "woutT": woutT,
            "lng": np.ascontiguousarray(P["L0_attn_ln_g"][None, :]).astype(ml_dtypes.bfloat16),
            "lnb": np.ascontiguousarray(P["L0_attn_ln_b"][None, :]).astype(ml_dtypes.bfloat16),
            "ebank": ebank, "tril": tril, "trius": trius, "logtri": logtri,
            "ident": _ident128(),
        })
    return in_maps




def build_l1all(n_b=4, S=2048):
    D = 1024
    E, F = 8, 2048
    NTU = n_b * S // 512

    nc = bass.Bass("TRN2", target_bir_lowering=False)
    xs2 = nc.dram_tensor("xs2", [n_b, S, D], F32, kind="ExternalInput").ap()
    wqkv = nc.dram_tensor("wqkv", [D, 3 * D], F32, kind="ExternalInput").ap()
    proj = nc.dram_tensor("proj", [16, 64, 256], F32, kind="ExternalInput").ap()
    wout = nc.dram_tensor("wout", [D, D], F32, kind="ExternalInput").ap()
    bout = nc.dram_tensor("bout", [1, D], F32, kind="ExternalInput").ap()
    g1 = nc.dram_tensor("g1", [1, D], F32, kind="ExternalInput").ap()
    wdown = nc.dram_tensor("wdown", [D, 512], F32, kind="ExternalInput").ap()
    wqkvc = nc.dram_tensor("wqkvc", [512, 3 * 512], F32, kind="ExternalInput").ap()
    wup = nc.dram_tensor("wup", [512, D], F32, kind="ExternalInput").ap()
    cosr = nc.dram_tensor("cosr", [S, 256], BF16, kind="ExternalInput").ap()
    sinr = nc.dram_tensor("sinr", [S, 256], BF16, kind="ExternalInput").ap()
    cmask = nc.dram_tensor("cmask", [128, 4 * 512], F32, kind="ExternalInput").ap()
    g2 = nc.dram_tensor("g2", [1, D], F32, kind="ExternalInput").ap()
    rw = nc.dram_tensor("rw", [D, E], F32, kind="ExternalInput").ap()
    w1 = nc.dram_tensor("w1", [E, D, F], F32, kind="ExternalInput").ap()
    w3 = nc.dram_tensor("w3", [E, D, F], F32, kind="ExternalInput").ap()
    w2 = nc.dram_tensor("w2", [E, F, D], F32, kind="ExternalInput").ap()
    ident_in = nc.dram_tensor("ident", [128, 128], BF16, kind="ExternalInput").ap()
    xs4 = nc.dram_tensor("xs4", [n_b, S, D], F32, kind="ExternalOutput").ap()

    x3buf = nc.dram_tensor("x3buf", [n_b, S, D], F32, kind="Internal").ap()
    xpbuf = nc.dram_tensor("xpbuf", [n_b, S, D], F32, kind="Internal").ap()
    tfm = nc.dram_tensor("tfm", [n_b, D, S], BF16, kind="Internal").ap()
    hidb = nc.dram_tensor("hidb", [E, NTU, F, 512], BF16, kind="Internal").ap()
    ybuf = nc.dram_tensor("ybuf", [n_b * S, D], F32, kind="Internal").ap()

    es = contextlib.ExitStack()
    with es:
        p = Prog(nc, es)
        chs = _mk_chans(p)

        sub = contextlib.ExitStack()
        p.alloc_es, p.prefix = sub, "a_"
        emit_l1attn(nc, p, chs, xs2, wqkv, proj, wout, bout, ident_in, x3buf, n_b, S)
        p.barrier("tqvsg")
        sub.close()

        sub = contextlib.ExitStack()
        p.alloc_es, p.prefix = sub, "b_"
        emit_l1ffn_a(nc, p, chs, x3buf, g1, wdown, wqkvc, wup, cosr, sinr, cmask,
                     ident_in, xpbuf, n_b, S)
        p.barrier("tqvsg")
        sub.close()

        sub = contextlib.ExitStack()
        p.alloc_es, p.prefix = sub, "c_"
        emit_l1ffn_b(nc, p, chs, x3buf, xpbuf, g2, rw, w1, w3, w2, ident_in, xs4,
                     tfm, hidb, ybuf, n_b, S)
        p.g.wait(chs["chst"])
        p.finalize()
        sub.close()
    return nc


def _host_l1all_inputs(x2_full, P):
    wqkvc = np.ascontiguousarray(
        np.concatenate([P["L1_ffn_wq"], P["L1_ffn_wk"], P["L1_ffn_wv"]], axis=1))
    cosr, sinr = _rope_tables()
    cosr = cosr.astype(ml_dtypes.bfloat16)
    sinr = sinr.astype(ml_dtypes.bfloat16)
    in_maps = []
    for c in range(N_CORES):
        in_maps.append({
            "xs2": np.ascontiguousarray(x2_full[c * 4:(c + 1) * 4]),
            "wqkv": np.ascontiguousarray(P["L1_attn_w_qkv"]),
            "proj": np.ascontiguousarray(P["L1_attn_proj_matrix"]),
            "wout": np.ascontiguousarray(P["L1_attn_w_out"]),
            "bout": np.ascontiguousarray(P["L1_attn_b_out"][None, :]).astype(np.float32),
            "g1": np.ascontiguousarray(P["L1_ffn_g1"][None, :]),
            "wdown": np.ascontiguousarray(P["L1_ffn_w_down"]),
            "wqkvc": wqkvc,
            "wup": np.ascontiguousarray(P["L1_ffn_w_up"]),
            "cosr": cosr, "sinr": sinr, "cmask": _cmask_phases(),
            "g2": np.ascontiguousarray(P["L1_ffn_g2"][None, :]),
            "rw": np.ascontiguousarray(P["L1_ffn_router_w"]),
            "w1": np.ascontiguousarray(P["L1_ffn_w1"]),
            "w3": np.ascontiguousarray(P["L1_ffn_w3"]),
            "w2": np.ascontiguousarray(P["L1_ffn_w2"]),
            "ident": _ident128(),
        })
    return in_maps



def _ident128():
    return np.eye(128, dtype=ml_dtypes.bfloat16)


def _host_l0ffn_inputs(x1_full, P, n_steps=32):
    wtxT = np.ascontiguousarray(
        np.concatenate([P["L0_ffn_wf"], P["L0_ffn_wi"], P["L0_ffn_wc"]], axis=0).T)
    utT = np.ascontiguousarray(
        np.concatenate([P["L0_ffn_uf"], P["L0_ffn_ui"], P["L0_ffn_uc"]], axis=0).T)
    b3 = np.ascontiguousarray(
        np.concatenate([P["L0_ffn_bf"], P["L0_ffn_bi"], P["L0_ffn_bc"]])[None, :]).astype(np.float32)
    wot = np.ascontiguousarray(P["L0_ffn_wo"].T)
    uot = np.ascontiguousarray(P["L0_ffn_uo"].T)
    bo = np.ascontiguousarray(P["L0_ffn_bo"][None, :]).astype(np.float32)
    in_c = np.ascontiguousarray(P["L0_ffn_in_c"][None, :]).astype(np.float32)
    ident = _ident128()
    in_maps = []
    for c in range(N_CORES):
        xsc = np.ascontiguousarray(x1_full[:n_steps, c * 256:(c + 1) * 256, :])
        in_maps.append({
            "xs": xsc, "wtx": wtxT, "ut": utT, "b3": b3, "wot": wot, "uot": uot,
            "bo": bo, "in_c": in_c, "ident": ident,
        })
    return in_maps


def run_l0ffn(x1_full, P, n_steps=32):
    nc = build_l0ffn(n_steps=n_steps)
    in_maps = _host_l0ffn_inputs(x1_full, P, n_steps)
    res = run_bass_kernel_spmd(nc, in_maps, core_ids=list(range(N_CORES)))
    x2 = np.empty((n_steps, 2048, 1024), np.float32)
    for c in range(N_CORES):
        x2[:, c * 256:(c + 1) * 256, :] = res.results[c]["xs2"]
    return x2


_NC_CACHE = {}


def _get_nc(name, builder):
    if name not in _NC_CACHE:
        _NC_CACHE[name] = builder()
    return _NC_CACHE[name]


def kernel(**inputs):
    P = {k: np.ascontiguousarray(np.asarray(v, dtype=np.float32)) for k, v in inputs.items()}
    x0 = P["L0_attn_x"]

    nc1 = _get_nc("l0attn", build_l0attn)
    res = run_bass_kernel_spmd(nc1, _host_l0attn_inputs(x0, P), core_ids=list(range(N_CORES)))
    x1 = np.empty((32, 2048, 1024), np.float32)
    for c in range(N_CORES):
        x1[c * 4:(c + 1) * 4] = res.results[c]["xs1"]

    x2 = run_l0ffn(x1, P)

    nc3 = _get_nc("l1all", build_l1all)
    res = run_bass_kernel_spmd(nc3, _host_l1all_inputs(x2, P), core_ids=list(range(N_CORES)))
    x4 = np.empty((32, 2048, 1024), np.float32)
    for c in range(N_CORES):
        x4[c * 4:(c + 1) * 4] = res.results[c]["xs4"]
    return x4


# version 04
# speedup vs baseline: 1.0711x; 1.0711x over previous
import contextlib
import numpy as np
import ml_dtypes

import concourse.bass as bass
import concourse.mybir as mybir
from concourse.bass_utils import run_bass_kernel_spmd

F32 = mybir.dt.float32
BF16 = mybir.dt.bfloat16
AF = mybir.ActivationFunctionType
ALU = mybir.AluOpType
AX = mybir.AxisListType

N_CORES = 8
EPS = 1e-5


ENGINES = ("sync", "tensor", "vector", "scalar", "gpsimd")


class Chan:

    __slots__ = ("sem", "count", "name")

    def __init__(self, sem, name):
        self.sem = sem
        self.count = 0
        self.name = name


class _Rec:
    __slots__ = ("method", "args", "kwargs", "incs")

    def __init__(self, method, args, kwargs):
        self.method = method
        self.args = args
        self.kwargs = kwargs
        self.incs = []


class _RecHandle:
    __slots__ = ("rec", "_chans")

    def __init__(self, rec):
        self.rec = rec

    def inc(self, ch, n=1):
        self.rec.incs.append((ch.sem, n))
        ch.count += n
        return self


_NOSERIAL = {"wait_ge", "dma_start", "engine_nop", "nop", "indirect_dma_start",
             "collective_compute", "partition_id", "reg_save", "dma_start_transpose"}
_SERIAL_ENGINES = {"vector", "scalar", "gpsimd"}


class _EngRec:
    def __init__(self, prog, name):
        self._prog = prog
        self._name = name

    def __getattr__(self, method):
        prog, name = self._prog, self._name

        def call(*args, **kwargs):
            sd = kwargs.pop("_sd", 2)
            serial = name in _SERIAL_ENGINES and method not in _NOSERIAL
            if serial and sd > 0:
                sc = prog.self_chan[name]
                v = sc.count - (sd - 1)
                if v > 0:
                    prog.ops[name].append(_Rec("wait_ge", (sc.sem, v), {}))
            rec = _Rec(method, args, kwargs)
            prog.ops[name].append(rec)
            h = _RecHandle(rec)
            if serial:
                h.inc(prog.self_chan[name])
            return h

        return call

    def wait(self, ch, value=None):
        v = ch.count if value is None else value
        if v <= 0:
            return
        rec = _Rec("wait_ge", (ch.sem, v), {})
        self._prog.ops[self._name].append(rec)


class Prog:

    def __init__(self, nc, es):
        self.nc = nc
        self.es = es
        self.alloc_es = es
        self.prefix = ""
        self.ops = {e: [] for e in ENGINES}
        self.chans = []
        self.self_chan = {}
        for e in sorted(_SERIAL_ENGINES):
            self.self_chan[e] = self.chan(f"self_{e}")
        self.q = _EngRec(self, "sync")
        self.t = _EngRec(self, "tensor")
        self.v = _EngRec(self, "vector")
        self.s = _EngRec(self, "scalar")
        self.g = _EngRec(self, "gpsimd")
        self.cv = self.self_chan["vector"]
        self.cs = self.self_chan["scalar"]
        self.cg = self.self_chan["gpsimd"]

    def chan(self, name):
        sem = self.es.enter_context(self.nc.semaphore(name))
        c = Chan(sem, name)
        self.chans.append(c)
        return c

    def sbuf(self, name, shape, dtype):
        return self.alloc_es.enter_context(
            self.nc.sbuf_tensor(self.prefix + name, shape, dtype))

    def psum(self, name, shape, dtype):
        return self.alloc_es.enter_context(
            self.nc.psum_tensor(self.prefix + name, shape, dtype))

    def barrier(self, engines="tqvsg"):
        m = {"q": self.q, "t": self.t, "v": self.v, "s": self.s, "g": self.g}
        for e in engines:
            for c in self.chans:
                m[e].wait(c)

    def finalize(self):
        block = self.es.enter_context(self.nc.Block())
        for eng_name in ENGINES:
            ops = self.ops[eng_name]
            if not ops:
                continue

            def mk(ops):
                def body(eng):
                    for rec in ops:
                        instr = getattr(eng, rec.method)(*rec.args, **rec.kwargs)
                        for sem, n in rec.incs:
                            instr.then_inc(sem, n)

                return body

            getattr(block, eng_name)(mk(ops))



def emit_ln_stats(p, x_ap, width, sc, col0, scr, eps=EPS):
    msum = sc[:, col0 + 0: col0 + 1]
    nmu = sc[:, col0 + 1: col0 + 2]
    vsum = sc[:, col0 + 2: col0 + 3]
    rcol = sc[:, col0 + 3: col0 + 4]
    nmr = sc[:, col0 + 4: col0 + 5]
    p.s.activation(out=scr[:, 0:width], in_=x_ap, func=AF.Identity,
                   accum_out=msum)
    p.v.wait(p.cs)
    p.v.tensor_scalar(out=nmu, in0=msum, scalar1=-1.0 / width, scalar2=None,
                      op0=ALU.mult)
    p.s.wait(p.cv)
    p.s.activation(out=scr[:, 0:width], in_=x_ap, func=AF.Square, bias=nmu,
                   scale=1.0, accum_out=vsum)
    p.v.wait(p.cs)
    p.v.tensor_scalar(out=vsum, in0=vsum, scalar1=1.0 / width, scalar2=eps,
                      op0=ALU.mult, op1=ALU.add)
    p.s.wait(p.cv)
    p.s.activation(out=rcol, in_=vsum, func=AF.Sqrt)
    p.v.wait(p.cs)
    p.v.reciprocal(out=rcol, in_=rcol)
    p.v.scalar_tensor_tensor(out=nmr, in0=nmu, scalar=1.0, in1=rcol,
                             op0=ALU.mult, op1=ALU.mult, _sd=1)
    return nmu, rcol, nmr




def dram3(x2d, p=128):
    return x2d.rearrange("(g p) w -> p g w", p=p)


def sb3(t, g, w):
    return t[:, :].rearrange("p (g w) -> p g w", g=g)


def build_l0ffn(n_steps=32, rows=256):
    D = 1024
    E3 = 3072
    KCH = D // 128
    RT = rows // 128

    nc = bass.Bass("TRN2", target_bir_lowering=False)
    xs = nc.dram_tensor("xs", [n_steps, rows, D], F32, kind="ExternalInput").ap()
    wtx = nc.dram_tensor("wtx", [D, E3], F32, kind="ExternalInput").ap()
    ut = nc.dram_tensor("ut", [D, E3], F32, kind="ExternalInput").ap()
    b3 = nc.dram_tensor("b3", [1, E3], F32, kind="ExternalInput").ap()
    wot = nc.dram_tensor("wot", [D, D], F32, kind="ExternalInput").ap()
    uot = nc.dram_tensor("uot", [D, D], F32, kind="ExternalInput").ap()
    bo = nc.dram_tensor("bo", [1, D], F32, kind="ExternalInput").ap()
    in_c = nc.dram_tensor("in_c", [1, D], F32, kind="ExternalInput").ap()
    ident_in = nc.dram_tensor("ident", [128, 128], BF16, kind="ExternalInput").ap()
    xs2 = nc.dram_tensor("xs2", [n_steps, rows, D], F32, kind="ExternalOutput").ap()

    xnfm = nc.dram_tensor("xnfm", [n_steps, D, rows], BF16, kind="Internal").ap()
    hbuf = nc.dram_tensor("hbuf", [n_steps, rows, D], F32, kind="Internal").ap()

    es = contextlib.ExitStack()
    with es:
        p = Prog(nc, es)
        wbig = p.sbuf("wbig", [128, 2 * KCH * E3], BF16)
        b3r = p.sbuf("b3r", [1, E3], BF16)
        bor = p.sbuf("bor", [1, D], BF16)
        onesr = p.sbuf("onesr", [1, rows], BF16)
        ident = p.sbuf("identS", [128, 128], BF16)
        incol = p.sbuf("incol", [128, KCH], F32)
        onesT = p.sbuf("onesT", [128, rows], F32)
        cfm = p.sbuf("cfm", [128, KCH * rows], F32)
        cbf = p.sbuf("cbf", [128, KCH * rows], BF16)
        fsb = p.sbuf("fsb", [128, KCH * rows], F32)
        isb = p.sbuf("isb", [128, KCH * rows], F32)
        gsb = p.sbuf("gsb", [128, KCH * rows], F32)
        tmp1 = p.sbuf("tmp1", [128, rows], F32)
        tmp2 = p.sbuf("tmp2", [128, rows], F32)
        xraw = [p.sbuf(f"xraw{i}", [128, RT * D], F32) for i in range(2)]
        xtm = p.sbuf("xtm", [128, RT * D], BF16)
        xfm = [p.sbuf(f"xfm{i}", [128, KCH * rows], BF16) for i in range(2)]
        scr = p.sbuf("scr", [128, D], BF16)
        sc = p.sbuf("sc", [128, 16], F32)
        cuo = p.sbuf("cuo", [128, RT * D], F32)
        htile = p.sbuf("htile", [128, RT * D], F32)
        ps = [p.psum(f"ps{i}", [128, 512], F32) for i in range(6)]
        pt = [p.psum(f"pt{i}", [128, 128], BF16) for i in range(2)]

        chq = p.chan("chq")
        chg = p.chan("chg")
        chmm = p.chan("chmm")
        chst = p.chan("chst")

        p.g.dma_start(out=ident[:, :], in_=ident_in[:, :]).inc(chg, 16)
        p.g.dma_start(out=b3r[0:1, :], in_=b3[:, :]).inc(chg, 16)
        p.g.dma_start(out=bor[0:1, :], in_=bo[:, :]).inc(chg, 16)
        p.g.dma_start(out=incol[:, 0:KCH],
                      in_=in_c.rearrange("one (k p) -> (one p) k", p=128),
                      allow_slow_non_contiguous=True).inc(chg, 16)
        for k in range(KCH):
            p.g.dma_start(out=wbig[:, k * E3:(k + 1) * E3],
                          in_=wtx[k * 128:(k + 1) * 128, :]).inc(chg, 16)
            p.g.dma_start(out=wbig[:, (KCH + k) * E3:(KCH + k + 1) * E3],
                          in_=ut[k * 128:(k + 1) * 128, :]).inc(chg, 16)
        p.v.memset(onesr[0:1, :], 1.0)
        p.v.memset(onesT[:, :], 1.0)
        p.s.wait(chg)
        p.s.wait(p.cv)
        for k in range(KCH):
            p.s.activation(out=cfm[:, k * rows:(k + 1) * rows], in_=onesT[:, 0:rows],
                           func=AF.Identity, scale=incol[:, k:k + 1])
        p.v.wait(p.cs)
        p.v.tensor_copy(out=cbf[:, :], in_=cfm[:, :])
        cbf_ready = p.cv.count
        p.t.wait(chg)
        p.t.wait(p.cv)

        tr_slot_ev = [0, 0]
        for t in range(n_steps):
            buf = xraw[t % 2]
            p.q.wait(p.cs)
            p.q.wait(p.cv)
            p.q.dma_start(out=sb3(buf, RT, D), in_=dram3(xs[t])).inc(chq, 16)
            p.s.wait(chq)
            p.v.wait(chq)
            p.v.wait(chmm)
            for rt in range(RT):
                src0 = buf[:, rt * D:(rt + 1) * D]
                nmu, rcol, nmr = emit_ln_stats(p, src0, D, sc, 0 if rt == 0 else 8, scr)
                p.v.tensor_scalar(out=xtm[:, rt * D:(rt + 1) * D], in0=src0,
                                  scalar1=rcol, scalar2=nmr, op0=ALU.mult,
                                  op1=ALU.add, _sd=1)
            ln_done = p.cv.count
            fbuf = xfm[t % 2]
            for k in range(KCH):
                for rt in range(RT):
                    slot = (k * RT + rt) % 2
                    bank = pt[slot]
                    p.t.wait(p.cv, ln_done)
                    p.t.wait(p.cs, tr_slot_ev[slot])
                    p.t.transpose(bank[:, :],
                                  xtm[:, rt * D + k * 128: rt * D + (k + 1) * 128],
                                  ident[:, :]).inc(chmm)
                    p.s.wait(chmm)
                    p.s.activation(out=fbuf[:, k * rows + rt * 128: k * rows + (rt + 1) * 128],
                                   in_=bank[:, :], func=AF.Identity)
                    tr_slot_ev[slot] = p.cs.count
            p.g.wait(p.cs)
            p.g.dma_start(out=dram3(xnfm[t]),
                          in_=sb3(fbuf, KCH, rows)).inc(chg, 16)

        p.barrier("tqvsg")

        slot_ev = [0] * 6
        for t in range(n_steps):
            fbuf = xfm[t % 2]
            p.q.wait(chmm)
            p.q.dma_start(out=sb3(fbuf, KCH, rows), in_=dram3(xnfm[t])).inc(chq, 16)
            xload = chq.count
            cupd = p.cv.count
            for pair in range(12):
                slot = pair % 6
                bank = ps[slot]
                p.t.wait(p.cs, slot_ev[slot])
                if pair == 0:
                    p.t.wait(chq, xload)
                    p.t.wait(p.cv, cupd if t > 0 else cbf_ready)
                for half in range(2):
                    ec = pair * 2 + half
                    outap = bank[:, half * rows:(half + 1) * rows]
                    p.t.matmul(outap, b3r[0:1, ec * 128:(ec + 1) * 128],
                               onesr[0:1, 0:rows],
                               start=(half == 0), stop=False)
                    for k in range(KCH):
                        p.t.matmul(outap,
                                   wbig[:, k * E3 + ec * 128: k * E3 + (ec + 1) * 128],
                                   fbuf[:, k * rows:(k + 1) * rows], start=False,
                                   stop=False)
                    for k in range(KCH):
                        h = p.t.matmul(outap,
                                       wbig[:, (KCH + k) * E3 + ec * 128: (KCH + k) * E3 + (ec + 1) * 128],
                                       cbf[:, k * rows:(k + 1) * rows], start=False,
                                       stop=(half == 1 and k == KCH - 1))
                h.inc(chmm)
                p.s.wait(chmm)
                for half in range(2):
                    ec = pair * 2 + half
                    outap = bank[:, half * rows:(half + 1) * rows]
                    func = AF.Sigmoid if ec < 16 else AF.Tanh
                    dst = fsb if ec < 8 else (isb if ec < 16 else gsb)
                    j = ec % 8
                    p.s.activation(out=dst[:, j * rows:(j + 1) * rows], in_=outap,
                                   func=func)
                slot_ev[slot] = p.cs.count
            mm_all = chmm.count
            ev_all = p.cs.count
            p.v.wait(p.cs, ev_all)
            p.v.wait(chmm, mm_all)
            for j in range(KCH):
                sl = slice(j * rows, (j + 1) * rows)
                p.v.tensor_tensor(out=tmp1[:, 0:rows], in0=isb[:, sl], in1=gsb[:, sl],
                                  op=ALU.mult)
                p.v.tensor_tensor(out=tmp2[:, 0:rows], in0=fsb[:, sl], in1=cfm[:, sl],
                                  op=ALU.mult)
                p.v.tensor_tensor(out=cfm[:, sl], in0=tmp2[:, 0:rows],
                                  in1=tmp1[:, 0:rows], op=ALU.add, _sd=1)
                p.v.tensor_copy(out=cbf[:, sl], in_=cfm[:, sl], _sd=1)

        p.barrier("tqvsg")

        for k in range(KCH):
            p.g.dma_start(out=wbig[:, k * D:(k + 1) * D],
                          in_=wot[k * 128:(k + 1) * 128, :]).inc(chg, 16)
            p.g.dma_start(out=wbig[:, (KCH + k) * D:(KCH + k + 1) * D],
                          in_=uot[k * 128:(k + 1) * 128, :]).inc(chg, 16)
        wload = chg.count
        for rt in range(RT):
            for ot in range(2):
                bank = ps[rt * 2 + ot]
                outap = bank[:, 0:512]
                p.t.wait(chg, wload)
                p.t.matmul(outap, onesr[0:1, 0:128], bor[0:1, ot * 512:(ot + 1) * 512],
                           start=True, stop=False)
                for k in range(KCH):
                    h = p.t.matmul(outap,
                                   cbf[:, k * rows + rt * 128: k * rows + (rt + 1) * 128],
                                   wbig[:, (KCH + k) * D + ot * 512: (KCH + k) * D + (ot + 1) * 512],
                                   start=False, stop=(k == KCH - 1))
                h.inc(chmm)
                p.v.wait(chmm)
                p.v.tensor_copy(out=cuo[:, rt * D + ot * 512: rt * D + (ot + 1) * 512],
                                in_=outap)
        cuo_ready = p.cv.count

        hslot_dv = [0] * 4
        for t in range(n_steps):
            fbuf = xfm[t % 2]
            p.q.wait(chmm)
            p.q.dma_start(out=sb3(fbuf, KCH, rows), in_=dram3(xnfm[t])).inc(chq, 16)
            xload = chq.count
            p.v.wait(chg)
            for rt in range(RT):
                for ot in range(2):
                    slot = rt * 2 + ot
                    bank = ps[slot]
                    outap = bank[:, 0:512]
                    p.t.wait(chq, xload)
                    p.t.wait(p.cv, hslot_dv[slot] if t > 0 else cuo_ready)
                    for k in range(KCH):
                        h = p.t.matmul(outap,
                                       fbuf[:, k * rows + rt * 128: k * rows + (rt + 1) * 128],
                                       wbig[:, k * D + ot * 512: k * D + (ot + 1) * 512],
                                       start=(k == 0), stop=(k == KCH - 1))
                    h.inc(chmm)
                    hsl = slice(rt * D + ot * 512, rt * D + (ot + 1) * 512)
                    p.v.wait(chmm)
                    p.v.tensor_tensor(out=htile[:, hsl], in0=outap, in1=cuo[:, hsl],
                                      op=ALU.add)
                    hslot_dv[slot] = p.cv.count
                    p.s.wait(p.cv)
                    p.s.activation(out=htile[:, hsl], in_=htile[:, hsl],
                                   func=AF.Tanh)
            p.g.wait(p.cs)
            p.g.dma_start(out=dram3(hbuf[t]), in_=sb3(htile, RT, D)).inc(chg, 16)

        p.barrier("tqvsg")

        for t in range(n_steps):
            hb = xraw[t % 2]
            p.q.wait(p.cs)
            p.q.wait(p.cv)
            p.q.dma_start(out=sb3(hb, RT, D), in_=dram3(hbuf[t])).inc(chq, 16)
            p.q.dma_start(out=sb3(cuo, RT, D), in_=dram3(xs[t])).inc(chq, 16)
            p.s.wait(chq)
            p.v.wait(chq)
            p.v.wait(chst)
            for rt in range(RT):
                src0 = hb[:, rt * D:(rt + 1) * D]
                nmu, rcol, nmr = emit_ln_stats(p, src0, D, sc, 0 if rt == 0 else 8, scr)
                hsl = slice(rt * D, (rt + 1) * D)
                p.v.tensor_scalar(out=htile[:, hsl], in0=cuo[:, hsl], scalar1=nmr,
                                  scalar2=None, op0=ALU.add, _sd=1)
                p.v.scalar_tensor_tensor(out=htile[:, hsl], in0=src0, scalar=rcol,
                                         in1=htile[:, hsl], op0=ALU.mult,
                                         op1=ALU.add, _sd=1)
            p.g.wait(p.cv)
            p.g.dma_start(out=dram3(xs2[t]), in_=sb3(htile, RT, D)).inc(chst, 16)

        p.g.wait(chst)
        p.finalize()
    return nc



def _mk_chans(p):
    return {
        "chq": p.chan("chq"), "chg": p.chan("chg"), "chw": p.chan("chw"),
        "chws": p.chan("chws"), "chy": p.chan("chy"),
        "cht": [p.chan("cht0"), p.chan("cht1")],
        "chh": [p.chan("chh0"), p.chan("chh1")],
        "chmm": p.chan("chmm"), "chst": p.chan("chst"),
    }



def emit_l1attn(nc, p, chs, xs, wqkv, proj, wout, bout, ident_in, xs3, n_b=4, S=2048):
    D = 1024
    H, HD, FEAT = 16, 64, 256
    KCH = D // 128
    TT = S // 128

    xnfm = p.sbuf("xnfm", [128, KCH * S], BF16)
    kfm = p.sbuf("kfm", [128, KCH * S], BF16)
    vtm = p.sbuf("vtm", [128, TT * D], BF16)
    qfm = p.sbuf("qfm", [128, KCH * S], BF16)
    wsb = p.sbuf("wsb", [128, KCH * D], BF16)
    projsb = p.sbuf("projsb", [128, H * FEAT], BF16)
    boutr = p.sbuf("boutr", [1, D], BF16)
    onesr = p.sbuf("onesr", [1, 128], BF16)
    onescol = p.sbuf("onescol", [128, 1], BF16)
    ident = p.sbuf("identS", [128, 128], BF16)
    kproj = p.sbuf("kproj", [128, TT * FEAT], BF16)
    qproj = p.sbuf("qproj", [128, 2 * S], BF16)
    kvsb = p.sbuf("kvsb", [128, 2 * (HD + 1)], BF16)
    xtile = p.sbuf("xtile", [128, D], F32)
    xtm = p.sbuf("xtm", [128, D], BF16)
    scr = p.sbuf("scr", [128, D], BF16)
    scr1 = p.sbuf("scr1", [128, 512], F32)
    scr2 = p.sbuf("scr2", [128, 512], F32)
    rtile = p.sbuf("rtile", [128, D], F32)
    sc = p.sbuf("sc", [128, 16], F32)
    rc = p.sbuf("rc", [128, 1], F32)
    pf = [p.psum(f"pf{i}", [128, 512], F32) for i in range(4)]
    ptr = [p.psum(f"ptr{i}", [128, 128], BF16) for i in range(2)]

    chq = chs["chq"]
    chg = chs["chg"]
    chmm = chs["chmm"]
    chst = chs["chst"]

    p.g.dma_start(out=ident[:, :], in_=ident_in[:, :]).inc(chg, 16)
    p.g.dma_start(out=boutr[0:1, :], in_=bout[:, :]).inc(chg, 16)
    for h in range(H):
        p.g.dma_start(out=projsb[0:64, h * FEAT:(h + 1) * FEAT],
                      in_=proj[h]).inc(chg, 16)
        p.g.dma_start(out=projsb[64:128, h * FEAT:(h + 1) * FEAT],
                      in_=proj[h]).inc(chg, 16)
    p.v.memset(onesr[0:1, :], 1.0)
    p.v.memset(onescol[:, 0:1], 1.0)
    p.t.wait(chg)
    p.t.wait(p.cv)

    pf_slot = [0] * 4
    ptr_slot = [0] * 2

    def tr_tile(src_bf_ap, dst_ap, i):
        slot = i % 2
        bank = ptr[slot]
        p.t.wait(p.cs, ptr_slot[slot])
        p.t.transpose(bank[:, :], src_bf_ap, ident[:, :]).inc(chmm)
        p.s.wait(chmm)
        p.s.activation(out=dst_ap, in_=bank[:, :], func=AF.Identity)
        ptr_slot[slot] = p.cs.count

    for b in range(n_b):
        for tt in range(TT):
            p.q.wait(p.cs)
            p.q.wait(p.cv)
            p.q.wait(chmm)
            p.q.wait(chst)
            p.q.dma_start(out=xtile[:, :], in_=xs[b, tt * 128:(tt + 1) * 128, :]).inc(chq, 16)
            p.s.wait(chq)
            p.v.wait(chq)
            nmu, rcol, nmr = emit_ln_stats(p, xtile[:, :], D, sc, 0, scr)
            p.v.wait(chmm)
            p.v.tensor_scalar(out=xtm[:, :], in0=xtile[:, :], scalar1=rcol,
                              scalar2=nmr, op0=ALU.mult, op1=ALU.add, _sd=1)
            lnv = p.cv.count
            p.t.wait(p.cv, lnv)
            for k in range(KCH):
                tr_tile(xtm[:, k * 128:(k + 1) * 128],
                        xnfm[:, k * S + tt * 128: k * S + (tt + 1) * 128], k)
        xn_done = p.cs.count

        for (dst, col0, fm) in ((kfm, D, True), (vtm, 2 * D, False), (qfm, 0, True)):
            p.g.wait(chmm)
            for k in range(KCH):
                p.g.dma_start(out=wsb[:, k * D:(k + 1) * D],
                              in_=wqkv[k * 128:(k + 1) * 128, col0:col0 + D]).inc(chg, 16)
            wl = chg.count
            if fm:
                for ec in range(KCH):
                    for t5 in range(S // 512):
                        slot = (ec * (S // 512) + t5) % 4
                        bank = pf[slot]
                        p.t.wait(p.cv, pf_slot[slot])
                        p.t.wait(chg, wl)
                        p.t.wait(p.cs, xn_done)
                        for k in range(KCH):
                            hm = p.t.matmul(bank[:, :],
                                            wsb[:, k * D + ec * 128: k * D + (ec + 1) * 128],
                                            xnfm[:, k * S + t5 * 512: k * S + (t5 + 1) * 512],
                                            start=(k == 0), stop=(k == KCH - 1))
                        hm.inc(chmm)
                        p.v.wait(chmm)
                        p.v.tensor_copy(out=dst[:, ec * S + t5 * 512: ec * S + (t5 + 1) * 512],
                                        in_=bank[:, :])
                        pf_slot[slot] = p.cv.count
            else:
                for tt in range(TT):
                    for nn in range(2):
                        slot = (tt * 2 + nn) % 4
                        bank = pf[slot]
                        p.t.wait(p.cv, pf_slot[slot])
                        p.t.wait(chg, wl)
                        p.t.wait(p.cs, xn_done)
                        for k in range(KCH):
                            hm = p.t.matmul(bank[:, :],
                                            xnfm[:, k * S + tt * 128: k * S + (tt + 1) * 128],
                                            wsb[:, k * D + nn * 512: k * D + (nn + 1) * 512],
                                            start=(k == 0), stop=(k == KCH - 1))
                        hm.inc(chmm)
                        p.v.wait(chmm)
                        p.v.tensor_copy(out=dst[:, tt * D + nn * 512: tt * D + (nn + 1) * 512],
                                        in_=bank[:, :])
                        pf_slot[slot] = p.cv.count
        qkv_done = p.cv.count

        for h in range(H):
            base = (h % 2) * 64
            ch = h // 2
            for tt in range(TT):
                slot = tt % 4
                bank = pf[slot]
                p.t.wait(p.cv, pf_slot[slot])
                p.t.wait(p.cv, qkv_done)
                hm = p.t.matmul(bank[:, 0:FEAT],
                                kfm[base:base + 64, ch * S + tt * 128: ch * S + (tt + 1) * 128],
                                projsb[base:base + 64, h * FEAT:(h + 1) * FEAT],
                                start=True, stop=True)
                hm.inc(chmm)
                p.v.wait(chmm)
                p.v.tensor_scalar(out=scr1[:, 0:FEAT], in0=bank[:, 0:FEAT],
                                  scalar1=0.0, scalar2=None, op0=ALU.min)
                p.s.wait(p.cv)
                p.s.activation(out=scr2[:, 0:FEAT], in_=scr1[:, 0:FEAT], func=AF.Exp)
                p.v.wait(p.cs)
                p.v.scalar_tensor_tensor(out=kproj[:, tt * FEAT:(tt + 1) * FEAT],
                                         in0=bank[:, 0:FEAT], scalar=0.0,
                                         in1=scr2[:, 0:FEAT], op0=ALU.max, op1=ALU.add)
                pf_slot[slot] = p.cv.count
            kp_done = p.cv.count
            for fc in range(2):
                slot_kv = fc % 2
                bank = pf[slot_kv]
                bank2 = pf[2 + slot_kv]
                p.t.wait(p.cv, pf_slot[slot_kv])
                p.t.wait(p.cv, pf_slot[2 + slot_kv])
                p.t.wait(p.cv, kp_done)
                for tt in range(TT):
                    hm1 = p.t.matmul(bank[:, 0:HD],
                                     kproj[:, tt * FEAT + fc * 128: tt * FEAT + (fc + 1) * 128],
                                     vtm[:, tt * D + h * HD: tt * D + (h + 1) * HD],
                                     start=(tt == 0), stop=(tt == TT - 1))
                hm1.inc(chmm)
                for tt in range(TT):
                    hm2 = p.t.matmul(bank2[:, 0:1],
                                     kproj[:, tt * FEAT + fc * 128: tt * FEAT + (fc + 1) * 128],
                                     onescol[:, 0:1],
                                     start=(tt == 0), stop=(tt == TT - 1))
                hm2.inc(chmm)
                p.v.wait(chmm)
                p.v.tensor_copy(out=kvsb[:, fc * (HD + 1): fc * (HD + 1) + HD],
                                in_=bank[:, 0:HD])
                p.v.tensor_copy(out=kvsb[:, fc * (HD + 1) + HD: (fc + 1) * (HD + 1)],
                                in_=bank2[:, 0:1])
                pf_slot[slot_kv] = p.cv.count
                pf_slot[2 + slot_kv] = p.cv.count
            kv_done = p.cv.count

            for t5 in range(S // 512):
                for fc in range(2):
                    slot = (t5 * 2 + fc) % 4
                    bank = pf[slot]
                    p.t.wait(p.cv, pf_slot[slot])
                    p.t.wait(p.cv, qkv_done)
                    hm = p.t.matmul(bank[:, 0:512],
                                    projsb[base:base + 64, h * FEAT + fc * 128: h * FEAT + (fc + 1) * 128],
                                    qfm[base:base + 64, ch * S + t5 * 512: ch * S + (t5 + 1) * 512],
                                    start=True, stop=True)
                    hm.inc(chmm)
                    p.v.wait(chmm)
                    p.v.tensor_scalar(out=scr1[:, 0:512], in0=bank[:, 0:512],
                                      scalar1=0.0, scalar2=None, op0=ALU.min)
                    p.s.wait(p.cv)
                    p.s.activation(out=scr2[:, 0:512], in_=scr1[:, 0:512], func=AF.Exp)
                    p.v.wait(p.cs)
                    p.v.scalar_tensor_tensor(
                        out=qproj[:, fc * S + t5 * 512:fc * S + (t5 + 1) * 512],
                        in0=bank[:, 0:512], scalar=0.0,
                        in1=scr2[:, 0:512], op0=ALU.max, op1=ALU.add)
                    pf_slot[slot] = p.cv.count
            qp_done = p.cv.count
            for tt in range(TT):
                slot = tt % 4
                bank = pf[slot]
                p.t.wait(p.cv, pf_slot[slot])
                p.t.wait(p.cv, qp_done)
                p.t.wait(p.cv, kv_done)
                for fc in range(2):
                    hm = p.t.matmul(bank[:, 0:HD + 1],
                                    qproj[:, fc * S + tt * 128: fc * S + (tt + 1) * 128],
                                    kvsb[:, fc * (HD + 1):(fc + 1) * (HD + 1)],
                                    start=(fc == 0), stop=(fc == 1))
                hm.inc(chmm)
                p.v.wait(chmm)
                p.v.tensor_scalar(out=rc[:, 0:1], in0=bank[:, HD:HD + 1],
                                  scalar1=1e-8, scalar2=None, op0=ALU.add, _sd=1)
                p.v.reciprocal(out=rc[:, 0:1], in_=rc[:, 0:1], _sd=1)
                p.v.tensor_scalar(out=xnfm[:, tt * D + h * HD: tt * D + (h + 1) * HD],
                                  in0=bank[:, 0:HD], scalar1=rc[:, 0:1],
                                  scalar2=None, op0=ALU.mult, _sd=1)
                pf_slot[slot] = p.cv.count
        o_done = p.cv.count

        p.g.wait(chmm)
        for k in range(KCH):
            p.g.dma_start(out=wsb[:, k * D:(k + 1) * D],
                          in_=wout[k * 128:(k + 1) * 128, :]).inc(chg, 16)
        wl = chg.count
        for tt in range(TT):
            p.t.wait(p.cv, o_done)
            for k in range(KCH):
                tr_tile(xnfm[:, tt * D + k * 128: tt * D + (k + 1) * 128],
                        kfm[:, k * 128:(k + 1) * 128], k)
            ofm_done = p.cs.count
            for nn in range(2):
                slot = nn % 2
                bank = pf[slot]
                p.t.wait(p.cv, pf_slot[slot])
                p.t.wait(p.cs, ofm_done)
                p.t.wait(chg, wl)
                p.t.matmul(bank[:, 0:512], onesr[0:1, 0:128],
                           boutr[0:1, nn * 512:(nn + 1) * 512], start=True, stop=False)
                for k in range(KCH):
                    hm = p.t.matmul(bank[:, 0:512],
                                    kfm[:, k * 128:(k + 1) * 128],
                                    wsb[:, k * D + nn * 512: k * D + (nn + 1) * 512],
                                    start=False, stop=(k == KCH - 1))
                hm.inc(chmm)
                p.v.wait(chmm)
                p.v.tensor_copy(out=rtile[:, nn * 512:(nn + 1) * 512], in_=bank[:, 0:512])
                pf_slot[slot] = p.cv.count
            p.q.wait(p.cs)
            p.q.wait(p.cv)
            p.q.wait(chst)
            p.q.dma_start(out=xtile[:, :], in_=xs[b, tt * 128:(tt + 1) * 128, :]).inc(chq, 16)
            p.s.wait(p.cv)
            nmu, rcol, nmr = emit_ln_stats(p, rtile[:, :], D, sc, 8, scr)
            p.v.wait(chq)
            p.v.wait(chst)
            p.v.tensor_scalar(out=scr1[:, 0:512], in0=xtile[:, 0:512], scalar1=nmr,
                              scalar2=None, op0=ALU.add, _sd=1)
            p.v.tensor_scalar(out=scr2[:, 0:512], in0=xtile[:, 512:1024], scalar1=nmr,
                              scalar2=None, op0=ALU.add, _sd=1)
            p.v.scalar_tensor_tensor(out=xtile[:, 0:512], in0=rtile[:, 0:512],
                                     scalar=rcol, in1=scr1[:, 0:512],
                                     op0=ALU.mult, op1=ALU.add)
            p.v.scalar_tensor_tensor(out=xtile[:, 512:1024], in0=rtile[:, 512:1024],
                                     scalar=rcol, in1=scr2[:, 0:512],
                                     op0=ALU.mult, op1=ALU.add)
            p.g.wait(p.cv)
            p.g.dma_start(out=xs3[b, tt * 128:(tt + 1) * 128, :],
                          in_=xtile[:, :]).inc(chst, 16)


def build_l1attn(n_b=4, S=2048):
    D = 1024
    H, HD, FEAT = 16, 64, 256
    nc = bass.Bass("TRN2", target_bir_lowering=False)
    xs = nc.dram_tensor("xs", [n_b, S, D], F32, kind="ExternalInput").ap()
    wqkv = nc.dram_tensor("wqkv", [D, 3 * D], F32, kind="ExternalInput").ap()
    proj = nc.dram_tensor("proj", [H, HD, FEAT], F32, kind="ExternalInput").ap()
    wout = nc.dram_tensor("wout", [D, D], F32, kind="ExternalInput").ap()
    bout = nc.dram_tensor("bout", [1, D], F32, kind="ExternalInput").ap()
    ident_in = nc.dram_tensor("ident", [128, 128], BF16, kind="ExternalInput").ap()
    xs3 = nc.dram_tensor("xs3", [n_b, S, D], F32, kind="ExternalOutput").ap()

    es = contextlib.ExitStack()
    with es:
        p = Prog(nc, es)
        chs = _mk_chans(p)
        emit_l1attn(nc, p, chs, xs, wqkv, proj, wout, bout, ident_in, xs3, n_b, S)
        p.g.wait(chs["chst"])
        p.finalize()
    return nc


def _host_l1attn_inputs(x2_full, P):
    wqkv = np.ascontiguousarray(P["L1_attn_w_qkv"])
    proj = np.ascontiguousarray(P["L1_attn_proj_matrix"])
    wout = np.ascontiguousarray(P["L1_attn_w_out"])
    bout = np.ascontiguousarray(P["L1_attn_b_out"][None, :]).astype(np.float32)
    ident = _ident128()
    in_maps = []
    for c in range(N_CORES):
        in_maps.append({
            "xs": np.ascontiguousarray(x2_full[c * 4:(c + 1) * 4]),
            "wqkv": wqkv, "proj": proj, "wout": wout, "bout": bout, "ident": ident,
        })
    return in_maps


def run_l1attn(x2_full, P, n_b=4):
    nc = build_l1attn(n_b=n_b)
    in_maps = _host_l1attn_inputs(x2_full, P)
    res = run_bass_kernel_spmd(nc, in_maps, core_ids=list(range(N_CORES)))
    x3 = np.empty((N_CORES * n_b, 2048, 1024), np.float32)
    for c in range(N_CORES):
        x3[c * n_b:(c + 1) * n_b] = res.results[c]["xs3"]
    return x3




def emit_l1ffn_a(nc, p, chs, xs, g1, wdown, wqkvc, wup, cosr, sinr, cmask, ident_in, xp, n_b=4, S=2048):
    D = 1024
    L = 512
    H, HD = 16, 32
    KCH = D // 128
    LCH = L // 128
    TT = S // 128
    NP = H // 2
    RS = 0.17677669529663687

    qaug = p.sbuf("qaug", [128, TT * H * 33], BF16)
    kaug = p.sbuf("kaug", [128, TT * H * 33], BF16)
    qfm = [p.sbuf(f"qfm{i}", [128, S], BF16) for i in range(NP)]
    kfm = [p.sbuf(f"kfm{i}", [128, S], BF16) for i in range(NP)]
    vsb = p.sbuf("vsb", [128, TT * H * 33], BF16)
    otm = p.sbuf("otm", [128, TT * L], BF16)
    cossb = p.sbuf("cossb", [128, 256], BF16)
    sinsb = p.sbuf("sinsb", [128, 256], BF16)
    wdsb = p.sbuf("wdsb", [128, KCH * L], BF16)
    wqsb = p.sbuf("wqsb", [128, LCH * 3 * L], BF16)
    wupsb = p.sbuf("wupsb", [128, LCH * D], BF16)
    cmsk = p.sbuf("cmsk", [128, 4 * 512], F32)
    ident = p.sbuf("identS", [128, 128], BF16)
    g1col = p.sbuf("g1col", [128, KCH], F32)
    qn = p.sbuf("qn", [128, TT * H], F32)
    knm = p.sbuf("knm", [128, H], F32)
    knmb = p.sbuf("knmb", [128, H], BF16)
    kmaxb = p.sbuf("kmaxb", [128, H], F32)
    kmrow = p.sbuf("kmrow", [1, 64], BF16)
    kred = p.sbuf("kred", [64, 128], F32)
    kcol = p.sbuf("kcol", [64, 1], F32)
    kcolb = p.sbuf("kcolb", [64, 1], BF16)
    onesr = p.sbuf("onesr", [1, 128], BF16)
    xtile = p.sbuf("xtile", [128, D], F32)
    x2til = p.sbuf("x2til", [128, D], F32)
    t1tm = p.sbuf("t1tm", [128, D], BF16)
    ctm = p.sbuf("ctm", [128, L], BF16)
    cfmt = p.sbuf("cfmt", [128, L], BF16)
    xfmt = p.sbuf("xfmt", [128, D], BF16)
    qtm = p.sbuf("qtm", [128, L], F32)
    ktm = p.sbuf("ktm", [128, L], F32)
    scr = p.sbuf("scr", [128, D], BF16)
    scrf = p.sbuf("scrf", [128, 512], F32)
    ptb = [p.sbuf(f"ptb{i}", [128, 512], BF16) for i in range(2)]
    sc = p.sbuf("sc", [128, 16], F32)
    rc = p.sbuf("rc", [128, 4], F32)
    pf = [p.psum(f"pf{i}", [128, 512], F32) for i in range(2)]
    po = [p.psum(f"po{i}", [128, 512], F32) for i in range(4)]
    ptr = [p.psum(f"ptr{i}", [128, 128], BF16) for i in range(2)]

    chq = chs["chq"]
    chg = chs["chg"]
    chmm = chs["chmm"]
    chst = chs["chst"]
    chws = chs["chws"]

    p.g.dma_start(out=ident[:, :], in_=ident_in[:, :]).inc(chg, 16)
    p.g.dma_start(out=cmsk[:, :], in_=cmask[:, :]).inc(chg, 16)

    p.g.dma_start(out=g1col[:, 0:KCH],
                  in_=g1.rearrange("one (k p) -> (one p) k", p=128),
                  allow_slow_non_contiguous=True).inc(chg, 16)
    for k in range(LCH):
        p.g.dma_start(out=wqsb[:, k * 3 * L:(k + 1) * 3 * L],
                      in_=wqkvc[k * 128:(k + 1) * 128, :]).inc(chg, 16)
        p.g.dma_start(out=wupsb[:, k * D:(k + 1) * D],
                      in_=wup[k * 128:(k + 1) * 128, :]).inc(chg, 16)
    for k in range(KCH):
        p.q.wait(p.cs)
        p.q.dma_start(out=scrf[:, 0:512],
                      in_=wdown[k * 128:(k + 1) * 128, :]).inc(chq, 16)
        p.s.wait(chq)
        p.s.wait(chg)
        p.s.activation(out=wdsb[:, k * L:(k + 1) * L], in_=scrf[:, 0:512],
                       func=AF.Identity, scale=g1col[:, k:k + 1])
    p.v.memset(onesr[0:1, :], 1.0)
    p.v.memset(vsb[:, :].rearrange("p (g w) -> p g w", w=33)[:, :, 32:33], 1.0)
    p.v.memset(kaug[:, :].rearrange("p (g w) -> p g w", w=33)[:, :, 32:33], 1.0)
    p.t.wait(chg)
    p.t.wait(chq)
    p.t.wait(p.cs)
    p.t.wait(p.cv)

    pfs = [(0, 0)] * 2
    pos = [(0, 0)] * 4
    ptrs = [(0, 0)] * 2
    ptbs = [0] * 2

    def wait_slot(eng, mark):
        eng.wait(p.cv, mark[0])
        eng.wait(p.cs, mark[1])

    for b in range(n_b):
        p.v.wait(p.cs)
        p.v.memset(knm[:, :], 0.0)
        for tt in range(TT):
            p.q.wait(p.cs)
            p.q.wait(p.cv)
            p.q.wait(chg)
            p.q.dma_start(out=xtile[:, :], in_=xs[b, tt * 128:(tt + 1) * 128, :]).inc(chq, 16)
            p.q.dma_start(out=cossb[:, :], in_=cosr[tt * 128:(tt + 1) * 128, :]).inc(chq, 16)
            p.q.dma_start(out=sinsb[:, :], in_=sinr[tt * 128:(tt + 1) * 128, :]).inc(chq, 16)
            p.s.wait(chq)
            p.v.wait(chq)
            nmu, rcol, nmr = emit_ln_stats(p, xtile[:, :], D, sc, 0, scr)
            p.v.wait(chst)
            p.v.wait(chg)
            p.v.tensor_scalar(out=x2til[:, :], in0=xtile[:, :], scalar1=rcol,
                              scalar2=nmr, op0=ALU.mult, op1=ALU.add, _sd=1)
            p.s.wait(p.cv)
            p.s.activation(out=scr[:, 0:D], in_=x2til[:, :], func=AF.Square,
                           accum_out=sc[:, 8:9])
            p.v.wait(p.cs)
            p.v.tensor_scalar(out=sc[:, 9:10], in0=sc[:, 8:9], scalar1=1.0 / D,
                              scalar2=EPS, op0=ALU.mult, op1=ALU.add)
            p.s.wait(p.cv)
            p.s.activation(out=sc[:, 10:11], in_=sc[:, 9:10], func=AF.Sqrt)
            p.v.wait(p.cs)
            p.v.reciprocal(out=sc[:, 10:11], in_=sc[:, 10:11])
            p.v.wait(chmm)
            p.v.tensor_scalar(out=t1tm[:, :], in0=x2til[:, :],
                              scalar1=sc[:, 10:11], scalar2=None, op0=ALU.mult, _sd=1)
            t1_done = p.cv.count
            p.g.wait(p.cv, t1_done)
            p.g.dma_start(out=xp[b, tt * 128:(tt + 1) * 128, :],
                          in_=x2til[:, :]).inc(chg, 16)
            for k in range(KCH):
                slot = k % 2
                bank = ptr[slot]
                wait_slot(p.t, ptrs[slot])
                p.t.wait(p.cv, t1_done)
                p.t.transpose(bank[:, :], t1tm[:, k * 128:(k + 1) * 128],
                              ident[:, :]).inc(chmm)
                p.v.wait(chmm)
                p.v.tensor_copy(out=xfmt[:, k * 128:(k + 1) * 128], in_=bank[:, :])
                ptrs[slot] = (p.cv.count, 0)
            xfm_done = p.cv.count
            slot = 0
            bank = pf[slot]
            wait_slot(p.t, pfs[slot])
            p.t.wait(p.cv, xfm_done)
            for k in range(KCH):
                hm = p.t.matmul(bank[:, :], xfmt[:, k * 128:(k + 1) * 128],
                                wdsb[:, k * L:(k + 1) * L],
                                start=(k == 0), stop=(k == KCH - 1))
            hm.inc(chmm)
            p.v.wait(chmm)
            p.v.tensor_copy(out=ctm[:, :], in_=bank[:, :])
            pfs[slot] = (p.cv.count, 0)
            for k in range(LCH):
                slot = k % 2
                bank = ptr[slot]
                wait_slot(p.t, ptrs[slot])
                p.t.wait(p.cv)
                p.t.transpose(bank[:, :], ctm[:, k * 128:(k + 1) * 128],
                              ident[:, :]).inc(chmm)
                p.v.wait(chmm)
                p.v.tensor_copy(out=cfmt[:, k * 128:(k + 1) * 128], in_=bank[:, :])
                ptrs[slot] = (p.cv.count, 0)
            cfm_done = p.cv.count
            for part in range(3):
                slot = part % 2
                bank = pf[slot]
                wait_slot(p.t, pfs[slot])
                p.t.wait(p.cv, cfm_done)
                for k in range(LCH):
                    hm = p.t.matmul(bank[:, :], cfmt[:, k * 128:(k + 1) * 128],
                                    wqsb[:, k * 3 * L + part * L: k * 3 * L + (part + 1) * L],
                                    start=(k == 0), stop=(k == LCH - 1))
                hm.inc(chmm)
                p.v.wait(chmm)
                if part == 0:
                    p.v.tensor_copy(out=qtm[:, :], in_=bank[:, :])
                elif part == 1:
                    p.v.tensor_copy(out=ktm[:, :], in_=bank[:, :])
                else:
                    p.v.tensor_copy(
                        out=vsb[:, :].rearrange("p (g w) -> p g w", w=33)
                            [:, tt * H:(tt + 1) * H, 0:32],
                        in_=bank[:, :].rearrange("p (g w) -> p g w", w=32))
                pfs[slot] = (p.cv.count, 0)
            for (tsrc, dst) in ((qtm, qaug), (ktm, kaug)):
                s3 = tsrc[:, :].rearrange("p (g w) -> p g w", w=32)
                d3 = dst[:, :].rearrange("p (g w) -> p g w", w=33)[:, tt * H:(tt + 1) * H, :]
                c3 = cossb[:, :].rearrange("p (g w) -> p g w", w=16)
                s3s = sinsb[:, :].rearrange("p (g w) -> p g w", w=16)
                p.v.tensor_tensor(out=scrf[:, 0:256].rearrange("p (g w) -> p g w", w=16),
                                  in0=s3[:, :, 0:32:2], in1=c3, op=ALU.mult, _sd=1)
                p.v.tensor_tensor(out=scrf[:, 256:512].rearrange("p (g w) -> p g w", w=16),
                                  in0=s3[:, :, 1:32:2], in1=s3s, op=ALU.mult, _sd=1)
                p.v.tensor_tensor(out=d3[:, :, 0:32:2],
                                  in0=scrf[:, 0:256].rearrange("p (g w) -> p g w", w=16),
                                  in1=scrf[:, 256:512].rearrange("p (g w) -> p g w", w=16),
                                  op=ALU.subtract, _sd=1)
                p.v.tensor_tensor(out=scrf[:, 0:256].rearrange("p (g w) -> p g w", w=16),
                                  in0=s3[:, :, 0:32:2], in1=s3s, op=ALU.mult, _sd=1)
                p.v.tensor_tensor(out=scrf[:, 256:512].rearrange("p (g w) -> p g w", w=16),
                                  in0=s3[:, :, 1:32:2], in1=c3, op=ALU.mult, _sd=1)
                p.v.tensor_tensor(out=d3[:, :, 1:33:2],
                                  in0=scrf[:, 0:256].rearrange("p (g w) -> p g w", w=16),
                                  in1=scrf[:, 256:512].rearrange("p (g w) -> p g w", w=16),
                                  op=ALU.add, _sd=1)
            q3 = qaug[:, :].rearrange("p (g w) -> p g w", w=33)[:, tt * H:(tt + 1) * H, 0:32]
            k3 = kaug[:, :].rearrange("p (g w) -> p g w", w=33)[:, tt * H:(tt + 1) * H, 0:32]
            p.s.wait(p.cv)
            p.s.activation(out=scrf[:, 0:512].rearrange("p (g w) -> p g w", w=32),
                           in_=q3, func=AF.Square)
            p.v.wait(p.cs)
            p.v.tensor_reduce(out=qn[:, tt * H:(tt + 1) * H].rearrange("p (g w) -> p g w", w=1),
                              in_=scrf[:, 0:512].rearrange("p (g w) -> p g w", w=32),
                              axis=AX.X, op=ALU.add)
            p.s.wait(p.cv)
            p.s.activation(out=qn[:, tt * H:(tt + 1) * H],
                           in_=qn[:, tt * H:(tt + 1) * H], func=AF.Sqrt)
            p.s.activation(out=scrf[:, 0:512].rearrange("p (g w) -> p g w", w=32),
                           in_=k3, func=AF.Square)
            p.v.wait(p.cs)
            p.v.tensor_reduce(out=sc[:, 0:16].rearrange("p (g w) -> p g w", w=1),
                              in_=scrf[:, 0:512].rearrange("p (g w) -> p g w", w=32),
                              axis=AX.X, op=ALU.add)
            p.v.tensor_tensor(out=knm[:, :], in0=knm[:, :], in1=sc[:, 0:16],
                              op=ALU.max, _sd=1)
        p.v.tensor_copy(out=knmb[:, :], in_=knm[:, :], _sd=1)
        kn_done = p.cv.count
        slot = 0
        bank = ptr[slot]
        wait_slot(p.t, ptrs[slot])
        p.t.wait(p.cv, kn_done)
        p.t.transpose(bank[0:16, 0:128], knmb[:, 0:16], ident[:, :]).inc(chmm)
        p.v.wait(chmm)
        p.v.tensor_copy(out=kred[0:16, :], in_=bank[0:16, :])
        p.v.tensor_reduce(out=kcol[0:16, 0:1], in_=kred[0:16, :], axis=AX.X,
                          op=ALU.max, _sd=1)
        p.v.tensor_scalar(out=kcol[0:16, 0:1], in0=kcol[0:16, 0:1], scalar1=0.0,
                          scalar2=1e18, op0=ALU.max, op1=ALU.min, _sd=1)
        p.s.wait(p.cv)
        p.s.activation(out=kcol[0:16, 0:1], in_=kcol[0:16, 0:1], func=AF.Sqrt)
        p.v.wait(p.cs)
        p.v.tensor_copy(out=kcolb[0:16, 0:1], in_=kcol[0:16, 0:1], _sd=1)
        ptrs[slot] = (p.cv.count, 0)
        kb_done = p.cv.count
        slot = 1
        bank = ptr[slot]
        wait_slot(p.t, ptrs[slot])
        p.t.wait(p.cv, kb_done)
        p.t.transpose(bank[0:1, 0:16], kcolb[0:16, 0:1], ident[0:16, 0:16]).inc(chmm)
        p.v.wait(chmm)
        p.v.tensor_copy(out=kmrow[0:1, 0:16], in_=bank[0:1, 0:16])
        ptrs[slot] = (p.cv.count, 0)
        km_done = p.cv.count
        slot = 0
        bank = pf[slot]
        wait_slot(p.t, pfs[slot])
        p.t.wait(p.cv, km_done)
        p.t.matmul(bank[:, 0:16], onesr[0:1, 0:128], kmrow[0:1, 0:16],
                   start=True, stop=True).inc(chmm)
        p.v.wait(chmm)
        p.v.tensor_copy(out=kmaxb[:, :], in_=bank[:, 0:16])
        pfs[slot] = (p.cv.count, 0)

        q33 = qaug[:, :].rearrange("p (g w) -> p g w", w=33)
        p.v.wait(p.cs)
        for tt in range(TT):
            for h in range(H):
                p.v.scalar_tensor_tensor(
                    out=q33[:, tt * H + h, 32:33],
                    in0=qn[:, tt * H + h: tt * H + h + 1], scalar=-1.0,
                    in1=kmaxb[:, h:h + 1], op0=ALU.mult, op1=ALU.mult, _sd=1)
        ccol_done = p.cv.count
        for tt in range(TT):
            for h in range(H):
                pair, base = h // 2, 64 * (h % 2)
                for (srcbuf, dstbuf) in ((qaug, qfm[pair]), (kaug, kfm[pair])):
                    slot = (tt * H + h) % 2
                    bank = ptr[slot]
                    wait_slot(p.t, ptrs[slot])
                    p.t.wait(p.cv, ccol_done)
                    p.t.transpose(bank[0:33, :],
                                  srcbuf[:, tt * 528 + h * 33: tt * 528 + (h + 1) * 33],
                                  ident[:, :]).inc(chmm)
                    p.v.wait(chmm)
                    p.v.tensor_copy(out=dstbuf[base:base + 33, tt * 128:(tt + 1) * 128],
                                    in_=bank[0:33, :])
                    ptrs[slot] = (p.cv.count, 0)
        aug_done = p.cv.count

        for h in range(H):
            pair, base = h // 2, 64 * (h % 2)
            for g in range(4):
                nki = 4 * g + 4
                for ki in range(nki):
                    slot = ki % 2
                    bank = pf[slot]
                    wait_slot(p.t, pfs[slot])
                    p.t.wait(p.cv, aug_done)
                    p.t.matmul(bank[:, :],
                               kfm[pair][base:base + 33, ki * 128:(ki + 1) * 128],
                               qfm[pair][base:base + 33, g * 512:(g + 1) * 512],
                               start=True, stop=True).inc(chmm)
                    mmv = chmm.count
                    if ki >= 4 * g:
                        phi = ki - 4 * g
                        p.v.wait(chmm, mmv)
                        p.v.scalar_tensor_tensor(out=bank[:, :], in0=bank[:, :],
                                                 scalar=0.0, in1=cmsk[:, phi * 512:(phi + 1) * 512],
                                                 op0=ALU.add, op1=ALU.add)
                        dvm = p.cv.count
                    else:
                        dvm = 0
                    pslot = ki % 2
                    p.s.wait(chmm, ptbs[pslot])
                    p.s.wait(chmm, mmv)
                    if dvm:
                        p.s.wait(p.cv, dvm)
                    p.s.activation(out=ptb[pslot][:, :], in_=bank[:, :],
                                   func=AF.Exp, scale=RS)
                    pfs[slot] = (p.cv.count, p.cs.count)
                    pt_done = p.cs.count
                    last = None
                    for j in range(4):
                        qt = 4 * g + j
                        if qt < ki:
                            continue
                        ob = po[j]
                        if ki == 0:
                            wait_slot(p.t, pos[j])
                        p.t.wait(p.cs, pt_done)
                        last = p.t.matmul(ob[:, 0:33],
                                          ptb[pslot][:, j * 128:(j + 1) * 128],
                                          vsb[:, ki * 528 + h * 33: ki * 528 + (h + 1) * 33],
                                          start=(ki == 0), stop=(ki == qt))
                    last.inc(chmm)
                    ptbs[pslot] = chmm.count
                p.v.wait(chmm)
                for j in range(4):
                    qt = 4 * g + j
                    ob = po[j]
                    p.v.reciprocal(out=rc[:, j:j + 1], in_=ob[:, 32:33])
                    p.v.tensor_scalar(out=otm[:, qt * 512 + h * 32: qt * 512 + (h + 1) * 32],
                                      in0=ob[:, 0:32], scalar1=rc[:, j:j + 1],
                                      scalar2=None, op0=ALU.mult, _sd=1)
                    pos[j] = (p.cv.count, 0)
        o_done = p.cv.count

        for tt in range(TT):
            p.q.wait(p.cs)
            p.q.wait(p.cv)
            p.q.wait(chst)
            p.q.dma_start(out=xtile[:, :], in_=xp[b, tt * 128:(tt + 1) * 128, :]).inc(chq, 16)
            for k in range(LCH):
                slot = k % 2
                bank = ptr[slot]
                wait_slot(p.t, ptrs[slot])
                p.t.wait(p.cv, o_done)
                p.t.transpose(bank[:, :], otm[:, tt * 512 + k * 128: tt * 512 + (k + 1) * 128],
                              ident[:, :]).inc(chmm)
                p.v.wait(chmm)
                p.v.tensor_copy(out=cfmt[:, k * 128:(k + 1) * 128], in_=bank[:, :])
                ptrs[slot] = (p.cv.count, 0)
            ofm_done = p.cv.count
            for nn in range(2):
                slot = nn % 2
                bank = pf[slot]
                wait_slot(p.t, pfs[slot])
                p.t.wait(p.cv, ofm_done)
                for k in range(LCH):
                    hm = p.t.matmul(bank[:, :], cfmt[:, k * 128:(k + 1) * 128],
                                    wupsb[:, k * D + nn * 512: k * D + (nn + 1) * 512],
                                    start=(k == 0), stop=(k == LCH - 1))
                hm.inc(chmm)
                p.v.wait(chmm)
                p.v.wait(chq)
                p.v.wait(chg)
                p.v.tensor_tensor(out=x2til[:, nn * 512:(nn + 1) * 512],
                                  in0=bank[:, :], in1=xtile[:, nn * 512:(nn + 1) * 512],
                                  op=ALU.add)
                pfs[slot] = (p.cv.count, 0)
            p.g.wait(p.cv)
            p.g.dma_start(out=xp[b, tt * 128:(tt + 1) * 128, :],
                          in_=x2til[:, :]).inc(chst, 16)


def build_l1ffn_a(n_b=4, S=2048):
    D, L = 1024, 512
    nc = bass.Bass("TRN2", target_bir_lowering=False)
    xs = nc.dram_tensor("xs", [n_b, S, D], F32, kind="ExternalInput").ap()
    g1 = nc.dram_tensor("g1", [1, D], F32, kind="ExternalInput").ap()
    wdown = nc.dram_tensor("wdown", [D, L], F32, kind="ExternalInput").ap()
    wqkvc = nc.dram_tensor("wqkvc", [L, 3 * L], F32, kind="ExternalInput").ap()
    wup = nc.dram_tensor("wup", [L, D], F32, kind="ExternalInput").ap()
    cosr = nc.dram_tensor("cosr", [S, 256], BF16, kind="ExternalInput").ap()
    sinr = nc.dram_tensor("sinr", [S, 256], BF16, kind="ExternalInput").ap()
    cmask = nc.dram_tensor("cmask", [128, 4 * 512], F32, kind="ExternalInput").ap()
    ident_in = nc.dram_tensor("ident", [128, 128], BF16, kind="ExternalInput").ap()
    xp = nc.dram_tensor("xp", [n_b, S, D], F32, kind="ExternalOutput").ap()

    es = contextlib.ExitStack()
    with es:
        p = Prog(nc, es)
        chs = _mk_chans(p)
        emit_l1ffn_a(nc, p, chs, xs, g1, wdown, wqkvc, wup, cosr, sinr, cmask, ident_in, xp, n_b, S)
        p.g.wait(chs["chst"])
        p.finalize()
    return nc


def _rope_tables(S=2048, HD=32, H=16):
    inv = 1.0 / (10000.0 ** (np.arange(0, HD, 2, dtype=np.float32) / HD))
    ang = np.arange(S, dtype=np.float32)[:, None] * inv[None, :]
    cos = np.cos(ang).astype(np.float32)
    sin = np.sin(ang).astype(np.float32)
    cosr = np.tile(cos, (1, H))
    sinr = np.tile(sin, (1, H))
    return np.ascontiguousarray(cosr), np.ascontiguousarray(sinr)


def _cmask_phases():
    m = np.zeros((128, 4 * 512), np.float32)
    for phi in range(4):
        k = np.arange(128)[:, None]
        j = np.arange(512)[None, :]
        m[:, phi * 512:(phi + 1) * 512] = np.where(j >= k + 128 * phi, 0.0, -1e30)
    return np.ascontiguousarray(m)


def _host_l1ffna_inputs(x3_full, P):
    wqkvc = np.ascontiguousarray(
        np.concatenate([P["L1_ffn_wq"], P["L1_ffn_wk"], P["L1_ffn_wv"]], axis=1))
    cosr, sinr = _rope_tables()
    cosr = cosr.astype(ml_dtypes.bfloat16)
    sinr = sinr.astype(ml_dtypes.bfloat16)
    in_maps = []
    for c in range(N_CORES):
        in_maps.append({
            "xs": np.ascontiguousarray(x3_full[c * 4:(c + 1) * 4]),
            "g1": np.ascontiguousarray(P["L1_ffn_g1"][None, :]),
            "wdown": np.ascontiguousarray(P["L1_ffn_w_down"]),
            "wqkvc": wqkvc,
            "wup": np.ascontiguousarray(P["L1_ffn_w_up"]),
            "cosr": cosr, "sinr": sinr, "cmask": _cmask_phases(),
            "ident": _ident128(),
        })
    return in_maps




def emit_l1ffn_b(nc, p, chs, xs, xpb, g2, rw, w1, w3, w2, ident_in, xs4, tfm, hidb, ybuf, n_b=4, S=2048):
    D = 1024
    F = 2048
    E = 8
    KCH = D // 128
    FCH = F // 128
    TT = S // 128
    NTU = n_b * S // 512

    w13 = p.sbuf("w13", [128, KCH * 2 * F], BF16)
    w2sb = p.sbuf("w2sb", [128, FCH * D], BF16)
    tfsb = [p.sbuf(f"tfsb{i}", [128, KCH * 512], BF16) for i in range(2)]
    hidw = p.sbuf("hidw", [128, FCH * 512], BF16)
    hidr = [p.sbuf(f"hidr{i}", [128, FCH * 512], BF16) for i in range(2)]
    yacc = p.sbuf("yacc", [128, D], F32)
    ybf = p.sbuf("ybf", [128, D], F32)
    gatesb = p.sbuf("gatesb", [128, n_b * TT * E], F32)
    lgsb = p.sbuf("lgsb", [128, n_b * TT * E], F32)
    rwsb = p.sbuf("rwsb", [128, KCH * E], BF16)
    g2col = p.sbuf("g2col", [128, KCH], F32)
    ident = p.sbuf("identS", [128, 128], BF16)
    xtile = p.sbuf("xtile", [128, D], F32)
    x2til = p.sbuf("x2til", [128, D], F32)
    t1tm = p.sbuf("t1tm", [128, D], BF16)
    xfmt = p.sbuf("xfmt", [128, D], BF16)
    scr = p.sbuf("scr", [128, D], BF16)
    scrf = p.sbuf("scrf", [128, F], F32)
    scrb = p.sbuf("scrb", [128, 512], BF16)
    sc = p.sbuf("sc", [128, 16], F32)
    pf = [p.psum(f"pf{i}", [128, 512], F32) for i in range(6)]
    ptr = [p.psum(f"ptr{i}", [128, 128], BF16) for i in range(2)]

    chq = chs["chq"]
    chg = chs["chg"]
    chw = chs["chw"]
    chy = chs["chy"]
    chws = chs["chws"]
    cht = chs["cht"]
    chh = chs["chh"]
    chmm = chs["chmm"]
    chst = chs["chst"]

    p.g.dma_start(out=ident[:, :], in_=ident_in[:, :]).inc(chw, 16)
    p.g.dma_start(out=g2col[:, 0:KCH],
                  in_=g2.rearrange("one (k p) -> (one p) k", p=128),
                  allow_slow_non_contiguous=True).inc(chw, 16)
    for k in range(KCH):
        p.q.wait(p.cs)
        p.q.dma_start(out=scrf[:, 0:E], in_=rw[k * 128:(k + 1) * 128, :]).inc(chws, 16)
        p.s.wait(chws)
        p.s.wait(chw)
        p.s.activation(out=rwsb[:, k * E:(k + 1) * E], in_=scrf[:, 0:E],
                       func=AF.Identity, scale=g2col[:, k:k + 1])
    p.t.wait(chw)
    p.t.wait(chws)
    p.t.wait(p.cs)

    pfs = [(0, 0)] * 6
    ptrs = [(0, 0)] * 2

    def wait_slot(eng, mark):
        eng.wait(p.cv, mark[0])
        eng.wait(p.cs, mark[1])

    for b in range(n_b):
        for tt in range(TT):
            p.q.wait(p.cs)
            p.q.wait(p.cv)
            p.q.dma_start(out=xtile[:, :], in_=xpb[b, tt * 128:(tt + 1) * 128, :]).inc(chq, 16)
            p.s.wait(chq)
            p.s.activation(out=scr[:, 0:D], in_=xtile[:, :], func=AF.Square,
                           accum_out=sc[:, 0:1])
            p.v.wait(p.cs)
            p.v.tensor_scalar(out=sc[:, 1:2], in0=sc[:, 0:1], scalar1=1.0 / D,
                              scalar2=EPS, op0=ALU.mult, op1=ALU.add)
            p.s.wait(p.cv)
            p.s.activation(out=sc[:, 2:3], in_=sc[:, 1:2], func=AF.Sqrt)
            p.v.wait(p.cs)
            p.v.reciprocal(out=sc[:, 2:3], in_=sc[:, 2:3], _sd=1)
            p.v.wait(chmm)
            p.v.tensor_scalar(out=t1tm[:, :], in0=xtile[:, :],
                              scalar1=sc[:, 2:3], scalar2=None, op0=ALU.mult, _sd=1)
            t1_done = p.cv.count
            for k in range(KCH):
                slot = k % 2
                bank = ptr[slot]
                wait_slot(p.t, ptrs[slot])
                p.t.wait(p.cv, t1_done)
                p.t.transpose(bank[:, :], t1tm[:, k * 128:(k + 1) * 128],
                              ident[:, :]).inc(chmm)
                p.v.wait(chmm)
                p.v.wait(chg)
                p.v.tensor_copy(out=xfmt[:, k * 128:(k + 1) * 128], in_=bank[:, :])
                ptrs[slot] = (p.cv.count, 0)
            xfm_done = p.cv.count
            p.g.wait(p.cv, xfm_done)
            p.g.dma_start(
                out=tfm[b].rearrange("(k p) s -> p k s", p=128)[:, :, tt * 128:(tt + 1) * 128],
                in_=xfmt[:, :].rearrange("p (k s) -> p k s", s=128)).inc(chg, 16)
            slot = 0
            bank = pf[slot]
            wait_slot(p.t, pfs[slot])
            p.t.wait(p.cv, xfm_done)
            for k in range(KCH):
                hm = p.t.matmul(bank[:, 0:E], xfmt[:, k * 128:(k + 1) * 128],
                                rwsb[:, k * E:(k + 1) * E],
                                start=(k == 0), stop=(k == KCH - 1))
            hm.inc(chmm)
            p.v.wait(chmm)
            p.v.tensor_copy(out=lgsb[:, (b * TT + tt) * E:(b * TT + tt + 1) * E],
                            in_=bank[:, 0:E])
            pfs[slot] = (p.cv.count, 0)

    for i in range(n_b * TT):
        lg = lgsb[:, i * E:(i + 1) * E]
        gt = gatesb[:, i * E:(i + 1) * E]
        p.v.tensor_reduce(out=sc[:, 0:1], in_=lg, axis=AX.X, op=ALU.max)
        p.v.tensor_scalar(out=sc[:, 1:2], in0=sc[:, 0:1], scalar1=-1.0,
                          scalar2=None, op0=ALU.mult, _sd=1)
        p.s.wait(p.cv)
        p.s.activation(out=scrf[:, 0:E], in_=lg, func=AF.Exp, bias=sc[:, 1:2])
        p.v.wait(p.cs)
        p.v.tensor_reduce(out=sc[:, 2:3], in_=scrf[:, 0:E], axis=AX.X, op=ALU.max)
        p.v.tensor_scalar(out=scrf[:, 8:16], in0=scrf[:, 0:E], scalar1=sc[:, 2:3],
                          scalar2=None, op0=ALU.is_equal, _sd=1)
        p.v.tensor_tensor(out=scrf[:, 16:24], in0=scrf[:, 0:E], in1=scrf[:, 8:16],
                          op=ALU.mult, _sd=1)
        p.v.tensor_tensor(out=scrf[:, 24:32], in0=scrf[:, 0:E], in1=scrf[:, 16:24],
                          op=ALU.subtract, _sd=1)
        p.v.tensor_reduce(out=sc[:, 3:4], in_=scrf[:, 24:32], axis=AX.X, op=ALU.max, _sd=1)
        p.v.tensor_scalar(out=scrf[:, 32:40], in0=scrf[:, 24:32], scalar1=sc[:, 3:4],
                          scalar2=None, op0=ALU.is_equal, _sd=1)
        p.v.tensor_tensor(out=scrf[:, 40:48], in0=scrf[:, 8:16], in1=scrf[:, 32:40],
                          op=ALU.add, _sd=1)
        p.v.tensor_tensor(out=sc[:, 4:5], in0=sc[:, 2:3], in1=sc[:, 3:4], op=ALU.add)
        p.v.reciprocal(out=sc[:, 4:5], in_=sc[:, 4:5], _sd=1)
        p.v.tensor_tensor(out=scrf[:, 48:56], in0=scrf[:, 0:E], in1=scrf[:, 40:48],
                          op=ALU.mult, _sd=1)
        p.v.tensor_scalar(out=gt, in0=scrf[:, 48:56], scalar1=sc[:, 4:5],
                          scalar2=None, op0=ALU.mult, _sd=1)
    gates_done = p.cv.count

    for e in range(E):
        for k in range(KCH):
            for (wt, off) in ((w1, 0), (w3, F)):
                p.q.wait(p.cs)
                p.q.wait(p.cv)
                p.q.dma_start(out=scrf[:, 0:F],
                              in_=wt[e, k * 128:(k + 1) * 128, :]).inc(chws, 16)
                p.s.wait(chws)
                p.s.wait(chmm)
                p.s.activation(out=w13[:, k * 2 * F + off: k * 2 * F + off + F],
                               in_=scrf[:, 0:F], func=AF.Identity,
                               scale=g2col[:, k:k + 1])
        w_done = p.cs.count
        for tu in range(NTU):
            tb = tfsb[tu % 2]
            p.q.wait(chmm)
            p.q.dma_start(
                out=tb[:, :].rearrange("p (k s) -> p k s", s=512),
                in_=tfm[tu // 4].rearrange("(k p) s -> p k s", p=128)
                    [:, :, (tu % 4) * 512:(tu % 4 + 1) * 512]).inc(cht[tu % 2], 16)
            tchan, tload = cht[tu % 2], cht[tu % 2].count
            for fch in range(FCH):
                sa = fch % 3
                sb_ = 3 + fch % 3
                banka, bankb = pf[sa], pf[sb_]
                wait_slot(p.t, pfs[sa])
                wait_slot(p.t, pfs[sb_])
                p.t.wait(tchan, tload)
                p.t.wait(p.cs, w_done)
                for k in range(KCH):
                    p.t.matmul(banka[:, :],
                               w13[:, k * 2 * F + fch * 128: k * 2 * F + (fch + 1) * 128],
                               tb[:, k * 512:(k + 1) * 512],
                               start=(k == 0), stop=(k == KCH - 1))
                for k in range(KCH):
                    hm = p.t.matmul(bankb[:, :],
                                    w13[:, k * 2 * F + F + fch * 128: k * 2 * F + F + (fch + 1) * 128],
                                    tb[:, k * 512:(k + 1) * 512],
                                    start=(k == 0), stop=(k == KCH - 1))
                hm.inc(chmm)
                p.s.wait(chmm)
                p.s.wait(p.cv)
                p.s.activation(out=scrb[:, 0:512], in_=banka[:, :], func=AF.Silu)
                p.v.wait(p.cs)
                p.v.wait(chg)
                p.v.scalar_tensor_tensor(out=hidw[:, fch * 512:(fch + 1) * 512],
                                         in0=bankb[:, :], scalar=0.0,
                                         in1=scrb[:, 0:512], op0=ALU.add,
                                         op1=ALU.mult)
                pfs[sa] = (0, p.cs.count)
                pfs[sb_] = (p.cv.count, 0)
            p.g.wait(p.cv)
            p.g.dma_start(out=dram3(hidb[e, tu]),
                          in_=hidw[:, :].rearrange("p (k s) -> p k s", s=512)).inc(chg, 16)

    p.q.wait(chg)
    for e in range(E):
        for k in range(FCH):
            p.g.wait(chmm)
            p.g.dma_start(out=w2sb[:, k * D:(k + 1) * D],
                          in_=w2[e, k * 128:(k + 1) * 128, :]).inc(chw, 16)
        w2_done = chw.count
        p.g.wait(chy)
        for tu in range(NTU):
            hb = hidr[tu % 2]
            p.q.wait(chmm)
            p.q.dma_start(out=hb[:, :].rearrange("p (k s) -> p k s", s=512),
                          in_=dram3(hidb[e, tu])).inc(chh[tu % 2], 16)
            hchan, hload = chh[tu % 2], chh[tu % 2].count
            for t4 in range(4):
                gi = tu * 4 + t4
                for nn in range(2):
                    slot = (t4 * 2 + nn) % 6
                    bank = pf[slot]
                    wait_slot(p.t, pfs[slot])
                    p.t.wait(hchan, hload)
                    p.t.wait(chw, w2_done)
                    for k in range(FCH):
                        hm = p.t.matmul(bank[:, :],
                                        hb[:, k * 512 + t4 * 128: k * 512 + (t4 + 1) * 128],
                                        w2sb[:, k * D + nn * 512: k * D + (nn + 1) * 512],
                                        start=(k == 0), stop=(k == FCH - 1))
                    hm.inc(chmm)
                    p.v.wait(chmm)
                    p.v.wait(chy)
                    p.v.tensor_scalar(out=ybf[:, nn * 512:(nn + 1) * 512],
                                      in0=bank[:, :],
                                      scalar1=gatesb[:, gi * E + e: gi * E + e + 1],
                                      scalar2=None, op0=ALU.mult)
                    pfs[slot] = (p.cv.count, 0)
                p.g.wait(p.cv)
                p.g.dma_start(out=ybuf[gi * 128:(gi + 1) * 128, :], in_=ybf[:, :],
                              accum_op=(ALU.bypass if e == 0 else ALU.add)).inc(chy, 16)

    p.q.wait(chy)
    for b in range(n_b):
        for tt in range(TT):
            gi = b * TT + tt
            p.q.wait(p.cs)
            p.q.wait(p.cv)
            p.q.wait(chst)
            p.q.dma_start(out=yacc[:, :], in_=ybuf[gi * 128:(gi + 1) * 128, :]).inc(chq, 16)
            p.q.dma_start(out=xtile[:, :], in_=xpb[b, tt * 128:(tt + 1) * 128, :]).inc(chq, 16)
            p.q.dma_start(out=x2til[:, :], in_=xs[b, tt * 128:(tt + 1) * 128, :]).inc(chq, 16)
            p.v.wait(chq)
            p.v.tensor_tensor(out=yacc[:, :], in0=yacc[:, :], in1=xtile[:, :],
                              op=ALU.add)
            p.s.wait(p.cv)
            nmu, rcol, nmr = emit_ln_stats(p, yacc[:, :], D, sc, 0, scr)
            p.v.tensor_scalar(out=x2til[:, :], in0=x2til[:, :], scalar1=nmr,
                              scalar2=None, op0=ALU.add, _sd=1)
            p.v.scalar_tensor_tensor(out=x2til[:, :], in0=yacc[:, :], scalar=rcol,
                                     in1=x2til[:, :], op0=ALU.mult, op1=ALU.add, _sd=1)
            p.g.wait(p.cv)
            p.g.dma_start(out=xs4[b, tt * 128:(tt + 1) * 128, :],
                          in_=x2til[:, :]).inc(chst, 16)


def build_l1ffn_b(n_b=4, S=2048):
    D, F, E = 1024, 2048, 8
    NTU = n_b * S // 512
    nc = bass.Bass("TRN2", target_bir_lowering=False)
    xs = nc.dram_tensor("xs", [n_b, S, D], F32, kind="ExternalInput").ap()
    xpb = nc.dram_tensor("xpb", [n_b, S, D], F32, kind="ExternalInput").ap()
    g2 = nc.dram_tensor("g2", [1, D], F32, kind="ExternalInput").ap()
    rw = nc.dram_tensor("rw", [D, E], F32, kind="ExternalInput").ap()
    w1 = nc.dram_tensor("w1", [E, D, F], F32, kind="ExternalInput").ap()
    w3 = nc.dram_tensor("w3", [E, D, F], F32, kind="ExternalInput").ap()
    w2 = nc.dram_tensor("w2", [E, F, D], F32, kind="ExternalInput").ap()
    ident_in = nc.dram_tensor("ident", [128, 128], BF16, kind="ExternalInput").ap()
    xs4 = nc.dram_tensor("xs4", [n_b, S, D], F32, kind="ExternalOutput").ap()

    tfm = nc.dram_tensor("tfm", [n_b, D, S], BF16, kind="Internal").ap()
    hidb = nc.dram_tensor("hidb", [E, NTU, F, 512], BF16, kind="Internal").ap()
    ybuf = nc.dram_tensor("ybuf", [n_b * S, D], F32, kind="Internal").ap()

    es = contextlib.ExitStack()
    with es:
        p = Prog(nc, es)
        chs = _mk_chans(p)
        emit_l1ffn_b(nc, p, chs, xs, xpb, g2, rw, w1, w3, w2, ident_in, xs4, tfm, hidb, ybuf, n_b, S)
        p.g.wait(chs["chst"])
        p.finalize()
    return nc


def _host_l1ffnb_inputs(x3_full, xp_full, P):
    in_maps = []
    for c in range(N_CORES):
        in_maps.append({
            "xs": np.ascontiguousarray(x3_full[c * 4:(c + 1) * 4]),
            "xpb": np.ascontiguousarray(xp_full[c * 4:(c + 1) * 4]),
            "g2": np.ascontiguousarray(P["L1_ffn_g2"][None, :]),
            "rw": np.ascontiguousarray(P["L1_ffn_router_w"]),
            "w1": np.ascontiguousarray(P["L1_ffn_w1"]),
            "w3": np.ascontiguousarray(P["L1_ffn_w3"]),
            "w2": np.ascontiguousarray(P["L1_ffn_w2"]),
            "ident": _ident128(),
        })
    return in_maps



def build_l0attn(n_b=4, S=2048):
    D = 1024
    DI = 2048
    CV = 2304
    DIP = 4384
    NH, P_ = 32, 64
    KCH = D // 128
    TT = S // 128
    ZCH = DI // 128
    XCH = CV // 128

    nc = bass.Bass("TRN2", target_bir_lowering=False)
    xs = nc.dram_tensor("xs", [n_b, S, D], F32, kind="ExternalInput").ap()
    wipT = nc.dram_tensor("wipT", [D, DIP], F32, kind="ExternalInput").ap()
    convw = nc.dram_tensor("convw", [CV, 4], F32, kind="ExternalInput").ap()
    convb = nc.dram_tensor("convb", [CV, 1], F32, kind="ExternalInput").ap()
    dtb = nc.dram_tensor("dtb", [NH, 1], F32, kind="ExternalInput").ap()
    alog = nc.dram_tensor("alog", [NH, 1], F32, kind="ExternalInput").ap()
    dskip = nc.dram_tensor("dskip", [1, NH], BF16, kind="ExternalInput").ap()
    dskrep = nc.dram_tensor("dskrep", [1, NH * P_], BF16, kind="ExternalInput").ap()
    normw = nc.dram_tensor("normw", [DI, 1], F32, kind="ExternalInput").ap()
    woutT = nc.dram_tensor("woutT", [DI, D], F32, kind="ExternalInput").ap()
    lng = nc.dram_tensor("lng", [1, D], BF16, kind="ExternalInput").ap()
    lnb = nc.dram_tensor("lnb", [1, D], BF16, kind="ExternalInput").ap()
    ebank = nc.dram_tensor("ebank", [NH, NH * 128], BF16, kind="ExternalInput").ap()
    tril_ = nc.dram_tensor("tril", [128, 128], BF16, kind="ExternalInput").ap()
    trius_ = nc.dram_tensor("trius", [128, 128], BF16, kind="ExternalInput").ap()
    logtri_ = nc.dram_tensor("logtri", [128, 128], F32, kind="ExternalInput").ap()
    ident_in = nc.dram_tensor("ident", [128, 128], BF16, kind="ExternalInput").ap()
    xs1 = nc.dram_tensor("xs1", [n_b, S, D], F32, kind="ExternalOutput").ap()

    abuf = nc.dram_tensor("abuf", [n_b, S, D], F32, kind="Internal").ap()
    zbuf = nc.dram_tensor("zbuf", [n_b, S, DI], BF16, kind="Internal").ap()
    xbcbuf = nc.dram_tensor("xbcbuf", [n_b, CV, S], BF16, kind="Internal").ap()
    cvbuf = nc.dram_tensor("cvbuf", [n_b, CV, S], BF16, kind="Internal").ap()
    ybuf = nc.dram_tensor("ybuf", [n_b, S, DI], BF16, kind="Internal").ap()

    es = contextlib.ExitStack()
    with es:
        p = Prog(nc, es)
        wsb = p.sbuf("wsb", [128, KCH * 2304], BF16)
        xnfm = p.sbuf("xnfm", [128, KCH * S], BF16)
        xtile = p.sbuf("xtile", [128, D], F32)
        xtm = p.sbuf("xtm", [128, D], BF16)
        scr = p.sbuf("scr", [128, DI], BF16)
        scrf = p.sbuf("scrf", [128, 1024], F32)
        sc = p.sbuf("sc", [128, 16], F32)
        ident = p.sbuf("identS", [128, 128], BF16)
        cwcol = p.sbuf("cwcol", [128, XCH * 4], F32)
        cbcol = p.sbuf("cbcol", [128, XCH], F32)
        cbufs = [p.sbuf(f"cbuf{i}", [128, 515], BF16) for i in range(2)]
        ebk = p.sbuf("ebk", [NH, NH * 128], BF16)
        trilS = p.sbuf("trilS", [128, 128], BF16)
        triuS = p.sbuf("triuS", [128, 128], BF16)
        logtS = p.sbuf("logtS", [128, 128], F32)
        dtbc = p.sbuf("dtbc", [NH, 1], F32)
        acol = p.sbuf("acol", [NH, 1], F32)
        dskb = p.sbuf("dskb", [128, NH], F32)
        dskx = p.sbuf("dskx", [128, NH * P_], BF16)
        dskrow = p.sbuf("dskrow", [1, NH], BF16)
        dskreprow = p.sbuf("dskreprow", [1, NH * P_], BF16)
        onesr = p.sbuf("onesr", [1, 128], BF16)
        dtmix = p.sbuf("dtmix", [32, 2 * S], F32)
        dtmixb = p.sbuf("dtmixb", [32, 2 * S], BF16)
        dtat = p.sbuf("dtat", [128, 64], F32)
        dtatb = p.sbuf("dtatb", [128, 64], BF16)
        eal = p.sbuf("eal", [128, NH], F32)
        ebe = p.sbuf("ebe", [128, NH], F32)
        dteb = p.sbuf("dteb", [128, NH], F32)
        gcol = p.sbuf("gcol", [128, NH], F32)
        afmb = p.sbuf("afmb", [NH, 128], BF16)
        bfm = p.sbuf("bfm", [128, S], BF16)
        cfmS = p.sbuf("cfmS", [128, S], BF16)
        ldxh = p.sbuf("ldxh", [128, 16 * 128], BF16)
        xhtm = p.sbuf("xhtm", [128, 16 * 128], BF16)
        u2bf = p.sbuf("u2bf", [128, 16 * 128], BF16)
        btm = p.sbuf("btm", [128, 128], BF16)
        dmt = p.sbuf("dmt", [128, 4 * 128], F32)
        ssba = p.sbuf("ssba", [128, 16 * 128], BF16)
        expd = p.sbuf("expd", [128, 4 * 128], F32)
        gsbb = p.sbuf("gsbb", [128, 128], BF16)
        ldtc = p.sbuf("ldtc", [128, NH], F32)
        lcol = p.sbuf("lcol", [128, NH], F32)
        hstate = p.sbuf("hstate", [128, NH * P_], F32)
        hstb = p.sbuf("hstb", [128, NH * P_], BF16)
        t1sb = p.sbuf("t1sb", [128, 2 * P_], F32)
        t2sb = p.sbuf("t2sb", [128, 2 * P_], F32)
        ytm = p.sbuf("ytm", [128, DI], BF16)
        nwcol = p.sbuf("nwcol", [128, ZCH], F32)
        lngb = p.sbuf("lngb", [128, 2 * D], BF16)
        ytile = p.sbuf("ytile", [128, DI], BF16)
        sztile = p.sbuf("sztile", [128, DI], BF16)
        ygb = p.sbuf("ygb", [128, DI], BF16)
        ygfm = p.sbuf("ygfm", [128, DI], BF16)
        rtile = p.sbuf("rtile", [128, D], F32)
        atile = p.sbuf("atile", [128, D], F32)
        lngrow = p.sbuf("lngrow", [1, D], BF16)
        lnbrow = p.sbuf("lnbrow", [1, D], BF16)

        pfp = [p.psum(f"pf{i}", [128, 512], F32) for i in range(4)]
        ptr = [p.psum(f"ptr{i}", [128, 128], BF16) for i in range(2)]
        pss = [p.psum(f"ps{i}", [128, 128], F32) for i in range(2)]

        chq = p.chan("chq")
        chg = p.chan("chg")
        chw = p.chan("chw")
        chc = [p.chan("chc0"), p.chan("chc1")]
        chx = p.chan("chx")
        chbc = p.chan("chbc")
        chmm = p.chan("chmm")
        chst = p.chan("chst")

        pfs = [(0, 0)] * 4
        ptrs = [(0, 0)] * 2
        psss = [(0, 0)] * 2

        def wait_slot(eng, mark):
            eng.wait(p.cv, mark[0])
            eng.wait(p.cs, mark[1])

        p.g.dma_start(out=ident[:, :], in_=ident_in[:, :]).inc(chw, 16)
        p.g.dma_start(out=ebk[:, :], in_=ebank[:, :]).inc(chw, 16)
        p.g.dma_start(out=trilS[:, :], in_=tril_[:, :]).inc(chw, 16)
        p.g.dma_start(out=triuS[:, :], in_=trius_[:, :]).inc(chw, 16)
        p.g.dma_start(out=logtS[:, :], in_=logtri_[:, :]).inc(chw, 16)
        p.g.dma_start(out=cwcol[:, :], in_=dram3(convw)).inc(chw, 16)
        p.g.dma_start(out=cbcol[:, :].rearrange("p (g w) -> p g w", w=1),
                      in_=dram3(convb), allow_slow_non_contiguous=True).inc(chw, 16)
        p.g.dma_start(out=dtbc[0:NH, 0:1], in_=dtb[:, :]).inc(chw, 16)
        p.g.dma_start(out=acol[0:NH, 0:1], in_=alog[:, :]).inc(chw, 16)
        p.g.dma_start(out=dskrow[0:1, :], in_=dskip[:, :]).inc(chw, 16)
        p.g.dma_start(out=dskreprow[0:1, :], in_=dskrep[:, :]).inc(chw, 16)
        p.g.dma_start(out=nwcol[:, :].rearrange("p (g w) -> p g w", w=1),
                      in_=dram3(normw), allow_slow_non_contiguous=True).inc(chw, 16)
        p.g.dma_start(out=lngrow[0:1, :], in_=lng[:, :]).inc(chw, 16)
        p.g.dma_start(out=lnbrow[0:1, :], in_=lnb[:, :]).inc(chw, 16)
        p.v.memset(onesr[0:1, :], 1.0)
        p.v.wait(chw)
        p.v.memset(hstate[:, :], 0.0)
        p.s.wait(chw)
        p.s.activation(out=acol[0:NH, 0:1], in_=acol[0:NH, 0:1], func=AF.Exp)
        p.v.wait(p.cs)
        p.v.tensor_scalar(out=acol[0:NH, 0:1], in0=acol[0:NH, 0:1], scalar1=-1.0,
                          scalar2=None, op0=ALU.mult)
        cst_done = p.cv.count
        p.t.wait(chw)
        p.t.wait(p.cv, cst_done)
        p.t.matmul(pfp[0][:, 0:NH], onesr[0:1, 0:128], dskrow[0:1, 0:NH],
                   start=True, stop=True)
        for q4 in range(4):
            p.t.wait(p.cv)
            hm4 = p.t.matmul(pfp[3][:, :], onesr[0:1, 0:128],
                             dskreprow[0:1, q4 * 512:(q4 + 1) * 512],
                             start=True, stop=True)
            hm4.inc(chmm)
            p.v.wait(chmm)
            p.v.tensor_copy(out=dskx[:, q4 * 512:(q4 + 1) * 512], in_=pfp[3][:, :])
        p.t.matmul(pfp[1][:, :], onesr[0:1, 0:128], lngrow[0:1, 0:512],
                   start=True, stop=True)
        hm = p.t.matmul(pfp[2][:, :], onesr[0:1, 0:128], lnbrow[0:1, 0:512],
                        start=True, stop=True)
        hm.inc(chmm)
        p.v.wait(chmm)
        p.v.tensor_copy(out=dskb[:, :], in_=pfp[0][:, 0:NH])
        p.v.tensor_copy(out=lngb[:, 0:512], in_=pfp[1][:, :])
        p.v.tensor_copy(out=lngb[:, D:D + 512], in_=pfp[2][:, :])
        bcast_done = p.cv.count
        pfs[0] = (bcast_done, 0)
        pfs[1] = (bcast_done, 0)
        pfs[2] = (bcast_done, 0)
        p.t.wait(p.cv, bcast_done)
        p.t.matmul(pfp[1][:, :], onesr[0:1, 0:128], lngrow[0:1, 512:1024],
                   start=True, stop=True)
        hm = p.t.matmul(pfp[2][:, :], onesr[0:1, 0:128], lnbrow[0:1, 512:1024],
                        start=True, stop=True)
        hm.inc(chmm)
        p.v.wait(chmm)
        p.v.tensor_copy(out=lngb[:, 512:1024], in_=pfp[1][:, :])
        p.v.tensor_copy(out=lngb[:, D + 512:2 * D], in_=pfp[2][:, :])
        bcast_done = p.cv.count
        pfs[1] = (bcast_done, 0)
        pfs[2] = (bcast_done, 0)

        for b in range(n_b):
            for tt in range(TT):
                p.q.wait(p.cs)
                p.q.wait(p.cv)
                p.q.wait(chg)
                p.q.dma_start(out=xtile[:, :], in_=xs[b, tt * 128:(tt + 1) * 128, :]).inc(chq, 16)
                p.s.wait(chq)
                p.v.wait(chq)
                nmu, rcol, nmr = emit_ln_stats(p, xtile[:, :], D, sc, 0, scr)
                p.v.wait(chg)
                p.v.wait(chmm)
                p.v.tensor_scalar(out=atile[:, 0:D], in0=xtile[:, :], scalar1=rcol,
                                  scalar2=nmr, op0=ALU.mult, op1=ALU.add, _sd=1)
                p.v.tensor_copy(out=xtm[:, :], in_=atile[:, 0:D], _sd=1)
                ln_done = p.cv.count
                p.g.wait(p.cv, ln_done)
                p.g.dma_start(out=abuf[b, tt * 128:(tt + 1) * 128, :],
                              in_=atile[:, 0:D]).inc(chg, 16)
                for k in range(KCH):
                    slot = k % 2
                    bank = ptr[slot]
                    wait_slot(p.t, ptrs[slot])
                    p.t.wait(p.cv, ln_done)
                    p.t.transpose(bank[:, :], xtm[:, k * 128:(k + 1) * 128],
                                  ident[:, :]).inc(chmm)
                    p.v.wait(chmm)
                    p.v.tensor_copy(out=xnfm[:, k * S + tt * 128: k * S + (tt + 1) * 128],
                                    in_=bank[:, :])
                    ptrs[slot] = (p.cv.count, 0)
            xn_done = p.cv.count

            p.g.wait(chmm)
            for k in range(KCH):
                p.g.dma_start(out=wsb[:, k * DI:(k + 1) * DI],
                              in_=wipT[k * 128:(k + 1) * 128, 0:DI]).inc(chg, 16)
            wl = chg.count
            for tt in range(TT):
                for nn in range(4):
                    slot = nn % 4
                    bank = pfp[slot]
                    wait_slot(p.t, pfs[slot])
                    p.t.wait(chg, wl)
                    p.t.wait(p.cv, xn_done)
                    for k in range(KCH):
                        hm = p.t.matmul(bank[:, :],
                                        xnfm[:, k * S + tt * 128: k * S + (tt + 1) * 128],
                                        wsb[:, k * DI + nn * 512: k * DI + (nn + 1) * 512],
                                        start=(k == 0), stop=(k == KCH - 1))
                    hm.inc(chmm)
                    p.s.wait(chmm)
                    p.s.wait(chg)
                    p.s.activation(out=sztile[:, nn * 512:(nn + 1) * 512],
                                   in_=bank[:, :], func=AF.Silu)
                    pfs[slot] = (0, p.cs.count)
                p.g.wait(p.cs)
                p.g.dma_start(out=zbuf[b, tt * 128:(tt + 1) * 128, :],
                              in_=sztile[:, :]).inc(chg, 16)
            p.g.wait(chmm)
            for k in range(KCH):
                p.g.dma_start(out=wsb[:, k * 2304:(k + 1) * 2304],
                              in_=wipT[k * 128:(k + 1) * 128, DI:DI + 2304]).inc(chg, 16)
            wl = chg.count
            for ec in range(XCH):
                for t5 in range(4):
                    slot = t5 % 4
                    bank = pfp[slot]
                    wait_slot(p.t, pfs[slot])
                    p.t.wait(chg, wl)
                    p.t.wait(p.cv, xn_done)
                    for k in range(KCH):
                        hm = p.t.matmul(bank[:, :],
                                        wsb[:, k * 2304 + ec * 128: k * 2304 + (ec + 1) * 128],
                                        xnfm[:, k * S + t5 * 512: k * S + (t5 + 1) * 512],
                                        start=(k == 0), stop=(k == KCH - 1))
                    hm.inc(chmm)
                    p.v.wait(chmm)
                    p.v.wait(chg)
                    p.v.tensor_copy(out=scr[:, t5 * 512:(t5 + 1) * 512], in_=bank[:, :])
                    pfs[slot] = (p.cv.count, 0)
                p.g.wait(p.cv)
                p.g.dma_start(out=xbcbuf[b, ec * 128:(ec + 1) * 128, :],
                              in_=scr[:, 0:S]).inc(chg, 16)
            p.g.wait(chmm)
            for k in range(KCH):
                p.g.dma_start(out=wsb[:, k * 32:(k + 1) * 32],
                              in_=wipT[k * 128:(k + 1) * 128, DI + 2304:DIP]).inc(chg, 16)
            wl = chg.count
            for t5 in range(4):
                slot = t5 % 2
                bank = pfp[slot]
                wait_slot(p.t, pfs[slot])
                p.t.wait(chg, wl)
                p.t.wait(p.cv, xn_done)
                for k in range(KCH):
                    hm = p.t.matmul(bank[0:32, 0:512],
                                    wsb[:, k * 32:(k + 1) * 32],
                                    xnfm[:, k * S + t5 * 512: k * S + (t5 + 1) * 512],
                                    start=(k == 0), stop=(k == KCH - 1))
                hm.inc(chmm)
                p.v.wait(chmm)
                p.v.tensor_copy(out=dtmix[0:32, t5 * 512:(t5 + 1) * 512],
                                in_=bank[0:32, 0:512])
                pfs[slot] = (p.cv.count, 0)

            for ec in range(XCH):
                for t5 in range(4):
                    cb = cbufs[t5 % 2]
                    ch_ = chc[t5 % 2]
                    p.q.wait(p.cs)
                    p.q.wait(p.cv)
                    if t5 == 0:
                        p.v.wait(ch_)
                        p.v.memset(cb[:, 0:3], 0.0)
                        pad_done = p.cv.count
                        p.q.wait(p.cv, pad_done)
                        p.q.dma_start(out=cb[:, 3:515],
                                      in_=xbcbuf[b, ec * 128:(ec + 1) * 128, 0:512]).inc(ch_, 16)
                    else:
                        p.q.dma_start(out=cb[:, 0:515],
                                      in_=xbcbuf[b, ec * 128:(ec + 1) * 128,
                                                 t5 * 512 - 3: t5 * 512 + 512]).inc(ch_, 16)
                    p.v.wait(ch_)
                    cw = cwcol[:, ec * 4: ec * 4 + 4]
                    p.v.tensor_scalar(out=scrf[:, 0:512], in0=cb[:, 0:512],
                                      scalar1=cw[:, 0:1], scalar2=None, op0=ALU.mult)
                    p.v.scalar_tensor_tensor(out=scrf[:, 0:512], in0=cb[:, 1:513],
                                             scalar=cw[:, 1:2], in1=scrf[:, 0:512],
                                             op0=ALU.mult, op1=ALU.add, _sd=1)
                    p.v.scalar_tensor_tensor(out=scrf[:, 0:512], in0=cb[:, 2:514],
                                             scalar=cw[:, 2:3], in1=scrf[:, 0:512],
                                             op0=ALU.mult, op1=ALU.add, _sd=1)
                    p.v.scalar_tensor_tensor(out=scrf[:, 0:512], in0=cb[:, 3:515],
                                             scalar=cw[:, 3:4], in1=scrf[:, 0:512],
                                             op0=ALU.mult, op1=ALU.add, _sd=1)
                    p.s.wait(p.cv)
                    p.s.wait(chg)
                    p.s.activation(out=scr[:, t5 * 512:(t5 + 1) * 512], in_=scrf[:, 0:512],
                                   func=AF.Silu, bias=cbcol[:, ec:ec + 1], scale=1.0)
                p.g.wait(p.cs)
                p.g.dma_start(out=cvbuf[b, ec * 128:(ec + 1) * 128, :],
                              in_=scr[:, 0:S]).inc(chg, 16)

            p.v.wait(p.cs)
            p.v.tensor_scalar(out=dtmix[0:32, 0:S], in0=dtmix[0:32, 0:S],
                              scalar1=dtbc[0:32, 0:1], scalar2=None, op0=ALU.add, _sd=1)
            p.s.wait(p.cv)
            p.s.activation(out=dtmix[0:32, S:2 * S], in_=dtmix[0:32, 0:S],
                           func=AF.Abs)
            p.s.activation(out=dtmix[0:32, S:2 * S], in_=dtmix[0:32, S:2 * S],
                           func=AF.Exp, scale=-1.0, _sd=1)
            p.v.wait(p.cs)
            p.v.tensor_scalar(out=dtmix[0:32, S:2 * S], in0=dtmix[0:32, S:2 * S],
                              scalar1=1.0, scalar2=None, op0=ALU.add)
            p.s.wait(p.cv)
            p.s.activation(out=dtmix[0:32, S:2 * S], in_=dtmix[0:32, S:2 * S],
                           func=AF.Ln)
            p.v.wait(p.cs)
            p.v.scalar_tensor_tensor(out=dtmix[0:32, 0:S], in0=dtmix[0:32, 0:S],
                                     scalar=0.0, in1=dtmix[0:32, S:2 * S], op0=ALU.max,
                                     op1=ALU.add, _sd=1)
            p.v.tensor_scalar(out=dtmix[0:32, S:2 * S], in0=dtmix[0:32, 0:S],
                              scalar1=acol[0:32, 0:1], scalar2=None, op0=ALU.mult, _sd=1)
            p.v.tensor_copy(out=dtmixb[:, :], in_=dtmix[:, :], _sd=1)
            dt_done = p.cv.count
            p.q.wait(chmm)
            p.q.wait(chg)
            p.q.dma_start(out=bfm[:, :], in_=cvbuf[b, 2048:2176, :]).inc(chbc, 16)
            p.q.dma_start(out=cfmS[:, :], in_=cvbuf[b, 2176:2304, :]).inc(chbc, 16)
            bc_load = chbc.count
            hb_mark = p.cg.count
            sst_done = p.cg.count
            dmt_done = p.cs.count
            ssb_rd_all = chmm.count

            for c in range(16):
                csl = slice(c * 128, (c + 1) * 128)
                wait_slot(p.t, ptrs[0])
                wait_slot(p.t, ptrs[1])
                p.t.wait(p.cv, dt_done)
                p.t.transpose(ptr[0][:, 0:32], dtmixb[0:32, csl],
                              ident[0:32, 0:32]).inc(chmm)
                p.t.transpose(ptr[1][:, 0:32],
                              dtmixb[0:32, S + c * 128: S + (c + 1) * 128],
                              ident[0:32, 0:32]).inc(chmm)
                p.v.wait(chmm)
                p.v.tensor_copy(out=dtat[:, 0:32], in_=ptr[0][:, 0:32])
                p.v.tensor_copy(out=dtat[:, 32:64], in_=ptr[1][:, 0:32])
                p.v.tensor_copy(out=dtatb[:, :], in_=dtat[:, :], _sd=1)
                ptrs[0] = (p.cv.count, 0)
                ptrs[1] = (p.cv.count, 0)
                dtt_done = p.cv.count
                slot = 0
                bank = pss[slot]
                wait_slot(p.t, psss[slot])
                p.t.wait(p.cv, dtt_done)
                p.t.matmul(bank[:, 0:32], trilS[:, :], dtatb[:, 32:64],
                           start=True, stop=False)
                hm = p.t.matmul(bank[:, 32:64], triuS[:, :], dtatb[:, 32:64],
                                start=False, stop=True)
                hm.inc(chmm)
                aslot = 0
                abank = pfp[aslot]
                wait_slot(p.t, pfs[aslot])
                hm = p.t.matmul(abank[0:32, 0:128], dtatb[:, 32:64], trilS[:, :],
                                start=True, stop=True)
                hm.inc(chmm)
                p.v.wait(chmm)
                p.v.tensor_copy(out=dtat[:, 32:64], in_=bank[:, 0:32])
                p.s.wait(chmm)
                p.s.activation(out=eal[:, :], in_=bank[:, 0:32], func=AF.Exp)
                p.s.activation(out=ebe[:, :], in_=bank[:, 32:64], func=AF.Exp)
                p.v.wait(p.cs)
                p.v.tensor_copy(out=afmb[0:32, :], in_=abank[0:32, 0:128])
                pfs[aslot] = (p.cv.count, 0)
                p.v.tensor_tensor(out=dteb[:, :], in0=dtat[:, 0:32], in1=ebe[:, :],
                                  op=ALU.mult)
                p.v.tensor_tensor(out=gcol[:, :], in0=ebe[:, :], in1=eal[:, :],
                                  op=ALU.mult)
                p.s.wait(p.cv)
                p.s.activation(out=ldtc[:, :], in_=dtat[:, 0:32], func=AF.Ln)
                p.v.wait(p.cs)
                p.v.tensor_tensor(out=lcol[:, :], in0=ldtc[:, :], in1=dtat[:, 32:64],
                                  op=ALU.subtract)
                psss[slot] = (p.cv.count, p.cs.count)
                sh_done = p.cv.count
                p.q.wait(chmm)
                p.q.dma_start(out=ldxh[:, :].rearrange("p (g w) -> p g w", w=128),
                              in_=dram3(cvbuf[b, 0:2048, :])[:, :, csl]).inc(chx, 16)
                xh_load = chx.count
                for pr in range(16):
                    slot = pr % 2
                    bank = ptr[slot]
                    wait_slot(p.t, ptrs[slot])
                    p.t.wait(chx, xh_load)
                    p.t.transpose(bank[:, :], ldxh[:, pr * 128:(pr + 1) * 128],
                                  ident[:, :]).inc(chmm)
                    p.v.wait(chmm)
                    p.v.tensor_copy(out=xhtm[:, pr * 128:(pr + 1) * 128], in_=bank[:, :])
                    ptrs[slot] = (p.cv.count, 0)
                    for hh in range(2):
                        h = pr * 2 + hh
                        dsl = slice(pr * 128 + hh * 64, pr * 128 + (hh + 1) * 64)
                        p.s.wait(p.cv)
                        p.s.activation(out=u2bf[:, dsl], in_=xhtm[:, dsl],
                                       func=AF.Identity, scale=dteb[:, h:h + 1])
                u_done = p.cv.count
                u_done_s = p.cs.count
                slot = 0
                bank = ptr[slot]
                wait_slot(p.t, ptrs[slot])
                p.t.wait(chbc, bc_load)
                p.t.transpose(bank[:, :], bfm[:, csl], ident[:, :]).inc(chmm)
                p.v.wait(chmm)
                p.v.tensor_copy(out=btm[:, :], in_=bank[:, :])
                ptrs[slot] = (p.cv.count, 0)
                btm_done = p.cv.count
                slot = 1
                gbank = pss[slot]
                wait_slot(p.t, psss[slot])
                p.t.wait(chbc, bc_load)
                p.t.matmul(gbank[:, :], bfm[:, csl], cfmS[:, csl],
                           start=True, stop=True).inc(chmm)
                gt_mm = chmm.count
                p.v.wait(chmm, gt_mm)
                p.v.tensor_copy(out=gsbb[:, :], in_=gbank[:, :])
                psss[slot] = (p.cv.count, 0)
                gsb_mark = p.cv.count
                sstm = {}
                dmtp = [dmt_done] * 4
                sstp4 = [sst_done] * 4
                t1_rd = [0, 0]
                t2_rd = [0, 0]
                for hbase in (0, 16):
                  for h in range(hbase, hbase + 16):
                    slot = 2 + (h % 2)
                    par4 = h % 4
                    psl4 = slice(par4 * 128, (par4 + 1) * 128)
                    dbank = pfp[slot]
                    wait_slot(p.t, pfs[slot])
                    p.t.wait(p.cv, sh_done)
                    p.t.matmul(dbank[:, 0:128], ebk[:, h * 128:(h + 1) * 128],
                               afmb[0:32, :], start=True, stop=True).inc(chmm)
                    dmm = chmm.count
                    p.v.wait(chmm, dmm)
                    p.v.wait(p.cs, dmtp[par4])
                    p.v.scalar_tensor_tensor(out=dmt[:, psl4], in0=dbank[:, 0:128],
                                             scalar=lcol[:, h:h + 1],
                                             in1=logtS[:, :], op0=ALU.add,
                                             op1=ALU.add, _sd=0)
                    dm_done = p.cv.count
                    pfs[slot] = (dm_done, 0)
                    p.s.wait(p.cv, dm_done)
                    p.s.wait(p.cg, sstp4[par4])
                    p.s.activation(out=expd[:, psl4], in_=dmt[:, psl4], func=AF.Exp)
                    dmtp[par4] = p.cs.count
                    p.g.wait(p.cs, p.cs.count)
                    if h == hbase:
                        p.g.wait(chmm, ssb_rd_all)
                        p.g.wait(p.cv, gsb_mark)
                    p.g.tensor_tensor(out=ssba[:, (h - hbase) * 128:(h - hbase + 1) * 128],
                                      in0=expd[:, psl4], in1=gsbb[:, :],
                                      op=ALU.mult, _sd=0)
                    sstp4[par4] = p.cg.count
                    sstm[h] = p.cg.count
                  for h in range(hbase, hbase + 16):
                    par = h % 2
                    hp = slice((h // 2) * 128 + (h % 2) * 64, (h // 2) * 128 + (h % 2 + 1) * 64)
                    p6 = slice(par * P_, (par + 1) * P_)
                    ybank = pfp[par]
                    wait_slot(p.t, pfs[par])
                    p.t.wait(p.cg, sstm[h])
                    p.t.matmul(ybank[:, 0:P_],
                               ssba[:, (h - hbase) * 128:(h - hbase + 1) * 128],
                               xhtm[:, hp], start=True, stop=True).inc(chmm)
                    yi_mm = chmm.count
                    if c > 0:
                        ibank = pss[par]
                        wait_slot(p.t, psss[par])
                        p.t.wait(p.cg, hb_mark)
                        p.t.matmul(ibank[:, 0:P_], cfmS[:, csl],
                                   hstb[:, h * P_:(h + 1) * P_],
                                   start=True, stop=True).inc(chmm)
                        yint_mm = chmm.count
                        p.s.wait(chmm, yint_mm)
                        p.s.wait(p.cg, t1_rd[par])
                        p.s.activation(out=t1sb[:, p6], in_=ibank[:, 0:P_],
                                       func=AF.Identity, scale=eal[:, h:h + 1])
                        psss[par] = (0, p.cs.count)
                        t1_done = p.cs.count
                        p.g.wait(p.cs, t1_done)
                        p.g.wait(p.cv, max(u_done, t2_rd[par]))
                        p.g.tensor_tensor(out=t2sb[:, p6], in0=xhtm[:, hp],
                                          in1=dskx[:, h * P_:(h + 1) * P_],
                                          op=ALU.mult, _sd=0)
                        p.g.tensor_tensor(out=t2sb[:, p6], in0=t2sb[:, p6],
                                          in1=t1sb[:, p6], op=ALU.add, _sd=1)
                        t1_rd[par] = p.cg.count
                        t2_done = p.cg.count
                    else:
                        p.g.wait(p.cv, max(u_done, t2_rd[par]))
                        p.g.tensor_tensor(out=t2sb[:, p6], in0=xhtm[:, hp],
                                          in1=dskx[:, h * P_:(h + 1) * P_],
                                          op=ALU.mult, _sd=0)
                        t2_done = p.cg.count
                    sbank = pfp[2 + par]
                    wait_slot(p.t, pfs[2 + par])
                    p.t.wait(p.cv, btm_done)
                    p.t.wait(p.cs, u_done_s)
                    p.t.matmul(sbank[:, 0:P_], btm[:, :], u2bf[:, hp],
                               start=True, stop=True).inc(chmm)
                    hst_mm = chmm.count
                    p.v.wait(chmm, yi_mm)
                    p.v.wait(p.cg, t2_done)
                    if h == 0:
                        p.v.wait(chg)
                    p.v.tensor_tensor(out=ytm[:, h * P_:(h + 1) * P_],
                                      in0=ybank[:, 0:P_], in1=t2sb[:, p6],
                                      op=ALU.add, _sd=0)
                    t2_rd[par] = p.cv.count
                    pfs[par] = (p.cv.count, 0)
                    p.v.wait(chmm, hst_mm)
                    if c > 0:
                        p.v.scalar_tensor_tensor(out=hstate[:, h * P_:(h + 1) * P_],
                                                 in0=hstate[:, h * P_:(h + 1) * P_],
                                                 scalar=gcol[:, h:h + 1],
                                                 in1=sbank[:, 0:P_], op0=ALU.mult,
                                                 op1=ALU.add, _sd=0)
                    else:
                        p.v.tensor_copy(out=hstate[:, h * P_:(h + 1) * P_],
                                        in_=sbank[:, 0:P_], _sd=0)
                    pfs[2 + par] = (p.cv.count, 0)
                    hs_done = p.cv.count
                    p.g.wait(p.cv, hs_done)
                    p.g.tensor_copy(out=hstb[:, h * P_:(h + 1) * P_],
                                    in_=hstate[:, h * P_:(h + 1) * P_], _sd=0)
                  ssb_rd_all = chmm.count
                hb_mark = p.cg.count
                sst_done = p.cg.count
                dmt_done = p.cs.count
                p.v.wait(p.cg)
                p.g.wait(p.cv)
                p.g.dma_start(out=ybuf[b, csl, :], in_=ytm[:, :]).inc(chg, 16)
            p.v.wait(chmm)
            p.v.memset(hstate[:, :], 0.0)

        p.g.wait(chmm)
        for k in range(ZCH):
            p.g.dma_start(out=wsb[:, k * D:(k + 1) * D],
                          in_=woutT[k * 128:(k + 1) * 128, :]).inc(chg, 16)
        wl = chg.count
        p.v.wait(chg, wl)
        for k in range(ZCH):
            p.v.tensor_scalar(out=wsb[:, k * D:(k + 1) * D], in0=wsb[:, k * D:(k + 1) * D],
                              scalar1=nwcol[:, k:k + 1], scalar2=None, op0=ALU.mult)
        wout_done = p.cv.count
        for b in range(n_b):
            for tt in range(TT):
                tsl = slice(tt * 128, (tt + 1) * 128)
                p.q.wait(p.cs)
                p.q.wait(p.cv)
                p.q.wait(chst)
                p.q.dma_start(out=ytile[:, :], in_=ybuf[b, tsl, :]).inc(chq, 16)
                p.q.dma_start(out=sztile[:, :], in_=zbuf[b, tsl, :]).inc(chq, 16)
                p.q.dma_start(out=atile[:, :], in_=abuf[b, tsl, :]).inc(chq, 16)
                p.q.dma_start(out=xtile[:, :], in_=xs[b, tsl, :]).inc(chq, 16)
                p.v.wait(chq)
                p.v.wait(chmm)
                p.v.tensor_tensor(out=ytile[:, :], in0=ytile[:, :], in1=sztile[:, :],
                                  op=ALU.mult)
                p.s.wait(p.cv)
                p.s.activation(out=scr[:, 0:DI], in_=ytile[:, :], func=AF.Square,
                               accum_out=sc[:, 8:9])
                p.v.wait(p.cs)
                p.v.tensor_scalar(out=sc[:, 9:10], in0=sc[:, 8:9], scalar1=1.0 / DI,
                                  scalar2=EPS, op0=ALU.mult, op1=ALU.add)
                p.s.wait(p.cv)
                p.s.activation(out=sc[:, 10:11], in_=sc[:, 9:10], func=AF.Sqrt)
                p.v.wait(p.cs)
                p.v.reciprocal(out=sc[:, 10:11], in_=sc[:, 10:11])
                p.v.tensor_scalar(out=ygb[:, :], in0=ytile[:, :], scalar1=sc[:, 10:11],
                                  scalar2=None, op0=ALU.mult, _sd=1)
                yg_done = p.cv.count
                for k in range(ZCH):
                    slot = k % 2
                    bank = ptr[slot]
                    wait_slot(p.t, ptrs[slot])
                    p.t.wait(p.cv, yg_done)
                    p.t.wait(p.cv, wout_done)
                    p.t.transpose(bank[:, :], ygb[:, k * 128:(k + 1) * 128],
                                  ident[:, :]).inc(chmm)
                    p.v.wait(chmm)
                    p.v.tensor_copy(out=ygfm[:, k * 128:(k + 1) * 128], in_=bank[:, :])
                    ptrs[slot] = (p.cv.count, 0)
                ygfm_done = p.cv.count
                for nn in range(2):
                    slot = nn % 2
                    bank = pfp[slot]
                    wait_slot(p.t, pfs[slot])
                    p.t.wait(p.cv, ygfm_done)
                    for k in range(ZCH):
                        hm = p.t.matmul(bank[:, :], ygfm[:, k * 128:(k + 1) * 128],
                                        wsb[:, k * D + nn * 512: k * D + (nn + 1) * 512],
                                        start=(k == 0), stop=(k == ZCH - 1))
                    hm.inc(chmm)
                    p.v.wait(chmm)
                    p.v.tensor_copy(out=rtile[:, nn * 512:(nn + 1) * 512], in_=bank[:, :])
                    pfs[slot] = (p.cv.count, 0)
                p.s.wait(p.cv)
                nmu, rcol, nmr = emit_ln_stats(p, rtile[:, :], D, sc, 0, scr)
                p.v.tensor_scalar(out=rtile[:, :], in0=rtile[:, :], scalar1=rcol,
                                  scalar2=nmr, op0=ALU.mult, op1=ALU.add, _sd=1)
                p.v.tensor_tensor(out=rtile[:, :], in0=rtile[:, :], in1=lngb[:, 0:D],
                                  op=ALU.mult, _sd=1)
                p.v.tensor_tensor(out=rtile[:, :], in0=rtile[:, :], in1=lngb[:, D:2 * D],
                                  op=ALU.add, _sd=1)
                p.v.tensor_tensor(out=rtile[:, :], in0=rtile[:, :], in1=atile[:, :],
                                  op=ALU.add, _sd=1)
                p.s.wait(p.cv)
                nmu, rcol, nmr = emit_ln_stats(p, rtile[:, :], D, sc, 8, scr)
                p.v.tensor_scalar(out=xtile[:, :], in0=xtile[:, :], scalar1=nmr,
                                  scalar2=None, op0=ALU.add, _sd=1)
                p.v.scalar_tensor_tensor(out=xtile[:, :], in0=rtile[:, :], scalar=rcol,
                                         in1=xtile[:, :], op0=ALU.mult, op1=ALU.add, _sd=1)
                p.g.wait(p.cv)
                p.g.dma_start(out=xs1[b, tsl, :], in_=xtile[:, :]).inc(chst, 16)

        p.g.wait(chst)
        p.finalize()
    return nc


def _host_l0attn_inputs(x0_full, P):
    NH = 32
    wipT = np.ascontiguousarray(P["L0_attn_in_proj_w"].T)
    woutT = np.ascontiguousarray(P["L0_attn_out_w"].T)
    ebank = np.zeros((NH, NH * 128), ml_dtypes.bfloat16)
    for h in range(NH):
        ebank[h, h * 128:(h + 1) * 128] = 1.0
    s_ = np.arange(128)[:, None]
    t_ = np.arange(128)[None, :]
    tril = (s_ <= t_).astype(ml_dtypes.bfloat16)
    trius = (s_ > t_).astype(ml_dtypes.bfloat16)
    logtri = np.where(s_ <= t_, 0.0, -1e30).astype(np.float32)
    in_maps = []
    for c in range(N_CORES):
        in_maps.append({
            "xs": np.ascontiguousarray(x0_full[c * 4:(c + 1) * 4]),
            "wipT": wipT,
            "convw": np.ascontiguousarray(P["L0_attn_conv_w"]),
            "convb": np.ascontiguousarray(P["L0_attn_conv_b"][:, None]),
            "dtb": np.ascontiguousarray(P["L0_attn_dt_bias"][:, None]),
            "alog": np.ascontiguousarray(P["L0_attn_A_log"][:, None]),
            "dskip": np.ascontiguousarray(P["L0_attn_D_skip"][None, :]).astype(ml_dtypes.bfloat16),
            "dskrep": np.ascontiguousarray(np.repeat(P["L0_attn_D_skip"], 64)[None, :]).astype(ml_dtypes.bfloat16),
            "normw": np.ascontiguousarray(P["L0_attn_norm_w"][:, None]),
            "woutT": woutT,
            "lng": np.ascontiguousarray(P["L0_attn_ln_g"][None, :]).astype(ml_dtypes.bfloat16),
            "lnb": np.ascontiguousarray(P["L0_attn_ln_b"][None, :]).astype(ml_dtypes.bfloat16),
            "ebank": ebank, "tril": tril, "trius": trius, "logtri": logtri,
            "ident": _ident128(),
        })
    return in_maps




def build_l1all(n_b=4, S=2048):
    D = 1024
    E, F = 8, 2048
    NTU = n_b * S // 512

    nc = bass.Bass("TRN2", target_bir_lowering=False)
    xs2 = nc.dram_tensor("xs2", [n_b, S, D], F32, kind="ExternalInput").ap()
    wqkv = nc.dram_tensor("wqkv", [D, 3 * D], F32, kind="ExternalInput").ap()
    proj = nc.dram_tensor("proj", [16, 64, 256], F32, kind="ExternalInput").ap()
    wout = nc.dram_tensor("wout", [D, D], F32, kind="ExternalInput").ap()
    bout = nc.dram_tensor("bout", [1, D], F32, kind="ExternalInput").ap()
    g1 = nc.dram_tensor("g1", [1, D], F32, kind="ExternalInput").ap()
    wdown = nc.dram_tensor("wdown", [D, 512], F32, kind="ExternalInput").ap()
    wqkvc = nc.dram_tensor("wqkvc", [512, 3 * 512], F32, kind="ExternalInput").ap()
    wup = nc.dram_tensor("wup", [512, D], F32, kind="ExternalInput").ap()
    cosr = nc.dram_tensor("cosr", [S, 256], BF16, kind="ExternalInput").ap()
    sinr = nc.dram_tensor("sinr", [S, 256], BF16, kind="ExternalInput").ap()
    cmask = nc.dram_tensor("cmask", [128, 4 * 512], F32, kind="ExternalInput").ap()
    g2 = nc.dram_tensor("g2", [1, D], F32, kind="ExternalInput").ap()
    rw = nc.dram_tensor("rw", [D, E], F32, kind="ExternalInput").ap()
    w1 = nc.dram_tensor("w1", [E, D, F], F32, kind="ExternalInput").ap()
    w3 = nc.dram_tensor("w3", [E, D, F], F32, kind="ExternalInput").ap()
    w2 = nc.dram_tensor("w2", [E, F, D], F32, kind="ExternalInput").ap()
    ident_in = nc.dram_tensor("ident", [128, 128], BF16, kind="ExternalInput").ap()
    xs4 = nc.dram_tensor("xs4", [n_b, S, D], F32, kind="ExternalOutput").ap()

    x3buf = nc.dram_tensor("x3buf", [n_b, S, D], F32, kind="Internal").ap()
    xpbuf = nc.dram_tensor("xpbuf", [n_b, S, D], F32, kind="Internal").ap()
    tfm = nc.dram_tensor("tfm", [n_b, D, S], BF16, kind="Internal").ap()
    hidb = nc.dram_tensor("hidb", [E, NTU, F, 512], BF16, kind="Internal").ap()
    ybuf = nc.dram_tensor("ybuf", [n_b * S, D], F32, kind="Internal").ap()

    es = contextlib.ExitStack()
    with es:
        p = Prog(nc, es)
        chs = _mk_chans(p)

        sub = contextlib.ExitStack()
        p.alloc_es, p.prefix = sub, "a_"
        emit_l1attn(nc, p, chs, xs2, wqkv, proj, wout, bout, ident_in, x3buf, n_b, S)
        p.barrier("tqvsg")
        sub.close()

        sub = contextlib.ExitStack()
        p.alloc_es, p.prefix = sub, "b_"
        emit_l1ffn_a(nc, p, chs, x3buf, g1, wdown, wqkvc, wup, cosr, sinr, cmask,
                     ident_in, xpbuf, n_b, S)
        p.barrier("tqvsg")
        sub.close()

        sub = contextlib.ExitStack()
        p.alloc_es, p.prefix = sub, "c_"
        emit_l1ffn_b(nc, p, chs, x3buf, xpbuf, g2, rw, w1, w3, w2, ident_in, xs4,
                     tfm, hidb, ybuf, n_b, S)
        p.g.wait(chs["chst"])
        p.finalize()
        sub.close()
    return nc


def _host_l1all_inputs(x2_full, P):
    wqkvc = np.ascontiguousarray(
        np.concatenate([P["L1_ffn_wq"], P["L1_ffn_wk"], P["L1_ffn_wv"]], axis=1))
    cosr, sinr = _rope_tables()
    cosr = cosr.astype(ml_dtypes.bfloat16)
    sinr = sinr.astype(ml_dtypes.bfloat16)
    in_maps = []
    for c in range(N_CORES):
        in_maps.append({
            "xs2": np.ascontiguousarray(x2_full[c * 4:(c + 1) * 4]),
            "wqkv": np.ascontiguousarray(P["L1_attn_w_qkv"]),
            "proj": np.ascontiguousarray(P["L1_attn_proj_matrix"]),
            "wout": np.ascontiguousarray(P["L1_attn_w_out"]),
            "bout": np.ascontiguousarray(P["L1_attn_b_out"][None, :]).astype(np.float32),
            "g1": np.ascontiguousarray(P["L1_ffn_g1"][None, :]),
            "wdown": np.ascontiguousarray(P["L1_ffn_w_down"]),
            "wqkvc": wqkvc,
            "wup": np.ascontiguousarray(P["L1_ffn_w_up"]),
            "cosr": cosr, "sinr": sinr, "cmask": _cmask_phases(),
            "g2": np.ascontiguousarray(P["L1_ffn_g2"][None, :]),
            "rw": np.ascontiguousarray(P["L1_ffn_router_w"]),
            "w1": np.ascontiguousarray(P["L1_ffn_w1"]),
            "w3": np.ascontiguousarray(P["L1_ffn_w3"]),
            "w2": np.ascontiguousarray(P["L1_ffn_w2"]),
            "ident": _ident128(),
        })
    return in_maps



def _ident128():
    return np.eye(128, dtype=ml_dtypes.bfloat16)


def _host_l0ffn_inputs(x1_full, P, n_steps=32):
    wtxT = np.ascontiguousarray(
        np.concatenate([P["L0_ffn_wf"], P["L0_ffn_wi"], P["L0_ffn_wc"]], axis=0).T)
    utT = np.ascontiguousarray(
        np.concatenate([P["L0_ffn_uf"], P["L0_ffn_ui"], P["L0_ffn_uc"]], axis=0).T)
    b3 = np.ascontiguousarray(
        np.concatenate([P["L0_ffn_bf"], P["L0_ffn_bi"], P["L0_ffn_bc"]])[None, :]).astype(np.float32)
    wot = np.ascontiguousarray(P["L0_ffn_wo"].T)
    uot = np.ascontiguousarray(P["L0_ffn_uo"].T)
    bo = np.ascontiguousarray(P["L0_ffn_bo"][None, :]).astype(np.float32)
    in_c = np.ascontiguousarray(P["L0_ffn_in_c"][None, :]).astype(np.float32)
    ident = _ident128()
    in_maps = []
    for c in range(N_CORES):
        xsc = np.ascontiguousarray(x1_full[:n_steps, c * 256:(c + 1) * 256, :])
        in_maps.append({
            "xs": xsc, "wtx": wtxT, "ut": utT, "b3": b3, "wot": wot, "uot": uot,
            "bo": bo, "in_c": in_c, "ident": ident,
        })
    return in_maps


def run_l0ffn(x1_full, P, n_steps=32):
    nc = build_l0ffn(n_steps=n_steps)
    in_maps = _host_l0ffn_inputs(x1_full, P, n_steps)
    res = run_bass_kernel_spmd(nc, in_maps, core_ids=list(range(N_CORES)))
    x2 = np.empty((n_steps, 2048, 1024), np.float32)
    for c in range(N_CORES):
        x2[:, c * 256:(c + 1) * 256, :] = res.results[c]["xs2"]
    return x2


_NC_CACHE = {}


def _get_nc(name, builder):
    if name not in _NC_CACHE:
        _NC_CACHE[name] = builder()
    return _NC_CACHE[name]


def kernel(**inputs):
    P = {k: np.ascontiguousarray(np.asarray(v, dtype=np.float32)) for k, v in inputs.items()}
    x0 = P["L0_attn_x"]

    nc1 = _get_nc("l0attn", build_l0attn)
    res = run_bass_kernel_spmd(nc1, _host_l0attn_inputs(x0, P), core_ids=list(range(N_CORES)))
    x1 = np.empty((32, 2048, 1024), np.float32)
    for c in range(N_CORES):
        x1[c * 4:(c + 1) * 4] = res.results[c]["xs1"]

    x2 = run_l0ffn(x1, P)

    nc3 = _get_nc("l1all", build_l1all)
    res = run_bass_kernel_spmd(nc3, _host_l1all_inputs(x2, P), core_ids=list(range(N_CORES)))
    x4 = np.empty((32, 2048, 1024), np.float32)
    for c in range(N_CORES):
        x4[c * 4:(c + 1) * 4] = res.results[c]["xs4"]
    return x4


# version 05 (final)
# speedup vs baseline: 1.2845x; 1.1992x over previous
import contextlib
import numpy as np
import ml_dtypes

import concourse.bass as bass
import concourse.mybir as mybir
from concourse.bass_utils import run_bass_kernel_spmd

F32 = mybir.dt.float32
BF16 = mybir.dt.bfloat16
AF = mybir.ActivationFunctionType
ALU = mybir.AluOpType
AX = mybir.AxisListType

N_CORES = 8
EPS = 1e-5


ENGINES = ("sync", "tensor", "vector", "scalar", "gpsimd")


class Chan:

    __slots__ = ("sem", "count", "name")

    def __init__(self, sem, name):
        self.sem = sem
        self.count = 0
        self.name = name


class _Rec:
    __slots__ = ("method", "args", "kwargs", "incs")

    def __init__(self, method, args, kwargs):
        self.method = method
        self.args = args
        self.kwargs = kwargs
        self.incs = []


class _RecHandle:
    __slots__ = ("rec", "_chans")

    def __init__(self, rec):
        self.rec = rec

    def inc(self, ch, n=1):
        self.rec.incs.append((ch.sem, n))
        ch.count += n
        return self


_NOSERIAL = {"wait_ge", "dma_start", "engine_nop", "nop", "indirect_dma_start",
             "collective_compute", "partition_id", "reg_save", "dma_start_transpose"}
_SERIAL_ENGINES = {"vector", "scalar", "gpsimd"}


class _EngRec:
    def __init__(self, prog, name):
        self._prog = prog
        self._name = name

    def __getattr__(self, method):
        prog, name = self._prog, self._name

        def call(*args, **kwargs):
            sd = kwargs.pop("_sd", 2)
            serial = name in _SERIAL_ENGINES and method not in _NOSERIAL
            if serial and sd > 0:
                sc = prog.self_chan[name]
                v = sc.count - (sd - 1)
                if v > 0:
                    w = prog.waited[name]
                    k = id(sc.sem)
                    if w.get(k, 0) < v:
                        w[k] = v
                        prog.ops[name].append(_Rec("wait_ge", (sc.sem, v), {}))
            rec = _Rec(method, args, kwargs)
            prog.ops[name].append(rec)
            h = _RecHandle(rec)
            if serial:
                h.inc(prog.self_chan[name])
            return h

        return call

    def wait(self, ch, value=None):
        v = ch.count if value is None else value
        if v <= 0:
            return
        w = self._prog.waited[self._name]
        k = id(ch.sem)
        if w.get(k, 0) >= v:
            return
        w[k] = v
        rec = _Rec("wait_ge", (ch.sem, v), {})
        self._prog.ops[self._name].append(rec)


class Prog:

    def __init__(self, nc, es):
        self.nc = nc
        self.es = es
        self.alloc_es = es
        self.prefix = ""
        self.ops = {e: [] for e in ENGINES}
        self.waited = {e: {} for e in ENGINES}
        self.chans = []
        self.self_chan = {}
        for e in sorted(_SERIAL_ENGINES):
            self.self_chan[e] = self.chan(f"self_{e}")
        self.q = _EngRec(self, "sync")
        self.t = _EngRec(self, "tensor")
        self.v = _EngRec(self, "vector")
        self.s = _EngRec(self, "scalar")
        self.g = _EngRec(self, "gpsimd")
        self.cv = self.self_chan["vector"]
        self.cs = self.self_chan["scalar"]
        self.cg = self.self_chan["gpsimd"]

    def chan(self, name):
        sem = self.es.enter_context(self.nc.semaphore(name))
        c = Chan(sem, name)
        self.chans.append(c)
        return c

    def sbuf(self, name, shape, dtype):
        return self.alloc_es.enter_context(
            self.nc.sbuf_tensor(self.prefix + name, shape, dtype))

    def psum(self, name, shape, dtype):
        return self.alloc_es.enter_context(
            self.nc.psum_tensor(self.prefix + name, shape, dtype))

    def barrier(self, engines="tqvsg"):
        m = {"q": self.q, "t": self.t, "v": self.v, "s": self.s, "g": self.g}
        for e in engines:
            for c in self.chans:
                m[e].wait(c)

    def finalize(self):
        block = self.es.enter_context(self.nc.Block())
        for eng_name in ENGINES:
            ops = self.ops[eng_name]
            if not ops:
                continue

            def mk(ops):
                def body(eng):
                    for rec in ops:
                        instr = getattr(eng, rec.method)(*rec.args, **rec.kwargs)
                        for sem, n in rec.incs:
                            instr.then_inc(sem, n)

                return body

            getattr(block, eng_name)(mk(ops))



def emit_ln_stats(p, x_ap, width, sc, col0, scr, eps=EPS):
    msum = sc[:, col0 + 0: col0 + 1]
    nmu = sc[:, col0 + 1: col0 + 2]
    vsum = sc[:, col0 + 2: col0 + 3]
    rcol = sc[:, col0 + 3: col0 + 4]
    nmr = sc[:, col0 + 4: col0 + 5]
    p.s.activation(out=scr[:, 0:width], in_=x_ap, func=AF.Identity,
                   accum_out=msum)
    p.v.wait(p.cs)
    p.v.tensor_scalar(out=nmu, in0=msum, scalar1=-1.0 / width, scalar2=None,
                      op0=ALU.mult)
    p.s.wait(p.cv)
    p.s.activation(out=scr[:, 0:width], in_=x_ap, func=AF.Square, bias=nmu,
                   scale=1.0, accum_out=vsum)
    p.v.wait(p.cs)
    p.v.tensor_scalar(out=vsum, in0=vsum, scalar1=1.0 / width, scalar2=eps,
                      op0=ALU.mult, op1=ALU.add)
    p.s.wait(p.cv)
    p.s.activation(out=rcol, in_=vsum, func=AF.Sqrt)
    p.v.wait(p.cs)
    p.v.reciprocal(out=rcol, in_=rcol)
    p.v.scalar_tensor_tensor(out=nmr, in0=nmu, scalar=1.0, in1=rcol,
                             op0=ALU.mult, op1=ALU.mult, _sd=1)
    return nmu, rcol, nmr




def dram3(x2d, p=128):
    return x2d.rearrange("(g p) w -> p g w", p=p)


def sb3(t, g, w):
    return t[:, :].rearrange("p (g w) -> p g w", g=g)


def build_l0ffn(n_steps=32, rows=256):
    D = 1024
    E3 = 3072
    KCH = D // 128
    RT = rows // 128

    nc = bass.Bass("TRN2", target_bir_lowering=False)
    xs = nc.dram_tensor("xs", [n_steps, rows, D], F32, kind="ExternalInput").ap()
    wtx = nc.dram_tensor("wtx", [D, E3], F32, kind="ExternalInput").ap()
    ut = nc.dram_tensor("ut", [D, E3], F32, kind="ExternalInput").ap()
    b3 = nc.dram_tensor("b3", [1, E3], F32, kind="ExternalInput").ap()
    wot = nc.dram_tensor("wot", [D, D], F32, kind="ExternalInput").ap()
    uot = nc.dram_tensor("uot", [D, D], F32, kind="ExternalInput").ap()
    bo = nc.dram_tensor("bo", [1, D], F32, kind="ExternalInput").ap()
    in_c = nc.dram_tensor("in_c", [1, D], F32, kind="ExternalInput").ap()
    ident_in = nc.dram_tensor("ident", [128, 128], BF16, kind="ExternalInput").ap()
    xs2 = nc.dram_tensor("xs2", [n_steps, rows, D], F32, kind="ExternalOutput").ap()

    xnfm = nc.dram_tensor("xnfm", [n_steps, D, rows], BF16, kind="Internal").ap()
    hbuf = nc.dram_tensor("hbuf", [n_steps, rows, D], F32, kind="Internal").ap()

    es = contextlib.ExitStack()
    with es:
        p = Prog(nc, es)
        wbig = p.sbuf("wbig", [128, 2 * KCH * E3], BF16)
        b3r = p.sbuf("b3r", [1, E3], BF16)
        bor = p.sbuf("bor", [1, D], BF16)
        onesr = p.sbuf("onesr", [1, rows], BF16)
        ident = p.sbuf("identS", [128, 128], BF16)
        incol = p.sbuf("incol", [128, KCH], F32)
        onesT = p.sbuf("onesT", [128, rows], F32)
        cfm = p.sbuf("cfm", [128, KCH * rows], F32)
        cbf = p.sbuf("cbf", [128, KCH * rows], BF16)
        fsb = p.sbuf("fsb", [128, KCH * rows], F32)
        isb = p.sbuf("isb", [128, KCH * rows], F32)
        gsb = p.sbuf("gsb", [128, KCH * rows], F32)
        tmp1 = p.sbuf("tmp1", [128, rows], F32)
        tmp2 = p.sbuf("tmp2", [128, rows], F32)
        xraw = [p.sbuf(f"xraw{i}", [128, RT * D], F32) for i in range(2)]
        xtm = p.sbuf("xtm", [128, RT * D], BF16)
        xfm = [p.sbuf(f"xfm{i}", [128, KCH * rows], BF16) for i in range(2)]
        scr = p.sbuf("scr", [128, D], BF16)
        sc = p.sbuf("sc", [128, 16], F32)
        cuo = p.sbuf("cuo", [128, RT * D], F32)
        htile = p.sbuf("htile", [128, RT * D], F32)
        ps = [p.psum(f"ps{i}", [128, 512], F32) for i in range(6)]
        pt = [p.psum(f"pt{i}", [128, 128], BF16) for i in range(2)]

        chq = p.chan("chq")
        chg = p.chan("chg")
        chmm = p.chan("chmm")
        chst = p.chan("chst")

        p.g.dma_start(out=ident[:, :], in_=ident_in[:, :]).inc(chg, 16)
        p.g.dma_start(out=b3r[0:1, :], in_=b3[:, :]).inc(chg, 16)
        p.g.dma_start(out=bor[0:1, :], in_=bo[:, :]).inc(chg, 16)
        p.g.dma_start(out=incol[:, 0:KCH],
                      in_=in_c.rearrange("one (k p) -> (one p) k", p=128),
                      allow_slow_non_contiguous=True).inc(chg, 16)
        for k in range(KCH):
            p.g.dma_start(out=wbig[:, k * E3:(k + 1) * E3],
                          in_=wtx[k * 128:(k + 1) * 128, :]).inc(chg, 16)
            p.g.dma_start(out=wbig[:, (KCH + k) * E3:(KCH + k + 1) * E3],
                          in_=ut[k * 128:(k + 1) * 128, :]).inc(chg, 16)
        p.v.memset(onesr[0:1, :], 1.0)
        p.v.memset(onesT[:, :], 1.0)
        p.s.wait(chg)
        p.s.wait(p.cv)
        for k in range(KCH):
            p.s.activation(out=cfm[:, k * rows:(k + 1) * rows], in_=onesT[:, 0:rows],
                           func=AF.Identity, scale=incol[:, k:k + 1])
        p.v.wait(p.cs)
        p.v.tensor_copy(out=cbf[:, :], in_=cfm[:, :])
        cbf_ready = p.cv.count
        p.t.wait(chg)
        p.t.wait(p.cv)

        tr_slot_ev = [0, 0]
        for t in range(n_steps):
            buf = xraw[t % 2]
            p.q.wait(p.cs)
            p.q.wait(p.cv)
            p.q.dma_start(out=sb3(buf, RT, D), in_=dram3(xs[t])).inc(chq, 16)
            p.s.wait(chq)
            p.v.wait(chq)
            p.v.wait(chmm)
            for rt in range(RT):
                src0 = buf[:, rt * D:(rt + 1) * D]
                nmu, rcol, nmr = emit_ln_stats(p, src0, D, sc, 0 if rt == 0 else 8, scr)
                p.v.tensor_scalar(out=xtm[:, rt * D:(rt + 1) * D], in0=src0,
                                  scalar1=rcol, scalar2=nmr, op0=ALU.mult,
                                  op1=ALU.add, _sd=1)
            ln_done = p.cv.count
            fbuf = xfm[t % 2]
            for k in range(KCH):
                for rt in range(RT):
                    slot = (k * RT + rt) % 2
                    bank = pt[slot]
                    p.t.wait(p.cv, ln_done)
                    p.t.wait(p.cs, tr_slot_ev[slot])
                    p.t.transpose(bank[:, :],
                                  xtm[:, rt * D + k * 128: rt * D + (k + 1) * 128],
                                  ident[:, :]).inc(chmm)
                    p.s.wait(chmm)
                    p.s.activation(out=fbuf[:, k * rows + rt * 128: k * rows + (rt + 1) * 128],
                                   in_=bank[:, :], func=AF.Identity)
                    tr_slot_ev[slot] = p.cs.count
            p.g.wait(p.cs)
            p.g.dma_start(out=dram3(xnfm[t]),
                          in_=sb3(fbuf, KCH, rows)).inc(chg, 16)

        p.barrier("tqvsg")

        slot_ev = [0] * 6
        for t in range(n_steps):
            fbuf = xfm[t % 2]
            p.q.wait(chmm)
            p.q.dma_start(out=sb3(fbuf, KCH, rows), in_=dram3(xnfm[t])).inc(chq, 16)
            xload = chq.count
            cupd = p.cv.count
            for pair in range(12):
                slot = pair % 6
                bank = ps[slot]
                p.t.wait(p.cs, slot_ev[slot])
                if pair == 0:
                    p.t.wait(chq, xload)
                    p.t.wait(p.cv, cupd if t > 0 else cbf_ready)
                for half in range(2):
                    ec = pair * 2 + half
                    outap = bank[:, half * rows:(half + 1) * rows]
                    p.t.matmul(outap, b3r[0:1, ec * 128:(ec + 1) * 128],
                               onesr[0:1, 0:rows],
                               start=(half == 0), stop=False)
                    for k in range(KCH):
                        p.t.matmul(outap,
                                   wbig[:, k * E3 + ec * 128: k * E3 + (ec + 1) * 128],
                                   fbuf[:, k * rows:(k + 1) * rows], start=False,
                                   stop=False)
                    for k in range(KCH):
                        h = p.t.matmul(outap,
                                       wbig[:, (KCH + k) * E3 + ec * 128: (KCH + k) * E3 + (ec + 1) * 128],
                                       cbf[:, k * rows:(k + 1) * rows], start=False,
                                       stop=(half == 1 and k == KCH - 1))
                h.inc(chmm)
                p.s.wait(chmm)
                for half in range(2):
                    ec = pair * 2 + half
                    outap = bank[:, half * rows:(half + 1) * rows]
                    func = AF.Sigmoid if ec < 16 else AF.Tanh
                    dst = fsb if ec < 8 else (isb if ec < 16 else gsb)
                    j = ec % 8
                    p.s.activation(out=dst[:, j * rows:(j + 1) * rows], in_=outap,
                                   func=func)
                slot_ev[slot] = p.cs.count
            mm_all = chmm.count
            ev_all = p.cs.count
            p.v.wait(p.cs, ev_all)
            p.v.wait(chmm, mm_all)
            for j in range(KCH):
                sl = slice(j * rows, (j + 1) * rows)
                p.v.tensor_tensor(out=tmp1[:, 0:rows], in0=isb[:, sl], in1=gsb[:, sl],
                                  op=ALU.mult)
                p.v.tensor_tensor(out=tmp2[:, 0:rows], in0=fsb[:, sl], in1=cfm[:, sl],
                                  op=ALU.mult)
                p.v.tensor_tensor(out=cfm[:, sl], in0=tmp2[:, 0:rows],
                                  in1=tmp1[:, 0:rows], op=ALU.add, _sd=1)
                p.v.tensor_copy(out=cbf[:, sl], in_=cfm[:, sl], _sd=1)

        p.barrier("tqvsg")

        for k in range(KCH):
            p.g.dma_start(out=wbig[:, k * D:(k + 1) * D],
                          in_=wot[k * 128:(k + 1) * 128, :]).inc(chg, 16)
            p.g.dma_start(out=wbig[:, (KCH + k) * D:(KCH + k + 1) * D],
                          in_=uot[k * 128:(k + 1) * 128, :]).inc(chg, 16)
        wload = chg.count
        for rt in range(RT):
            for ot in range(2):
                bank = ps[rt * 2 + ot]
                outap = bank[:, 0:512]
                p.t.wait(chg, wload)
                p.t.matmul(outap, onesr[0:1, 0:128], bor[0:1, ot * 512:(ot + 1) * 512],
                           start=True, stop=False)
                for k in range(KCH):
                    h = p.t.matmul(outap,
                                   cbf[:, k * rows + rt * 128: k * rows + (rt + 1) * 128],
                                   wbig[:, (KCH + k) * D + ot * 512: (KCH + k) * D + (ot + 1) * 512],
                                   start=False, stop=(k == KCH - 1))
                h.inc(chmm)
                p.v.wait(chmm)
                p.v.tensor_copy(out=cuo[:, rt * D + ot * 512: rt * D + (ot + 1) * 512],
                                in_=outap)
        cuo_ready = p.cv.count

        hslot_dv = [0] * 4
        for t in range(n_steps):
            fbuf = xfm[t % 2]
            p.q.wait(chmm)
            p.q.dma_start(out=sb3(fbuf, KCH, rows), in_=dram3(xnfm[t])).inc(chq, 16)
            xload = chq.count
            p.v.wait(chg)
            for rt in range(RT):
                for ot in range(2):
                    slot = rt * 2 + ot
                    bank = ps[slot]
                    outap = bank[:, 0:512]
                    p.t.wait(chq, xload)
                    p.t.wait(p.cv, hslot_dv[slot] if t > 0 else cuo_ready)
                    for k in range(KCH):
                        h = p.t.matmul(outap,
                                       fbuf[:, k * rows + rt * 128: k * rows + (rt + 1) * 128],
                                       wbig[:, k * D + ot * 512: k * D + (ot + 1) * 512],
                                       start=(k == 0), stop=(k == KCH - 1))
                    h.inc(chmm)
                    hsl = slice(rt * D + ot * 512, rt * D + (ot + 1) * 512)
                    p.v.wait(chmm)
                    p.v.tensor_tensor(out=htile[:, hsl], in0=outap, in1=cuo[:, hsl],
                                      op=ALU.add)
                    hslot_dv[slot] = p.cv.count
                    p.s.wait(p.cv)
                    p.s.activation(out=htile[:, hsl], in_=htile[:, hsl],
                                   func=AF.Tanh)
            p.g.wait(p.cs)
            p.g.dma_start(out=dram3(hbuf[t]), in_=sb3(htile, RT, D)).inc(chg, 16)

        p.barrier("tqvsg")

        for t in range(n_steps):
            hb = xraw[t % 2]
            p.q.wait(p.cs)
            p.q.wait(p.cv)
            p.q.dma_start(out=sb3(hb, RT, D), in_=dram3(hbuf[t])).inc(chq, 16)
            p.q.dma_start(out=sb3(cuo, RT, D), in_=dram3(xs[t])).inc(chq, 16)
            p.s.wait(chq)
            p.v.wait(chq)
            p.v.wait(chst)
            for rt in range(RT):
                src0 = hb[:, rt * D:(rt + 1) * D]
                nmu, rcol, nmr = emit_ln_stats(p, src0, D, sc, 0 if rt == 0 else 8, scr)
                hsl = slice(rt * D, (rt + 1) * D)
                p.v.tensor_scalar(out=htile[:, hsl], in0=cuo[:, hsl], scalar1=nmr,
                                  scalar2=None, op0=ALU.add, _sd=1)
                p.v.scalar_tensor_tensor(out=htile[:, hsl], in0=src0, scalar=rcol,
                                         in1=htile[:, hsl], op0=ALU.mult,
                                         op1=ALU.add, _sd=1)
            p.g.wait(p.cv)
            p.g.dma_start(out=dram3(xs2[t]), in_=sb3(htile, RT, D)).inc(chst, 16)

        p.g.wait(chst)
        p.finalize()
    return nc



def _mk_chans(p):
    return {
        "chq": p.chan("chq"), "chg": p.chan("chg"), "chw": p.chan("chw"),
        "chws": p.chan("chws"), "chy": p.chan("chy"),
        "cht": [p.chan("cht0"), p.chan("cht1")],
        "chh": [p.chan("chh0"), p.chan("chh1")],
        "chmm": p.chan("chmm"), "chst": p.chan("chst"),
    }



def emit_l1attn(nc, p, chs, xs, wqkv, proj, wout, bout, ident_in, xs3, n_b=4, S=2048):
    D = 1024
    H, HD, FEAT = 16, 64, 256
    KCH = D // 128
    TT = S // 128

    xnfm = p.sbuf("xnfm", [128, KCH * S], BF16)
    kfm = p.sbuf("kfm", [128, KCH * S], BF16)
    vtm = p.sbuf("vtm", [128, TT * D], BF16)
    qfm = p.sbuf("qfm", [128, KCH * S], BF16)
    wsb = p.sbuf("wsb", [128, KCH * D], BF16)
    projsb = p.sbuf("projsb", [128, H * FEAT], BF16)
    boutr = p.sbuf("boutr", [1, D], BF16)
    onesr = p.sbuf("onesr", [1, 128], BF16)
    onescol = p.sbuf("onescol", [128, 1], BF16)
    ident = p.sbuf("identS", [128, 128], BF16)
    kproj = p.sbuf("kproj", [128, TT * FEAT], BF16)
    qproj = p.sbuf("qproj", [128, 2 * S], BF16)
    kvsb = p.sbuf("kvsb", [128, 2 * (HD + 1)], BF16)
    xtile = p.sbuf("xtile", [128, D], F32)
    xtm = p.sbuf("xtm", [128, D], BF16)
    scr = p.sbuf("scr", [128, D], BF16)
    scr1 = p.sbuf("scr1", [128, 512], F32)
    scr2 = p.sbuf("scr2", [128, 512], F32)
    rtile = p.sbuf("rtile", [128, D], F32)
    sc = p.sbuf("sc", [128, 16], F32)
    rc = p.sbuf("rc", [128, 1], F32)
    pf = [p.psum(f"pf{i}", [128, 512], F32) for i in range(4)]
    ptr = [p.psum(f"ptr{i}", [128, 128], BF16) for i in range(2)]

    chq = chs["chq"]
    chg = chs["chg"]
    chmm = chs["chmm"]
    chst = chs["chst"]

    p.g.dma_start(out=ident[:, :], in_=ident_in[:, :]).inc(chg, 16)
    p.g.dma_start(out=boutr[0:1, :], in_=bout[:, :]).inc(chg, 16)
    for h in range(H):
        p.g.dma_start(out=projsb[0:64, h * FEAT:(h + 1) * FEAT],
                      in_=proj[h]).inc(chg, 16)
        p.g.dma_start(out=projsb[64:128, h * FEAT:(h + 1) * FEAT],
                      in_=proj[h]).inc(chg, 16)
    p.v.memset(onesr[0:1, :], 1.0)
    p.v.memset(onescol[:, 0:1], 1.0)
    p.t.wait(chg)
    p.t.wait(p.cv)

    pf_slot = [0] * 4
    ptr_slot = [0] * 2

    def tr_tile(src_bf_ap, dst_ap, i):
        slot = i % 2
        bank = ptr[slot]
        p.t.wait(p.cs, ptr_slot[slot])
        p.t.transpose(bank[:, :], src_bf_ap, ident[:, :]).inc(chmm)
        p.s.wait(chmm)
        p.s.activation(out=dst_ap, in_=bank[:, :], func=AF.Identity)
        ptr_slot[slot] = p.cs.count

    for b in range(n_b):
        for tt in range(TT):
            p.q.wait(p.cs)
            p.q.wait(p.cv)
            p.q.wait(chmm)
            p.q.wait(chst)
            p.q.dma_start(out=xtile[:, :], in_=xs[b, tt * 128:(tt + 1) * 128, :]).inc(chq, 16)
            p.s.wait(chq)
            p.v.wait(chq)
            nmu, rcol, nmr = emit_ln_stats(p, xtile[:, :], D, sc, 0, scr)
            p.v.wait(chmm)
            p.v.tensor_scalar(out=xtm[:, :], in0=xtile[:, :], scalar1=rcol,
                              scalar2=nmr, op0=ALU.mult, op1=ALU.add, _sd=1)
            lnv = p.cv.count
            p.t.wait(p.cv, lnv)
            for k in range(KCH):
                tr_tile(xtm[:, k * 128:(k + 1) * 128],
                        xnfm[:, k * S + tt * 128: k * S + (tt + 1) * 128], k)
        xn_done = p.cs.count

        for (dst, col0, fm) in ((kfm, D, True), (vtm, 2 * D, False), (qfm, 0, True)):
            p.g.wait(chmm)
            for k in range(KCH):
                p.g.dma_start(out=wsb[:, k * D:(k + 1) * D],
                              in_=wqkv[k * 128:(k + 1) * 128, col0:col0 + D]).inc(chg, 16)
            wl = chg.count
            if fm:
                for ec in range(KCH):
                    for t5 in range(S // 512):
                        slot = (ec * (S // 512) + t5) % 4
                        bank = pf[slot]
                        p.t.wait(p.cv, pf_slot[slot])
                        p.t.wait(chg, wl)
                        p.t.wait(p.cs, xn_done)
                        for k in range(KCH):
                            hm = p.t.matmul(bank[:, :],
                                            wsb[:, k * D + ec * 128: k * D + (ec + 1) * 128],
                                            xnfm[:, k * S + t5 * 512: k * S + (t5 + 1) * 512],
                                            start=(k == 0), stop=(k == KCH - 1))
                        hm.inc(chmm)
                        p.v.wait(chmm)
                        p.v.tensor_copy(out=dst[:, ec * S + t5 * 512: ec * S + (t5 + 1) * 512],
                                        in_=bank[:, :])
                        pf_slot[slot] = p.cv.count
            else:
                for tt in range(TT):
                    for nn in range(2):
                        slot = (tt * 2 + nn) % 4
                        bank = pf[slot]
                        p.t.wait(p.cv, pf_slot[slot])
                        p.t.wait(chg, wl)
                        p.t.wait(p.cs, xn_done)
                        for k in range(KCH):
                            hm = p.t.matmul(bank[:, :],
                                            xnfm[:, k * S + tt * 128: k * S + (tt + 1) * 128],
                                            wsb[:, k * D + nn * 512: k * D + (nn + 1) * 512],
                                            start=(k == 0), stop=(k == KCH - 1))
                        hm.inc(chmm)
                        p.v.wait(chmm)
                        p.v.tensor_copy(out=dst[:, tt * D + nn * 512: tt * D + (nn + 1) * 512],
                                        in_=bank[:, :])
                        pf_slot[slot] = p.cv.count
        qkv_done = p.cv.count

        for h in range(H):
            base = (h % 2) * 64
            ch = h // 2
            for tt in range(TT):
                slot = tt % 4
                bank = pf[slot]
                p.t.wait(p.cv, pf_slot[slot])
                p.t.wait(p.cv, qkv_done)
                hm = p.t.matmul(bank[:, 0:FEAT],
                                kfm[base:base + 64, ch * S + tt * 128: ch * S + (tt + 1) * 128],
                                projsb[base:base + 64, h * FEAT:(h + 1) * FEAT],
                                start=True, stop=True)
                hm.inc(chmm)
                p.v.wait(chmm)
                p.v.tensor_scalar(out=scr1[:, 0:FEAT], in0=bank[:, 0:FEAT],
                                  scalar1=0.0, scalar2=None, op0=ALU.min)
                p.s.wait(p.cv)
                p.s.activation(out=scr2[:, 0:FEAT], in_=scr1[:, 0:FEAT], func=AF.Exp)
                p.v.wait(p.cs)
                p.v.scalar_tensor_tensor(out=kproj[:, tt * FEAT:(tt + 1) * FEAT],
                                         in0=bank[:, 0:FEAT], scalar=0.0,
                                         in1=scr2[:, 0:FEAT], op0=ALU.max, op1=ALU.add)
                pf_slot[slot] = p.cv.count
            kp_done = p.cv.count
            for fc in range(2):
                slot_kv = fc % 2
                bank = pf[slot_kv]
                bank2 = pf[2 + slot_kv]
                p.t.wait(p.cv, pf_slot[slot_kv])
                p.t.wait(p.cv, pf_slot[2 + slot_kv])
                p.t.wait(p.cv, kp_done)
                for tt in range(TT):
                    hm1 = p.t.matmul(bank[:, 0:HD],
                                     kproj[:, tt * FEAT + fc * 128: tt * FEAT + (fc + 1) * 128],
                                     vtm[:, tt * D + h * HD: tt * D + (h + 1) * HD],
                                     start=(tt == 0), stop=(tt == TT - 1))
                hm1.inc(chmm)
                for tt in range(TT):
                    hm2 = p.t.matmul(bank2[:, 0:1],
                                     kproj[:, tt * FEAT + fc * 128: tt * FEAT + (fc + 1) * 128],
                                     onescol[:, 0:1],
                                     start=(tt == 0), stop=(tt == TT - 1))
                hm2.inc(chmm)
                p.v.wait(chmm)
                p.v.tensor_copy(out=kvsb[:, fc * (HD + 1): fc * (HD + 1) + HD],
                                in_=bank[:, 0:HD])
                p.v.tensor_copy(out=kvsb[:, fc * (HD + 1) + HD: (fc + 1) * (HD + 1)],
                                in_=bank2[:, 0:1])
                pf_slot[slot_kv] = p.cv.count
                pf_slot[2 + slot_kv] = p.cv.count
            kv_done = p.cv.count

            for t5 in range(S // 512):
                for fc in range(2):
                    slot = (t5 * 2 + fc) % 4
                    bank = pf[slot]
                    p.t.wait(p.cv, pf_slot[slot])
                    p.t.wait(p.cv, qkv_done)
                    hm = p.t.matmul(bank[:, 0:512],
                                    projsb[base:base + 64, h * FEAT + fc * 128: h * FEAT + (fc + 1) * 128],
                                    qfm[base:base + 64, ch * S + t5 * 512: ch * S + (t5 + 1) * 512],
                                    start=True, stop=True)
                    hm.inc(chmm)
                    p.v.wait(chmm)
                    p.v.tensor_scalar(out=scr1[:, 0:512], in0=bank[:, 0:512],
                                      scalar1=0.0, scalar2=None, op0=ALU.min)
                    p.s.wait(p.cv)
                    p.s.activation(out=scr2[:, 0:512], in_=scr1[:, 0:512], func=AF.Exp)
                    p.v.wait(p.cs)
                    p.v.scalar_tensor_tensor(
                        out=qproj[:, fc * S + t5 * 512:fc * S + (t5 + 1) * 512],
                        in0=bank[:, 0:512], scalar=0.0,
                        in1=scr2[:, 0:512], op0=ALU.max, op1=ALU.add)
                    pf_slot[slot] = p.cv.count
            qp_done = p.cv.count
            for tt in range(TT):
                slot = tt % 4
                bank = pf[slot]
                p.t.wait(p.cv, pf_slot[slot])
                p.t.wait(p.cv, qp_done)
                p.t.wait(p.cv, kv_done)
                for fc in range(2):
                    hm = p.t.matmul(bank[:, 0:HD + 1],
                                    qproj[:, fc * S + tt * 128: fc * S + (tt + 1) * 128],
                                    kvsb[:, fc * (HD + 1):(fc + 1) * (HD + 1)],
                                    start=(fc == 0), stop=(fc == 1))
                hm.inc(chmm)
                p.v.wait(chmm)
                p.v.tensor_scalar(out=rc[:, 0:1], in0=bank[:, HD:HD + 1],
                                  scalar1=1e-8, scalar2=None, op0=ALU.add, _sd=1)
                p.v.reciprocal(out=rc[:, 0:1], in_=rc[:, 0:1], _sd=1)
                p.v.tensor_scalar(out=xnfm[:, tt * D + h * HD: tt * D + (h + 1) * HD],
                                  in0=bank[:, 0:HD], scalar1=rc[:, 0:1],
                                  scalar2=None, op0=ALU.mult, _sd=1)
                pf_slot[slot] = p.cv.count
        o_done = p.cv.count

        p.g.wait(chmm)
        for k in range(KCH):
            p.g.dma_start(out=wsb[:, k * D:(k + 1) * D],
                          in_=wout[k * 128:(k + 1) * 128, :]).inc(chg, 16)
        wl = chg.count
        for tt in range(TT):
            p.t.wait(p.cv, o_done)
            for k in range(KCH):
                tr_tile(xnfm[:, tt * D + k * 128: tt * D + (k + 1) * 128],
                        kfm[:, k * 128:(k + 1) * 128], k)
            ofm_done = p.cs.count
            for nn in range(2):
                slot = nn % 2
                bank = pf[slot]
                p.t.wait(p.cv, pf_slot[slot])
                p.t.wait(p.cs, ofm_done)
                p.t.wait(chg, wl)
                p.t.matmul(bank[:, 0:512], onesr[0:1, 0:128],
                           boutr[0:1, nn * 512:(nn + 1) * 512], start=True, stop=False)
                for k in range(KCH):
                    hm = p.t.matmul(bank[:, 0:512],
                                    kfm[:, k * 128:(k + 1) * 128],
                                    wsb[:, k * D + nn * 512: k * D + (nn + 1) * 512],
                                    start=False, stop=(k == KCH - 1))
                hm.inc(chmm)
                p.v.wait(chmm)
                p.v.tensor_copy(out=rtile[:, nn * 512:(nn + 1) * 512], in_=bank[:, 0:512])
                pf_slot[slot] = p.cv.count
            p.q.wait(p.cs)
            p.q.wait(p.cv)
            p.q.wait(chst)
            p.q.dma_start(out=xtile[:, :], in_=xs[b, tt * 128:(tt + 1) * 128, :]).inc(chq, 16)
            p.s.wait(p.cv)
            nmu, rcol, nmr = emit_ln_stats(p, rtile[:, :], D, sc, 8, scr)
            p.v.wait(chq)
            p.v.wait(chst)
            p.v.tensor_scalar(out=scr1[:, 0:512], in0=xtile[:, 0:512], scalar1=nmr,
                              scalar2=None, op0=ALU.add, _sd=1)
            p.v.tensor_scalar(out=scr2[:, 0:512], in0=xtile[:, 512:1024], scalar1=nmr,
                              scalar2=None, op0=ALU.add, _sd=1)
            p.v.scalar_tensor_tensor(out=xtile[:, 0:512], in0=rtile[:, 0:512],
                                     scalar=rcol, in1=scr1[:, 0:512],
                                     op0=ALU.mult, op1=ALU.add)
            p.v.scalar_tensor_tensor(out=xtile[:, 512:1024], in0=rtile[:, 512:1024],
                                     scalar=rcol, in1=scr2[:, 0:512],
                                     op0=ALU.mult, op1=ALU.add)
            p.g.wait(p.cv)
            p.g.dma_start(out=xs3[b, tt * 128:(tt + 1) * 128, :],
                          in_=xtile[:, :]).inc(chst, 16)


def build_l1attn(n_b=4, S=2048):
    D = 1024
    H, HD, FEAT = 16, 64, 256
    nc = bass.Bass("TRN2", target_bir_lowering=False)
    xs = nc.dram_tensor("xs", [n_b, S, D], F32, kind="ExternalInput").ap()
    wqkv = nc.dram_tensor("wqkv", [D, 3 * D], F32, kind="ExternalInput").ap()
    proj = nc.dram_tensor("proj", [H, HD, FEAT], F32, kind="ExternalInput").ap()
    wout = nc.dram_tensor("wout", [D, D], F32, kind="ExternalInput").ap()
    bout = nc.dram_tensor("bout", [1, D], F32, kind="ExternalInput").ap()
    ident_in = nc.dram_tensor("ident", [128, 128], BF16, kind="ExternalInput").ap()
    xs3 = nc.dram_tensor("xs3", [n_b, S, D], F32, kind="ExternalOutput").ap()

    es = contextlib.ExitStack()
    with es:
        p = Prog(nc, es)
        chs = _mk_chans(p)
        emit_l1attn(nc, p, chs, xs, wqkv, proj, wout, bout, ident_in, xs3, n_b, S)
        p.g.wait(chs["chst"])
        p.finalize()
    return nc


def _host_l1attn_inputs(x2_full, P):
    wqkv = np.ascontiguousarray(P["L1_attn_w_qkv"])
    proj = np.ascontiguousarray(P["L1_attn_proj_matrix"])
    wout = np.ascontiguousarray(P["L1_attn_w_out"])
    bout = np.ascontiguousarray(P["L1_attn_b_out"][None, :]).astype(np.float32)
    ident = _ident128()
    in_maps = []
    for c in range(N_CORES):
        in_maps.append({
            "xs": np.ascontiguousarray(x2_full[c * 4:(c + 1) * 4]),
            "wqkv": wqkv, "proj": proj, "wout": wout, "bout": bout, "ident": ident,
        })
    return in_maps


def run_l1attn(x2_full, P, n_b=4):
    nc = build_l1attn(n_b=n_b)
    in_maps = _host_l1attn_inputs(x2_full, P)
    res = run_bass_kernel_spmd(nc, in_maps, core_ids=list(range(N_CORES)))
    x3 = np.empty((N_CORES * n_b, 2048, 1024), np.float32)
    for c in range(N_CORES):
        x3[c * n_b:(c + 1) * n_b] = res.results[c]["xs3"]
    return x3




def emit_l1ffn_a(nc, p, chs, xs, g1, wdown, wqkvc, wup, cosr, sinr, cmask, ident_in, xp, n_b=4, S=2048):
    D = 1024
    L = 512
    H, HD = 16, 32
    KCH = D // 128
    LCH = L // 128
    TT = S // 128
    NP = H // 2
    RS = 0.17677669529663687

    qaug = p.sbuf("qaug", [128, TT * H * 33], BF16)
    kaug = p.sbuf("kaug", [128, TT * H * 33], BF16)
    qfm = [p.sbuf(f"qfm{i}", [128, S], BF16) for i in range(NP)]
    kfm = [p.sbuf(f"kfm{i}", [128, S], BF16) for i in range(NP)]
    vsb = p.sbuf("vsb", [128, TT * H * 33], BF16)
    otm = p.sbuf("otm", [128, TT * L], BF16)
    cossb = p.sbuf("cossb", [128, 256], BF16)
    sinsb = p.sbuf("sinsb", [128, 256], BF16)
    wdsb = p.sbuf("wdsb", [128, KCH * L], BF16)
    wqsb = p.sbuf("wqsb", [128, LCH * 3 * L], BF16)
    wupsb = p.sbuf("wupsb", [128, LCH * D], BF16)
    cmsk = p.sbuf("cmsk", [128, 4 * 512], F32)
    ident = p.sbuf("identS", [128, 128], BF16)
    g1col = p.sbuf("g1col", [128, KCH], F32)
    qn = p.sbuf("qn", [128, TT * H], F32)
    knm = p.sbuf("knm", [128, H], F32)
    knmb = p.sbuf("knmb", [128, H], BF16)
    kmaxb = p.sbuf("kmaxb", [128, H], F32)
    kmrow = p.sbuf("kmrow", [1, 64], BF16)
    kred = p.sbuf("kred", [64, 128], F32)
    kcol = p.sbuf("kcol", [64, 1], F32)
    kcolb = p.sbuf("kcolb", [64, 1], BF16)
    onesr = p.sbuf("onesr", [1, 128], BF16)
    xtile = p.sbuf("xtile", [128, D], F32)
    x2til = p.sbuf("x2til", [128, D], F32)
    t1tm = p.sbuf("t1tm", [128, D], BF16)
    ctm = p.sbuf("ctm", [128, L], BF16)
    cfmt = p.sbuf("cfmt", [128, L], BF16)
    xfmt = p.sbuf("xfmt", [128, D], BF16)
    qtm = p.sbuf("qtm", [128, L], F32)
    ktm = p.sbuf("ktm", [128, L], F32)
    scr = p.sbuf("scr", [128, D], BF16)
    scrf = p.sbuf("scrf", [128, 512], F32)
    ptb = [p.sbuf(f"ptb{i}", [128, 512], BF16) for i in range(2)]
    sc = p.sbuf("sc", [128, 16], F32)
    rc = p.sbuf("rc", [128, 4], F32)
    pf = [p.psum(f"pf{i}", [128, 512], F32) for i in range(2)]
    po = [p.psum(f"po{i}", [128, 512], F32) for i in range(4)]
    ptr = [p.psum(f"ptr{i}", [128, 128], BF16) for i in range(2)]

    chq = chs["chq"]
    chg = chs["chg"]
    chmm = chs["chmm"]
    chst = chs["chst"]
    chws = chs["chws"]

    p.g.dma_start(out=ident[:, :], in_=ident_in[:, :]).inc(chg, 16)
    p.g.dma_start(out=cmsk[:, :], in_=cmask[:, :]).inc(chg, 16)

    p.g.dma_start(out=g1col[:, 0:KCH],
                  in_=g1.rearrange("one (k p) -> (one p) k", p=128),
                  allow_slow_non_contiguous=True).inc(chg, 16)
    for k in range(LCH):
        p.g.dma_start(out=wqsb[:, k * 3 * L:(k + 1) * 3 * L],
                      in_=wqkvc[k * 128:(k + 1) * 128, :]).inc(chg, 16)
        p.g.dma_start(out=wupsb[:, k * D:(k + 1) * D],
                      in_=wup[k * 128:(k + 1) * 128, :]).inc(chg, 16)
    for k in range(KCH):
        p.q.wait(p.cs)
        p.q.dma_start(out=scrf[:, 0:512],
                      in_=wdown[k * 128:(k + 1) * 128, :]).inc(chq, 16)
        p.s.wait(chq)
        p.s.wait(chg)
        p.s.activation(out=wdsb[:, k * L:(k + 1) * L], in_=scrf[:, 0:512],
                       func=AF.Identity, scale=g1col[:, k:k + 1])
    p.v.memset(onesr[0:1, :], 1.0)
    p.v.memset(vsb[:, :].rearrange("p (g w) -> p g w", w=33)[:, :, 32:33], 1.0)
    p.v.memset(kaug[:, :].rearrange("p (g w) -> p g w", w=33)[:, :, 32:33], 1.0)
    p.t.wait(chg)
    p.t.wait(chq)
    p.t.wait(p.cs)
    p.t.wait(p.cv)

    pfs = [(0, 0)] * 2
    pos = [(0, 0)] * 4
    ptrs = [(0, 0)] * 2
    ptbs = [0] * 2

    def wait_slot(eng, mark):
        eng.wait(p.cv, mark[0])
        eng.wait(p.cs, mark[1])

    for b in range(n_b):
        p.v.wait(p.cs)
        p.v.memset(knm[:, :], 0.0)
        for tt in range(TT):
            p.q.wait(p.cs)
            p.q.wait(p.cv)
            p.q.wait(chg)
            p.q.dma_start(out=xtile[:, :], in_=xs[b, tt * 128:(tt + 1) * 128, :]).inc(chq, 16)
            p.q.dma_start(out=cossb[:, :], in_=cosr[tt * 128:(tt + 1) * 128, :]).inc(chq, 16)
            p.q.dma_start(out=sinsb[:, :], in_=sinr[tt * 128:(tt + 1) * 128, :]).inc(chq, 16)
            p.s.wait(chq)
            p.v.wait(chq)
            nmu, rcol, nmr = emit_ln_stats(p, xtile[:, :], D, sc, 0, scr)
            p.v.wait(chst)
            p.v.wait(chg)
            p.v.tensor_scalar(out=x2til[:, :], in0=xtile[:, :], scalar1=rcol,
                              scalar2=nmr, op0=ALU.mult, op1=ALU.add, _sd=1)
            p.s.wait(p.cv)
            p.s.activation(out=scr[:, 0:D], in_=x2til[:, :], func=AF.Square,
                           accum_out=sc[:, 8:9])
            p.v.wait(p.cs)
            p.v.tensor_scalar(out=sc[:, 9:10], in0=sc[:, 8:9], scalar1=1.0 / D,
                              scalar2=EPS, op0=ALU.mult, op1=ALU.add)
            p.s.wait(p.cv)
            p.s.activation(out=sc[:, 10:11], in_=sc[:, 9:10], func=AF.Sqrt)
            p.v.wait(p.cs)
            p.v.reciprocal(out=sc[:, 10:11], in_=sc[:, 10:11])
            p.v.wait(chmm)
            p.v.tensor_scalar(out=t1tm[:, :], in0=x2til[:, :],
                              scalar1=sc[:, 10:11], scalar2=None, op0=ALU.mult, _sd=1)
            t1_done = p.cv.count
            p.g.wait(p.cv, t1_done)
            p.g.dma_start(out=xp[b, tt * 128:(tt + 1) * 128, :],
                          in_=x2til[:, :]).inc(chg, 16)
            for k in range(KCH):
                slot = k % 2
                bank = ptr[slot]
                wait_slot(p.t, ptrs[slot])
                p.t.wait(p.cv, t1_done)
                p.t.transpose(bank[:, :], t1tm[:, k * 128:(k + 1) * 128],
                              ident[:, :]).inc(chmm)
                p.v.wait(chmm)
                p.v.tensor_copy(out=xfmt[:, k * 128:(k + 1) * 128], in_=bank[:, :])
                ptrs[slot] = (p.cv.count, 0)
            xfm_done = p.cv.count
            slot = 0
            bank = pf[slot]
            wait_slot(p.t, pfs[slot])
            p.t.wait(p.cv, xfm_done)
            for k in range(KCH):
                hm = p.t.matmul(bank[:, :], xfmt[:, k * 128:(k + 1) * 128],
                                wdsb[:, k * L:(k + 1) * L],
                                start=(k == 0), stop=(k == KCH - 1))
            hm.inc(chmm)
            p.v.wait(chmm)
            p.v.tensor_copy(out=ctm[:, :], in_=bank[:, :])
            pfs[slot] = (p.cv.count, 0)
            for k in range(LCH):
                slot = k % 2
                bank = ptr[slot]
                wait_slot(p.t, ptrs[slot])
                p.t.wait(p.cv)
                p.t.transpose(bank[:, :], ctm[:, k * 128:(k + 1) * 128],
                              ident[:, :]).inc(chmm)
                p.v.wait(chmm)
                p.v.tensor_copy(out=cfmt[:, k * 128:(k + 1) * 128], in_=bank[:, :])
                ptrs[slot] = (p.cv.count, 0)
            cfm_done = p.cv.count
            for part in range(3):
                slot = part % 2
                bank = pf[slot]
                wait_slot(p.t, pfs[slot])
                p.t.wait(p.cv, cfm_done)
                for k in range(LCH):
                    hm = p.t.matmul(bank[:, :], cfmt[:, k * 128:(k + 1) * 128],
                                    wqsb[:, k * 3 * L + part * L: k * 3 * L + (part + 1) * L],
                                    start=(k == 0), stop=(k == LCH - 1))
                hm.inc(chmm)
                p.v.wait(chmm)
                if part == 0:
                    p.v.tensor_copy(out=qtm[:, :], in_=bank[:, :])
                elif part == 1:
                    p.v.tensor_copy(out=ktm[:, :], in_=bank[:, :])
                else:
                    p.v.tensor_copy(
                        out=vsb[:, :].rearrange("p (g w) -> p g w", w=33)
                            [:, tt * H:(tt + 1) * H, 0:32],
                        in_=bank[:, :].rearrange("p (g w) -> p g w", w=32))
                pfs[slot] = (p.cv.count, 0)
            for (tsrc, dst) in ((qtm, qaug), (ktm, kaug)):
                s3 = tsrc[:, :].rearrange("p (g w) -> p g w", w=32)
                d3 = dst[:, :].rearrange("p (g w) -> p g w", w=33)[:, tt * H:(tt + 1) * H, :]
                c3 = cossb[:, :].rearrange("p (g w) -> p g w", w=16)
                s3s = sinsb[:, :].rearrange("p (g w) -> p g w", w=16)
                p.v.tensor_tensor(out=scrf[:, 0:256].rearrange("p (g w) -> p g w", w=16),
                                  in0=s3[:, :, 0:32:2], in1=c3, op=ALU.mult, _sd=1)
                p.v.tensor_tensor(out=scrf[:, 256:512].rearrange("p (g w) -> p g w", w=16),
                                  in0=s3[:, :, 1:32:2], in1=s3s, op=ALU.mult, _sd=1)
                p.v.tensor_tensor(out=d3[:, :, 0:32:2],
                                  in0=scrf[:, 0:256].rearrange("p (g w) -> p g w", w=16),
                                  in1=scrf[:, 256:512].rearrange("p (g w) -> p g w", w=16),
                                  op=ALU.subtract, _sd=1)
                p.v.tensor_tensor(out=scrf[:, 0:256].rearrange("p (g w) -> p g w", w=16),
                                  in0=s3[:, :, 0:32:2], in1=s3s, op=ALU.mult, _sd=1)
                p.v.tensor_tensor(out=scrf[:, 256:512].rearrange("p (g w) -> p g w", w=16),
                                  in0=s3[:, :, 1:32:2], in1=c3, op=ALU.mult, _sd=1)
                p.v.tensor_tensor(out=d3[:, :, 1:33:2],
                                  in0=scrf[:, 0:256].rearrange("p (g w) -> p g w", w=16),
                                  in1=scrf[:, 256:512].rearrange("p (g w) -> p g w", w=16),
                                  op=ALU.add, _sd=1)
            q3 = qaug[:, :].rearrange("p (g w) -> p g w", w=33)[:, tt * H:(tt + 1) * H, 0:32]
            k3 = kaug[:, :].rearrange("p (g w) -> p g w", w=33)[:, tt * H:(tt + 1) * H, 0:32]
            p.s.wait(p.cv)
            p.s.activation(out=scrf[:, 0:512].rearrange("p (g w) -> p g w", w=32),
                           in_=q3, func=AF.Square)
            p.v.wait(p.cs)
            p.v.tensor_reduce(out=qn[:, tt * H:(tt + 1) * H].rearrange("p (g w) -> p g w", w=1),
                              in_=scrf[:, 0:512].rearrange("p (g w) -> p g w", w=32),
                              axis=AX.X, op=ALU.add)
            p.s.wait(p.cv)
            p.s.activation(out=qn[:, tt * H:(tt + 1) * H],
                           in_=qn[:, tt * H:(tt + 1) * H], func=AF.Sqrt)
            p.s.activation(out=scrf[:, 0:512].rearrange("p (g w) -> p g w", w=32),
                           in_=k3, func=AF.Square)
            p.v.wait(p.cs)
            p.v.tensor_reduce(out=sc[:, 0:16].rearrange("p (g w) -> p g w", w=1),
                              in_=scrf[:, 0:512].rearrange("p (g w) -> p g w", w=32),
                              axis=AX.X, op=ALU.add)
            p.v.tensor_tensor(out=knm[:, :], in0=knm[:, :], in1=sc[:, 0:16],
                              op=ALU.max, _sd=1)
        p.v.tensor_copy(out=knmb[:, :], in_=knm[:, :], _sd=1)
        kn_done = p.cv.count
        slot = 0
        bank = ptr[slot]
        wait_slot(p.t, ptrs[slot])
        p.t.wait(p.cv, kn_done)
        p.t.transpose(bank[0:16, 0:128], knmb[:, 0:16], ident[:, :]).inc(chmm)
        p.v.wait(chmm)
        p.v.tensor_copy(out=kred[0:16, :], in_=bank[0:16, :])
        p.v.tensor_reduce(out=kcol[0:16, 0:1], in_=kred[0:16, :], axis=AX.X,
                          op=ALU.max, _sd=1)
        p.v.tensor_scalar(out=kcol[0:16, 0:1], in0=kcol[0:16, 0:1], scalar1=0.0,
                          scalar2=1e18, op0=ALU.max, op1=ALU.min, _sd=1)
        p.s.wait(p.cv)
        p.s.activation(out=kcol[0:16, 0:1], in_=kcol[0:16, 0:1], func=AF.Sqrt)
        p.v.wait(p.cs)
        p.v.tensor_copy(out=kcolb[0:16, 0:1], in_=kcol[0:16, 0:1], _sd=1)
        ptrs[slot] = (p.cv.count, 0)
        kb_done = p.cv.count
        slot = 1
        bank = ptr[slot]
        wait_slot(p.t, ptrs[slot])
        p.t.wait(p.cv, kb_done)
        p.t.transpose(bank[0:1, 0:16], kcolb[0:16, 0:1], ident[0:16, 0:16]).inc(chmm)
        p.v.wait(chmm)
        p.v.tensor_copy(out=kmrow[0:1, 0:16], in_=bank[0:1, 0:16])
        ptrs[slot] = (p.cv.count, 0)
        km_done = p.cv.count
        slot = 0
        bank = pf[slot]
        wait_slot(p.t, pfs[slot])
        p.t.wait(p.cv, km_done)
        p.t.matmul(bank[:, 0:16], onesr[0:1, 0:128], kmrow[0:1, 0:16],
                   start=True, stop=True).inc(chmm)
        p.v.wait(chmm)
        p.v.tensor_copy(out=kmaxb[:, :], in_=bank[:, 0:16])
        pfs[slot] = (p.cv.count, 0)

        q33 = qaug[:, :].rearrange("p (g w) -> p g w", w=33)
        p.v.wait(p.cs)
        for tt in range(TT):
            for h in range(H):
                p.v.scalar_tensor_tensor(
                    out=q33[:, tt * H + h, 32:33],
                    in0=qn[:, tt * H + h: tt * H + h + 1], scalar=-1.0,
                    in1=kmaxb[:, h:h + 1], op0=ALU.mult, op1=ALU.mult, _sd=1)
        ccol_done = p.cv.count
        for tt in range(TT):
            for h in range(H):
                pair, base = h // 2, 64 * (h % 2)
                for (srcbuf, dstbuf) in ((qaug, qfm[pair]), (kaug, kfm[pair])):
                    slot = (tt * H + h) % 2
                    bank = ptr[slot]
                    wait_slot(p.t, ptrs[slot])
                    p.t.wait(p.cv, ccol_done)
                    p.t.transpose(bank[0:33, :],
                                  srcbuf[:, tt * 528 + h * 33: tt * 528 + (h + 1) * 33],
                                  ident[:, :]).inc(chmm)
                    p.v.wait(chmm)
                    p.v.tensor_copy(out=dstbuf[base:base + 33, tt * 128:(tt + 1) * 128],
                                    in_=bank[0:33, :])
                    ptrs[slot] = (p.cv.count, 0)
        aug_done = p.cv.count

        for h in range(H):
            pair, base = h // 2, 64 * (h % 2)
            for g in range(4):
                nki = 4 * g + 4
                for ki in range(nki):
                    slot = ki % 2
                    bank = pf[slot]
                    wait_slot(p.t, pfs[slot])
                    p.t.wait(p.cv, aug_done)
                    p.t.matmul(bank[:, :],
                               kfm[pair][base:base + 33, ki * 128:(ki + 1) * 128],
                               qfm[pair][base:base + 33, g * 512:(g + 1) * 512],
                               start=True, stop=True).inc(chmm)
                    mmv = chmm.count
                    if ki >= 4 * g:
                        phi = ki - 4 * g
                        p.v.wait(chmm, mmv)
                        p.v.scalar_tensor_tensor(out=bank[:, :], in0=bank[:, :],
                                                 scalar=0.0, in1=cmsk[:, phi * 512:(phi + 1) * 512],
                                                 op0=ALU.add, op1=ALU.add)
                        dvm = p.cv.count
                    else:
                        dvm = 0
                    pslot = ki % 2
                    p.s.wait(chmm, ptbs[pslot])
                    p.s.wait(chmm, mmv)
                    if dvm:
                        p.s.wait(p.cv, dvm)
                    p.s.activation(out=ptb[pslot][:, :], in_=bank[:, :],
                                   func=AF.Exp, scale=RS)
                    pfs[slot] = (p.cv.count, p.cs.count)
                    pt_done = p.cs.count
                    last = None
                    for j in range(4):
                        qt = 4 * g + j
                        if qt < ki:
                            continue
                        ob = po[j]
                        if ki == 0:
                            wait_slot(p.t, pos[j])
                        p.t.wait(p.cs, pt_done)
                        last = p.t.matmul(ob[:, 0:33],
                                          ptb[pslot][:, j * 128:(j + 1) * 128],
                                          vsb[:, ki * 528 + h * 33: ki * 528 + (h + 1) * 33],
                                          start=(ki == 0), stop=(ki == qt))
                    last.inc(chmm)
                    ptbs[pslot] = chmm.count
                p.v.wait(chmm)
                for j in range(4):
                    qt = 4 * g + j
                    ob = po[j]
                    p.v.reciprocal(out=rc[:, j:j + 1], in_=ob[:, 32:33])
                    p.v.tensor_scalar(out=otm[:, qt * 512 + h * 32: qt * 512 + (h + 1) * 32],
                                      in0=ob[:, 0:32], scalar1=rc[:, j:j + 1],
                                      scalar2=None, op0=ALU.mult, _sd=1)
                    pos[j] = (p.cv.count, 0)
        o_done = p.cv.count

        for tt in range(TT):
            p.q.wait(p.cs)
            p.q.wait(p.cv)
            p.q.wait(chst)
            p.q.dma_start(out=xtile[:, :], in_=xp[b, tt * 128:(tt + 1) * 128, :]).inc(chq, 16)
            for k in range(LCH):
                slot = k % 2
                bank = ptr[slot]
                wait_slot(p.t, ptrs[slot])
                p.t.wait(p.cv, o_done)
                p.t.transpose(bank[:, :], otm[:, tt * 512 + k * 128: tt * 512 + (k + 1) * 128],
                              ident[:, :]).inc(chmm)
                p.v.wait(chmm)
                p.v.tensor_copy(out=cfmt[:, k * 128:(k + 1) * 128], in_=bank[:, :])
                ptrs[slot] = (p.cv.count, 0)
            ofm_done = p.cv.count
            for nn in range(2):
                slot = nn % 2
                bank = pf[slot]
                wait_slot(p.t, pfs[slot])
                p.t.wait(p.cv, ofm_done)
                for k in range(LCH):
                    hm = p.t.matmul(bank[:, :], cfmt[:, k * 128:(k + 1) * 128],
                                    wupsb[:, k * D + nn * 512: k * D + (nn + 1) * 512],
                                    start=(k == 0), stop=(k == LCH - 1))
                hm.inc(chmm)
                p.v.wait(chmm)
                p.v.wait(chq)
                p.v.wait(chg)
                p.v.tensor_tensor(out=x2til[:, nn * 512:(nn + 1) * 512],
                                  in0=bank[:, :], in1=xtile[:, nn * 512:(nn + 1) * 512],
                                  op=ALU.add)
                pfs[slot] = (p.cv.count, 0)
            p.g.wait(p.cv)
            p.g.dma_start(out=xp[b, tt * 128:(tt + 1) * 128, :],
                          in_=x2til[:, :]).inc(chst, 16)


def build_l1ffn_a(n_b=4, S=2048):
    D, L = 1024, 512
    nc = bass.Bass("TRN2", target_bir_lowering=False)
    xs = nc.dram_tensor("xs", [n_b, S, D], F32, kind="ExternalInput").ap()
    g1 = nc.dram_tensor("g1", [1, D], F32, kind="ExternalInput").ap()
    wdown = nc.dram_tensor("wdown", [D, L], F32, kind="ExternalInput").ap()
    wqkvc = nc.dram_tensor("wqkvc", [L, 3 * L], F32, kind="ExternalInput").ap()
    wup = nc.dram_tensor("wup", [L, D], F32, kind="ExternalInput").ap()
    cosr = nc.dram_tensor("cosr", [S, 256], BF16, kind="ExternalInput").ap()
    sinr = nc.dram_tensor("sinr", [S, 256], BF16, kind="ExternalInput").ap()
    cmask = nc.dram_tensor("cmask", [128, 4 * 512], F32, kind="ExternalInput").ap()
    ident_in = nc.dram_tensor("ident", [128, 128], BF16, kind="ExternalInput").ap()
    xp = nc.dram_tensor("xp", [n_b, S, D], F32, kind="ExternalOutput").ap()

    es = contextlib.ExitStack()
    with es:
        p = Prog(nc, es)
        chs = _mk_chans(p)
        emit_l1ffn_a(nc, p, chs, xs, g1, wdown, wqkvc, wup, cosr, sinr, cmask, ident_in, xp, n_b, S)
        p.g.wait(chs["chst"])
        p.finalize()
    return nc


def _rope_tables(S=2048, HD=32, H=16):
    inv = 1.0 / (10000.0 ** (np.arange(0, HD, 2, dtype=np.float32) / HD))
    ang = np.arange(S, dtype=np.float32)[:, None] * inv[None, :]
    cos = np.cos(ang).astype(np.float32)
    sin = np.sin(ang).astype(np.float32)
    cosr = np.tile(cos, (1, H))
    sinr = np.tile(sin, (1, H))
    return np.ascontiguousarray(cosr), np.ascontiguousarray(sinr)


def _cmask_phases():
    m = np.zeros((128, 4 * 512), np.float32)
    for phi in range(4):
        k = np.arange(128)[:, None]
        j = np.arange(512)[None, :]
        m[:, phi * 512:(phi + 1) * 512] = np.where(j >= k + 128 * phi, 0.0, -1e30)
    return np.ascontiguousarray(m)


def _host_l1ffna_inputs(x3_full, P):
    wqkvc = np.ascontiguousarray(
        np.concatenate([P["L1_ffn_wq"], P["L1_ffn_wk"], P["L1_ffn_wv"]], axis=1))
    cosr, sinr = _rope_tables()
    cosr = cosr.astype(ml_dtypes.bfloat16)
    sinr = sinr.astype(ml_dtypes.bfloat16)
    in_maps = []
    for c in range(N_CORES):
        in_maps.append({
            "xs": np.ascontiguousarray(x3_full[c * 4:(c + 1) * 4]),
            "g1": np.ascontiguousarray(P["L1_ffn_g1"][None, :]),
            "wdown": np.ascontiguousarray(P["L1_ffn_w_down"]),
            "wqkvc": wqkvc,
            "wup": np.ascontiguousarray(P["L1_ffn_w_up"]),
            "cosr": cosr, "sinr": sinr, "cmask": _cmask_phases(),
            "ident": _ident128(),
        })
    return in_maps




def emit_l1ffn_b(nc, p, chs, xs, xpb, g2, rw, w1, w3, w2, ident_in, xs4, tfm, hidb, ybuf, n_b=4, S=2048):
    D = 1024
    F = 2048
    E = 8
    KCH = D // 128
    FCH = F // 128
    TT = S // 128
    NTU = n_b * S // 512

    w13 = p.sbuf("w13", [128, KCH * 2 * F], BF16)
    w2sb = p.sbuf("w2sb", [128, FCH * D], BF16)
    tfsb = [p.sbuf(f"tfsb{i}", [128, KCH * 512], BF16) for i in range(2)]
    hidw = p.sbuf("hidw", [128, FCH * 512], BF16)
    hidr = [p.sbuf(f"hidr{i}", [128, FCH * 512], BF16) for i in range(2)]
    yacc = p.sbuf("yacc", [128, D], F32)
    ybf = p.sbuf("ybf", [128, D], F32)
    gatesb = p.sbuf("gatesb", [128, n_b * TT * E], F32)
    lgsb = p.sbuf("lgsb", [128, n_b * TT * E], F32)
    rwsb = p.sbuf("rwsb", [128, KCH * E], BF16)
    g2col = p.sbuf("g2col", [128, KCH], F32)
    ident = p.sbuf("identS", [128, 128], BF16)
    xtile = p.sbuf("xtile", [128, D], F32)
    x2til = p.sbuf("x2til", [128, D], F32)
    t1tm = p.sbuf("t1tm", [128, D], BF16)
    xfmt = p.sbuf("xfmt", [128, D], BF16)
    scr = p.sbuf("scr", [128, D], BF16)
    scrf = p.sbuf("scrf", [128, F], F32)
    scrb = p.sbuf("scrb", [128, 512], BF16)
    sc = p.sbuf("sc", [128, 16], F32)
    pf = [p.psum(f"pf{i}", [128, 512], F32) for i in range(6)]
    ptr = [p.psum(f"ptr{i}", [128, 128], BF16) for i in range(2)]

    chq = chs["chq"]
    chg = chs["chg"]
    chw = chs["chw"]
    chy = chs["chy"]
    chws = chs["chws"]
    cht = chs["cht"]
    chh = chs["chh"]
    chmm = chs["chmm"]
    chst = chs["chst"]

    p.g.dma_start(out=ident[:, :], in_=ident_in[:, :]).inc(chw, 16)
    p.g.dma_start(out=g2col[:, 0:KCH],
                  in_=g2.rearrange("one (k p) -> (one p) k", p=128),
                  allow_slow_non_contiguous=True).inc(chw, 16)
    for k in range(KCH):
        p.q.wait(p.cs)
        p.q.dma_start(out=scrf[:, 0:E], in_=rw[k * 128:(k + 1) * 128, :]).inc(chws, 16)
        p.s.wait(chws)
        p.s.wait(chw)
        p.s.activation(out=rwsb[:, k * E:(k + 1) * E], in_=scrf[:, 0:E],
                       func=AF.Identity, scale=g2col[:, k:k + 1])
    p.t.wait(chw)
    p.t.wait(chws)
    p.t.wait(p.cs)

    pfs = [(0, 0)] * 6
    ptrs = [(0, 0)] * 2

    def wait_slot(eng, mark):
        eng.wait(p.cv, mark[0])
        eng.wait(p.cs, mark[1])

    for b in range(n_b):
        for tt in range(TT):
            p.q.wait(p.cs)
            p.q.wait(p.cv)
            p.q.dma_start(out=xtile[:, :], in_=xpb[b, tt * 128:(tt + 1) * 128, :]).inc(chq, 16)
            p.s.wait(chq)
            p.s.activation(out=scr[:, 0:D], in_=xtile[:, :], func=AF.Square,
                           accum_out=sc[:, 0:1])
            p.v.wait(p.cs)
            p.v.tensor_scalar(out=sc[:, 1:2], in0=sc[:, 0:1], scalar1=1.0 / D,
                              scalar2=EPS, op0=ALU.mult, op1=ALU.add)
            p.s.wait(p.cv)
            p.s.activation(out=sc[:, 2:3], in_=sc[:, 1:2], func=AF.Sqrt)
            p.v.wait(p.cs)
            p.v.reciprocal(out=sc[:, 2:3], in_=sc[:, 2:3], _sd=1)
            p.v.wait(chmm)
            p.v.tensor_scalar(out=t1tm[:, :], in0=xtile[:, :],
                              scalar1=sc[:, 2:3], scalar2=None, op0=ALU.mult, _sd=1)
            t1_done = p.cv.count
            for k in range(KCH):
                slot = k % 2
                bank = ptr[slot]
                wait_slot(p.t, ptrs[slot])
                p.t.wait(p.cv, t1_done)
                p.t.transpose(bank[:, :], t1tm[:, k * 128:(k + 1) * 128],
                              ident[:, :]).inc(chmm)
                p.v.wait(chmm)
                p.v.wait(chg)
                p.v.tensor_copy(out=xfmt[:, k * 128:(k + 1) * 128], in_=bank[:, :])
                ptrs[slot] = (p.cv.count, 0)
            xfm_done = p.cv.count
            p.g.wait(p.cv, xfm_done)
            p.g.dma_start(
                out=tfm[b].rearrange("(k p) s -> p k s", p=128)[:, :, tt * 128:(tt + 1) * 128],
                in_=xfmt[:, :].rearrange("p (k s) -> p k s", s=128)).inc(chg, 16)
            slot = 0
            bank = pf[slot]
            wait_slot(p.t, pfs[slot])
            p.t.wait(p.cv, xfm_done)
            for k in range(KCH):
                hm = p.t.matmul(bank[:, 0:E], xfmt[:, k * 128:(k + 1) * 128],
                                rwsb[:, k * E:(k + 1) * E],
                                start=(k == 0), stop=(k == KCH - 1))
            hm.inc(chmm)
            p.v.wait(chmm)
            p.v.tensor_copy(out=lgsb[:, (b * TT + tt) * E:(b * TT + tt + 1) * E],
                            in_=bank[:, 0:E])
            pfs[slot] = (p.cv.count, 0)

    for i in range(n_b * TT):
        lg = lgsb[:, i * E:(i + 1) * E]
        gt = gatesb[:, i * E:(i + 1) * E]
        p.v.tensor_reduce(out=sc[:, 0:1], in_=lg, axis=AX.X, op=ALU.max)
        p.v.tensor_scalar(out=sc[:, 1:2], in0=sc[:, 0:1], scalar1=-1.0,
                          scalar2=None, op0=ALU.mult, _sd=1)
        p.s.wait(p.cv)
        p.s.activation(out=scrf[:, 0:E], in_=lg, func=AF.Exp, bias=sc[:, 1:2])
        p.v.wait(p.cs)
        p.v.tensor_reduce(out=sc[:, 2:3], in_=scrf[:, 0:E], axis=AX.X, op=ALU.max)
        p.v.tensor_scalar(out=scrf[:, 8:16], in0=scrf[:, 0:E], scalar1=sc[:, 2:3],
                          scalar2=None, op0=ALU.is_equal, _sd=1)
        p.v.tensor_tensor(out=scrf[:, 16:24], in0=scrf[:, 0:E], in1=scrf[:, 8:16],
                          op=ALU.mult, _sd=1)
        p.v.tensor_tensor(out=scrf[:, 24:32], in0=scrf[:, 0:E], in1=scrf[:, 16:24],
                          op=ALU.subtract, _sd=1)
        p.v.tensor_reduce(out=sc[:, 3:4], in_=scrf[:, 24:32], axis=AX.X, op=ALU.max, _sd=1)
        p.v.tensor_scalar(out=scrf[:, 32:40], in0=scrf[:, 24:32], scalar1=sc[:, 3:4],
                          scalar2=None, op0=ALU.is_equal, _sd=1)
        p.v.tensor_tensor(out=scrf[:, 40:48], in0=scrf[:, 8:16], in1=scrf[:, 32:40],
                          op=ALU.add, _sd=1)
        p.v.tensor_tensor(out=sc[:, 4:5], in0=sc[:, 2:3], in1=sc[:, 3:4], op=ALU.add)
        p.v.reciprocal(out=sc[:, 4:5], in_=sc[:, 4:5], _sd=1)
        p.v.tensor_tensor(out=scrf[:, 48:56], in0=scrf[:, 0:E], in1=scrf[:, 40:48],
                          op=ALU.mult, _sd=1)
        p.v.tensor_scalar(out=gt, in0=scrf[:, 48:56], scalar1=sc[:, 4:5],
                          scalar2=None, op0=ALU.mult, _sd=1)
    gates_done = p.cv.count

    for e in range(E):
        for k in range(KCH):
            for (wt, off) in ((w1, 0), (w3, F)):
                p.q.wait(p.cs)
                p.q.wait(p.cv)
                p.q.dma_start(out=scrf[:, 0:F],
                              in_=wt[e, k * 128:(k + 1) * 128, :]).inc(chws, 16)
                p.s.wait(chws)
                p.s.wait(chmm)
                p.s.activation(out=w13[:, k * 2 * F + off: k * 2 * F + off + F],
                               in_=scrf[:, 0:F], func=AF.Identity,
                               scale=g2col[:, k:k + 1])
        w_done = p.cs.count
        for tu in range(NTU):
            tb = tfsb[tu % 2]
            p.q.wait(chmm)
            p.q.dma_start(
                out=tb[:, :].rearrange("p (k s) -> p k s", s=512),
                in_=tfm[tu // 4].rearrange("(k p) s -> p k s", p=128)
                    [:, :, (tu % 4) * 512:(tu % 4 + 1) * 512]).inc(cht[tu % 2], 16)
            tchan, tload = cht[tu % 2], cht[tu % 2].count
            for fch in range(FCH):
                sa = fch % 3
                sb_ = 3 + fch % 3
                banka, bankb = pf[sa], pf[sb_]
                wait_slot(p.t, pfs[sa])
                wait_slot(p.t, pfs[sb_])
                p.t.wait(tchan, tload)
                p.t.wait(p.cs, w_done)
                for k in range(KCH):
                    p.t.matmul(banka[:, :],
                               w13[:, k * 2 * F + fch * 128: k * 2 * F + (fch + 1) * 128],
                               tb[:, k * 512:(k + 1) * 512],
                               start=(k == 0), stop=(k == KCH - 1))
                for k in range(KCH):
                    hm = p.t.matmul(bankb[:, :],
                                    w13[:, k * 2 * F + F + fch * 128: k * 2 * F + F + (fch + 1) * 128],
                                    tb[:, k * 512:(k + 1) * 512],
                                    start=(k == 0), stop=(k == KCH - 1))
                hm.inc(chmm)
                p.s.wait(chmm)
                p.s.wait(p.cv)
                p.s.activation(out=scrb[:, 0:512], in_=banka[:, :], func=AF.Silu)
                p.v.wait(p.cs)
                p.v.wait(chg)
                p.v.scalar_tensor_tensor(out=hidw[:, fch * 512:(fch + 1) * 512],
                                         in0=bankb[:, :], scalar=0.0,
                                         in1=scrb[:, 0:512], op0=ALU.add,
                                         op1=ALU.mult)
                pfs[sa] = (0, p.cs.count)
                pfs[sb_] = (p.cv.count, 0)
            p.g.wait(p.cv)
            p.g.dma_start(out=dram3(hidb[e, tu]),
                          in_=hidw[:, :].rearrange("p (k s) -> p k s", s=512)).inc(chg, 16)

    p.q.wait(chg)
    for e in range(E):
        for k in range(FCH):
            p.g.wait(chmm)
            p.g.dma_start(out=w2sb[:, k * D:(k + 1) * D],
                          in_=w2[e, k * 128:(k + 1) * 128, :]).inc(chw, 16)
        w2_done = chw.count
        p.g.wait(chy)
        for tu in range(NTU):
            hb = hidr[tu % 2]
            p.q.wait(chmm)
            p.q.dma_start(out=hb[:, :].rearrange("p (k s) -> p k s", s=512),
                          in_=dram3(hidb[e, tu])).inc(chh[tu % 2], 16)
            hchan, hload = chh[tu % 2], chh[tu % 2].count
            for t4 in range(4):
                gi = tu * 4 + t4
                for nn in range(2):
                    slot = (t4 * 2 + nn) % 6
                    bank = pf[slot]
                    wait_slot(p.t, pfs[slot])
                    p.t.wait(hchan, hload)
                    p.t.wait(chw, w2_done)
                    for k in range(FCH):
                        hm = p.t.matmul(bank[:, :],
                                        hb[:, k * 512 + t4 * 128: k * 512 + (t4 + 1) * 128],
                                        w2sb[:, k * D + nn * 512: k * D + (nn + 1) * 512],
                                        start=(k == 0), stop=(k == FCH - 1))
                    hm.inc(chmm)
                    p.v.wait(chmm)
                    p.v.wait(chy)
                    p.v.tensor_scalar(out=ybf[:, nn * 512:(nn + 1) * 512],
                                      in0=bank[:, :],
                                      scalar1=gatesb[:, gi * E + e: gi * E + e + 1],
                                      scalar2=None, op0=ALU.mult)
                    pfs[slot] = (p.cv.count, 0)
                p.g.wait(p.cv)
                p.g.dma_start(out=ybuf[gi * 128:(gi + 1) * 128, :], in_=ybf[:, :],
                              accum_op=(ALU.bypass if e == 0 else ALU.add)).inc(chy, 16)

    p.q.wait(chy)
    for b in range(n_b):
        for tt in range(TT):
            gi = b * TT + tt
            p.q.wait(p.cs)
            p.q.wait(p.cv)
            p.q.wait(chst)
            p.q.dma_start(out=yacc[:, :], in_=ybuf[gi * 128:(gi + 1) * 128, :]).inc(chq, 16)
            p.q.dma_start(out=xtile[:, :], in_=xpb[b, tt * 128:(tt + 1) * 128, :]).inc(chq, 16)
            p.q.dma_start(out=x2til[:, :], in_=xs[b, tt * 128:(tt + 1) * 128, :]).inc(chq, 16)
            p.v.wait(chq)
            p.v.tensor_tensor(out=yacc[:, :], in0=yacc[:, :], in1=xtile[:, :],
                              op=ALU.add)
            p.s.wait(p.cv)
            nmu, rcol, nmr = emit_ln_stats(p, yacc[:, :], D, sc, 0, scr)
            p.v.tensor_scalar(out=x2til[:, :], in0=x2til[:, :], scalar1=nmr,
                              scalar2=None, op0=ALU.add, _sd=1)
            p.v.scalar_tensor_tensor(out=x2til[:, :], in0=yacc[:, :], scalar=rcol,
                                     in1=x2til[:, :], op0=ALU.mult, op1=ALU.add, _sd=1)
            p.g.wait(p.cv)
            p.g.dma_start(out=xs4[b, tt * 128:(tt + 1) * 128, :],
                          in_=x2til[:, :]).inc(chst, 16)


def build_l1ffn_b(n_b=4, S=2048):
    D, F, E = 1024, 2048, 8
    NTU = n_b * S // 512
    nc = bass.Bass("TRN2", target_bir_lowering=False)
    xs = nc.dram_tensor("xs", [n_b, S, D], F32, kind="ExternalInput").ap()
    xpb = nc.dram_tensor("xpb", [n_b, S, D], F32, kind="ExternalInput").ap()
    g2 = nc.dram_tensor("g2", [1, D], F32, kind="ExternalInput").ap()
    rw = nc.dram_tensor("rw", [D, E], F32, kind="ExternalInput").ap()
    w1 = nc.dram_tensor("w1", [E, D, F], F32, kind="ExternalInput").ap()
    w3 = nc.dram_tensor("w3", [E, D, F], F32, kind="ExternalInput").ap()
    w2 = nc.dram_tensor("w2", [E, F, D], F32, kind="ExternalInput").ap()
    ident_in = nc.dram_tensor("ident", [128, 128], BF16, kind="ExternalInput").ap()
    xs4 = nc.dram_tensor("xs4", [n_b, S, D], F32, kind="ExternalOutput").ap()

    tfm = nc.dram_tensor("tfm", [n_b, D, S], BF16, kind="Internal").ap()
    hidb = nc.dram_tensor("hidb", [E, NTU, F, 512], BF16, kind="Internal").ap()
    ybuf = nc.dram_tensor("ybuf", [n_b * S, D], F32, kind="Internal").ap()

    es = contextlib.ExitStack()
    with es:
        p = Prog(nc, es)
        chs = _mk_chans(p)
        emit_l1ffn_b(nc, p, chs, xs, xpb, g2, rw, w1, w3, w2, ident_in, xs4, tfm, hidb, ybuf, n_b, S)
        p.g.wait(chs["chst"])
        p.finalize()
    return nc


def _host_l1ffnb_inputs(x3_full, xp_full, P):
    in_maps = []
    for c in range(N_CORES):
        in_maps.append({
            "xs": np.ascontiguousarray(x3_full[c * 4:(c + 1) * 4]),
            "xpb": np.ascontiguousarray(xp_full[c * 4:(c + 1) * 4]),
            "g2": np.ascontiguousarray(P["L1_ffn_g2"][None, :]),
            "rw": np.ascontiguousarray(P["L1_ffn_router_w"]),
            "w1": np.ascontiguousarray(P["L1_ffn_w1"]),
            "w3": np.ascontiguousarray(P["L1_ffn_w3"]),
            "w2": np.ascontiguousarray(P["L1_ffn_w2"]),
            "ident": _ident128(),
        })
    return in_maps



def build_l0attn(n_b=4, S=2048):
    D = 1024
    DI = 2048
    CV = 2304
    DIP = 4384
    NH, P_ = 32, 64
    KCH = D // 128
    TT = S // 128
    ZCH = DI // 128
    XCH = CV // 128

    nc = bass.Bass("TRN2", target_bir_lowering=False)
    xs = nc.dram_tensor("xs", [n_b, S, D], F32, kind="ExternalInput").ap()
    wipT = nc.dram_tensor("wipT", [D, DIP], F32, kind="ExternalInput").ap()
    convw = nc.dram_tensor("convw", [CV, 4], F32, kind="ExternalInput").ap()
    convb = nc.dram_tensor("convb", [CV, 1], F32, kind="ExternalInput").ap()
    dtb = nc.dram_tensor("dtb", [NH, 1], F32, kind="ExternalInput").ap()
    alog = nc.dram_tensor("alog", [NH, 1], F32, kind="ExternalInput").ap()
    dskip = nc.dram_tensor("dskip", [1, NH], BF16, kind="ExternalInput").ap()
    dskrep = nc.dram_tensor("dskrep", [1, NH * P_], BF16, kind="ExternalInput").ap()
    normw = nc.dram_tensor("normw", [DI, 1], F32, kind="ExternalInput").ap()
    woutT = nc.dram_tensor("woutT", [DI, D], F32, kind="ExternalInput").ap()
    lng = nc.dram_tensor("lng", [1, D], BF16, kind="ExternalInput").ap()
    lnb = nc.dram_tensor("lnb", [1, D], BF16, kind="ExternalInput").ap()
    ebank = nc.dram_tensor("ebank", [NH, NH * 128], BF16, kind="ExternalInput").ap()
    tril_ = nc.dram_tensor("tril", [128, 128], BF16, kind="ExternalInput").ap()
    trius_ = nc.dram_tensor("trius", [128, 128], BF16, kind="ExternalInput").ap()
    logtri_ = nc.dram_tensor("logtri", [128, 128], F32, kind="ExternalInput").ap()
    ident_in = nc.dram_tensor("ident", [128, 128], BF16, kind="ExternalInput").ap()
    xs1 = nc.dram_tensor("xs1", [n_b, S, D], F32, kind="ExternalOutput").ap()

    abuf = nc.dram_tensor("abuf", [n_b, S, D], F32, kind="Internal").ap()
    zbuf = nc.dram_tensor("zbuf", [n_b, S, DI], BF16, kind="Internal").ap()
    xbcbuf = nc.dram_tensor("xbcbuf", [n_b, CV, S], BF16, kind="Internal").ap()
    cvbuf = nc.dram_tensor("cvbuf", [n_b, CV, S], BF16, kind="Internal").ap()
    ybuf = nc.dram_tensor("ybuf", [n_b, S, DI], BF16, kind="Internal").ap()

    es = contextlib.ExitStack()
    with es:
        p = Prog(nc, es)
        wsb = p.sbuf("wsb", [128, KCH * 2304], BF16)
        xnfm = p.sbuf("xnfm", [128, KCH * S], BF16)
        xtile = p.sbuf("xtile", [128, D], F32)
        xtm = p.sbuf("xtm", [128, D], BF16)
        scr = p.sbuf("scr", [128, DI], BF16)
        scrf = p.sbuf("scrf", [128, 1024], F32)
        sc = p.sbuf("sc", [128, 16], F32)
        ident = p.sbuf("identS", [128, 128], BF16)
        cwcol = p.sbuf("cwcol", [128, XCH * 4], F32)
        cbcol = p.sbuf("cbcol", [128, XCH], F32)
        cbufs = [p.sbuf(f"cbuf{i}", [128, 515], BF16) for i in range(2)]
        ebk = p.sbuf("ebk", [NH, NH * 128], BF16)
        trilS = p.sbuf("trilS", [128, 128], BF16)
        triuS = p.sbuf("triuS", [128, 128], BF16)
        logtS = p.sbuf("logtS", [128, 128], F32)
        dtbc = p.sbuf("dtbc", [NH, 1], F32)
        acol = p.sbuf("acol", [NH, 1], F32)
        dskb = p.sbuf("dskb", [128, NH], F32)
        dskx = p.sbuf("dskx", [128, NH * P_], BF16)
        dskrow = p.sbuf("dskrow", [1, NH], BF16)
        dskreprow = p.sbuf("dskreprow", [1, NH * P_], BF16)
        onesr = p.sbuf("onesr", [1, 128], BF16)
        dtmix = p.sbuf("dtmix", [32, 2 * S], F32)
        dtmixb = p.sbuf("dtmixb", [32, 2 * S], BF16)
        dtat = p.sbuf("dtat", [128, 64], F32)
        dtatb = p.sbuf("dtatb", [128, 64], BF16)
        eal = p.sbuf("eal", [128, NH], F32)
        ebe = p.sbuf("ebe", [128, NH], F32)
        dteb = p.sbuf("dteb", [128, NH], F32)
        gcol = p.sbuf("gcol", [128, NH], F32)
        afmb = p.sbuf("afmb", [NH, 128], BF16)
        bfm = p.sbuf("bfm", [128, S], BF16)
        cfmS = p.sbuf("cfmS", [128, S], BF16)
        ldxh = p.sbuf("ldxh", [128, 16 * 128], BF16)
        xhtm = p.sbuf("xhtm", [128, 16 * 128], BF16)
        u2bf = p.sbuf("u2bf", [128, 16 * 128], BF16)
        btm = p.sbuf("btm", [128, 128], BF16)
        dmt = p.sbuf("dmt", [128, 4 * 128], F32)
        ssba = p.sbuf("ssba", [128, 16 * 128], BF16)
        expd = p.sbuf("expd", [128, 4 * 128], F32)
        gsbb = p.sbuf("gsbb", [128, 128], BF16)
        ldtc = p.sbuf("ldtc", [128, NH], F32)
        lcol = p.sbuf("lcol", [128, NH], F32)
        hstate = p.sbuf("hstate", [128, NH * P_], F32)
        hstb = p.sbuf("hstb", [128, NH * P_], BF16)
        t1sb = p.sbuf("t1sb", [128, 2 * P_], F32)
        t2sb = p.sbuf("t2sb", [128, 2 * P_], F32)
        ytm = p.sbuf("ytm", [128, DI], BF16)
        nwcol = p.sbuf("nwcol", [128, ZCH], F32)
        lngb = p.sbuf("lngb", [128, 2 * D], BF16)
        ytile = p.sbuf("ytile", [128, DI], BF16)
        sztile = p.sbuf("sztile", [128, DI], BF16)
        ygb = p.sbuf("ygb", [128, DI], BF16)
        ygfm = p.sbuf("ygfm", [128, DI], BF16)
        rtile = p.sbuf("rtile", [128, D], F32)
        atile = p.sbuf("atile", [128, D], F32)
        lngrow = p.sbuf("lngrow", [1, D], BF16)
        lnbrow = p.sbuf("lnbrow", [1, D], BF16)

        pfp = [p.psum(f"pf{i}", [128, 512], F32) for i in range(4)]
        ptr = [p.psum(f"ptr{i}", [128, 128], BF16) for i in range(2)]
        pss = [p.psum(f"ps{i}", [128, 128], F32) for i in range(2)]

        chq = p.chan("chq")
        chg = p.chan("chg")
        chw = p.chan("chw")
        chc = [p.chan("chc0"), p.chan("chc1")]
        chx = p.chan("chx")
        chbc = p.chan("chbc")
        chmm = p.chan("chmm")
        chst = p.chan("chst")

        pfs = [(0, 0)] * 4
        ptrs = [(0, 0)] * 2
        psss = [(0, 0)] * 2

        def wait_slot(eng, mark):
            eng.wait(p.cv, mark[0])
            eng.wait(p.cs, mark[1])

        p.g.dma_start(out=ident[:, :], in_=ident_in[:, :]).inc(chw, 16)
        p.g.dma_start(out=ebk[:, :], in_=ebank[:, :]).inc(chw, 16)
        p.g.dma_start(out=trilS[:, :], in_=tril_[:, :]).inc(chw, 16)
        p.g.dma_start(out=triuS[:, :], in_=trius_[:, :]).inc(chw, 16)
        p.g.dma_start(out=logtS[:, :], in_=logtri_[:, :]).inc(chw, 16)
        p.g.dma_start(out=cwcol[:, :], in_=dram3(convw)).inc(chw, 16)
        p.g.dma_start(out=cbcol[:, :].rearrange("p (g w) -> p g w", w=1),
                      in_=dram3(convb), allow_slow_non_contiguous=True).inc(chw, 16)
        p.g.dma_start(out=dtbc[0:NH, 0:1], in_=dtb[:, :]).inc(chw, 16)
        p.g.dma_start(out=acol[0:NH, 0:1], in_=alog[:, :]).inc(chw, 16)
        p.g.dma_start(out=dskrow[0:1, :], in_=dskip[:, :]).inc(chw, 16)
        p.g.dma_start(out=dskreprow[0:1, :], in_=dskrep[:, :]).inc(chw, 16)
        p.g.dma_start(out=nwcol[:, :].rearrange("p (g w) -> p g w", w=1),
                      in_=dram3(normw), allow_slow_non_contiguous=True).inc(chw, 16)
        p.g.dma_start(out=lngrow[0:1, :], in_=lng[:, :]).inc(chw, 16)
        p.g.dma_start(out=lnbrow[0:1, :], in_=lnb[:, :]).inc(chw, 16)
        p.v.memset(onesr[0:1, :], 1.0)
        p.v.wait(chw)
        p.v.memset(hstate[:, :], 0.0)
        p.s.wait(chw)
        p.s.activation(out=acol[0:NH, 0:1], in_=acol[0:NH, 0:1], func=AF.Exp)
        p.v.wait(p.cs)
        p.v.tensor_scalar(out=acol[0:NH, 0:1], in0=acol[0:NH, 0:1], scalar1=-1.0,
                          scalar2=None, op0=ALU.mult)
        cst_done = p.cv.count
        p.t.wait(chw)
        p.t.wait(p.cv, cst_done)
        p.t.matmul(pfp[0][:, 0:NH], onesr[0:1, 0:128], dskrow[0:1, 0:NH],
                   start=True, stop=True)
        for q4 in range(4):
            p.t.wait(p.cv)
            hm4 = p.t.matmul(pfp[3][:, :], onesr[0:1, 0:128],
                             dskreprow[0:1, q4 * 512:(q4 + 1) * 512],
                             start=True, stop=True)
            hm4.inc(chmm)
            p.v.wait(chmm)
            p.v.tensor_copy(out=dskx[:, q4 * 512:(q4 + 1) * 512], in_=pfp[3][:, :])
        p.t.matmul(pfp[1][:, :], onesr[0:1, 0:128], lngrow[0:1, 0:512],
                   start=True, stop=True)
        hm = p.t.matmul(pfp[2][:, :], onesr[0:1, 0:128], lnbrow[0:1, 0:512],
                        start=True, stop=True)
        hm.inc(chmm)
        p.v.wait(chmm)
        p.v.tensor_copy(out=dskb[:, :], in_=pfp[0][:, 0:NH])
        p.v.tensor_copy(out=lngb[:, 0:512], in_=pfp[1][:, :])
        p.v.tensor_copy(out=lngb[:, D:D + 512], in_=pfp[2][:, :])
        bcast_done = p.cv.count
        pfs[0] = (bcast_done, 0)
        pfs[1] = (bcast_done, 0)
        pfs[2] = (bcast_done, 0)
        p.t.wait(p.cv, bcast_done)
        p.t.matmul(pfp[1][:, :], onesr[0:1, 0:128], lngrow[0:1, 512:1024],
                   start=True, stop=True)
        hm = p.t.matmul(pfp[2][:, :], onesr[0:1, 0:128], lnbrow[0:1, 512:1024],
                        start=True, stop=True)
        hm.inc(chmm)
        p.v.wait(chmm)
        p.v.tensor_copy(out=lngb[:, 512:1024], in_=pfp[1][:, :])
        p.v.tensor_copy(out=lngb[:, D + 512:2 * D], in_=pfp[2][:, :])
        bcast_done = p.cv.count
        pfs[1] = (bcast_done, 0)
        pfs[2] = (bcast_done, 0)

        for b in range(n_b):
            for tt in range(TT):
                p.q.wait(p.cs)
                p.q.wait(p.cv)
                p.q.wait(chg)
                p.q.dma_start(out=xtile[:, :], in_=xs[b, tt * 128:(tt + 1) * 128, :]).inc(chq, 16)
                p.s.wait(chq)
                p.v.wait(chq)
                nmu, rcol, nmr = emit_ln_stats(p, xtile[:, :], D, sc, 0, scr)
                p.v.wait(chg)
                p.v.wait(chmm)
                p.v.tensor_scalar(out=atile[:, 0:D], in0=xtile[:, :], scalar1=rcol,
                                  scalar2=nmr, op0=ALU.mult, op1=ALU.add, _sd=1)
                p.v.tensor_copy(out=xtm[:, :], in_=atile[:, 0:D], _sd=1)
                ln_done = p.cv.count
                p.g.wait(p.cv, ln_done)
                p.g.dma_start(out=abuf[b, tt * 128:(tt + 1) * 128, :],
                              in_=atile[:, 0:D]).inc(chg, 16)
                for k in range(KCH):
                    slot = k % 2
                    bank = ptr[slot]
                    wait_slot(p.t, ptrs[slot])
                    p.t.wait(p.cv, ln_done)
                    p.t.transpose(bank[:, :], xtm[:, k * 128:(k + 1) * 128],
                                  ident[:, :]).inc(chmm)
                    p.v.wait(chmm)
                    p.v.tensor_copy(out=xnfm[:, k * S + tt * 128: k * S + (tt + 1) * 128],
                                    in_=bank[:, :])
                    ptrs[slot] = (p.cv.count, 0)
            xn_done = p.cv.count

            p.g.wait(chmm)
            for k in range(KCH):
                p.g.dma_start(out=wsb[:, k * DI:(k + 1) * DI],
                              in_=wipT[k * 128:(k + 1) * 128, 0:DI]).inc(chg, 16)
            wl = chg.count
            for tt in range(TT):
                for nn in range(4):
                    slot = nn % 4
                    bank = pfp[slot]
                    wait_slot(p.t, pfs[slot])
                    p.t.wait(chg, wl)
                    p.t.wait(p.cv, xn_done)
                    for k in range(KCH):
                        hm = p.t.matmul(bank[:, :],
                                        xnfm[:, k * S + tt * 128: k * S + (tt + 1) * 128],
                                        wsb[:, k * DI + nn * 512: k * DI + (nn + 1) * 512],
                                        start=(k == 0), stop=(k == KCH - 1))
                    hm.inc(chmm)
                    p.s.wait(chmm)
                    p.s.wait(chg)
                    p.s.activation(out=sztile[:, nn * 512:(nn + 1) * 512],
                                   in_=bank[:, :], func=AF.Silu)
                    pfs[slot] = (0, p.cs.count)
                p.g.wait(p.cs)
                p.g.dma_start(out=zbuf[b, tt * 128:(tt + 1) * 128, :],
                              in_=sztile[:, :]).inc(chg, 16)
            p.g.wait(chmm)
            for k in range(KCH):
                p.g.dma_start(out=wsb[:, k * 2304:(k + 1) * 2304],
                              in_=wipT[k * 128:(k + 1) * 128, DI:DI + 2304]).inc(chg, 16)
            wl = chg.count
            for ec in range(XCH):
                for t5 in range(4):
                    slot = t5 % 4
                    bank = pfp[slot]
                    wait_slot(p.t, pfs[slot])
                    p.t.wait(chg, wl)
                    p.t.wait(p.cv, xn_done)
                    for k in range(KCH):
                        hm = p.t.matmul(bank[:, :],
                                        wsb[:, k * 2304 + ec * 128: k * 2304 + (ec + 1) * 128],
                                        xnfm[:, k * S + t5 * 512: k * S + (t5 + 1) * 512],
                                        start=(k == 0), stop=(k == KCH - 1))
                    hm.inc(chmm)
                    p.v.wait(chmm)
                    p.v.wait(chg)
                    p.v.tensor_copy(out=scr[:, t5 * 512:(t5 + 1) * 512], in_=bank[:, :])
                    pfs[slot] = (p.cv.count, 0)
                p.g.wait(p.cv)
                p.g.dma_start(out=xbcbuf[b, ec * 128:(ec + 1) * 128, :],
                              in_=scr[:, 0:S]).inc(chg, 16)
            p.g.wait(chmm)
            for k in range(KCH):
                p.g.dma_start(out=wsb[:, k * 32:(k + 1) * 32],
                              in_=wipT[k * 128:(k + 1) * 128, DI + 2304:DIP]).inc(chg, 16)
            wl = chg.count
            for t5 in range(4):
                slot = t5 % 2
                bank = pfp[slot]
                wait_slot(p.t, pfs[slot])
                p.t.wait(chg, wl)
                p.t.wait(p.cv, xn_done)
                for k in range(KCH):
                    hm = p.t.matmul(bank[0:32, 0:512],
                                    wsb[:, k * 32:(k + 1) * 32],
                                    xnfm[:, k * S + t5 * 512: k * S + (t5 + 1) * 512],
                                    start=(k == 0), stop=(k == KCH - 1))
                hm.inc(chmm)
                p.v.wait(chmm)
                p.v.tensor_copy(out=dtmix[0:32, t5 * 512:(t5 + 1) * 512],
                                in_=bank[0:32, 0:512])
                pfs[slot] = (p.cv.count, 0)

            for ec in range(XCH):
                for t5 in range(4):
                    cb = cbufs[t5 % 2]
                    ch_ = chc[t5 % 2]
                    p.q.wait(p.cs)
                    p.q.wait(p.cv)
                    if t5 == 0:
                        p.v.wait(ch_)
                        p.v.memset(cb[:, 0:3], 0.0)
                        pad_done = p.cv.count
                        p.q.wait(p.cv, pad_done)
                        p.q.dma_start(out=cb[:, 3:515],
                                      in_=xbcbuf[b, ec * 128:(ec + 1) * 128, 0:512]).inc(ch_, 16)
                    else:
                        p.q.dma_start(out=cb[:, 0:515],
                                      in_=xbcbuf[b, ec * 128:(ec + 1) * 128,
                                                 t5 * 512 - 3: t5 * 512 + 512]).inc(ch_, 16)
                    p.v.wait(ch_)
                    cw = cwcol[:, ec * 4: ec * 4 + 4]
                    p.v.tensor_scalar(out=scrf[:, 0:512], in0=cb[:, 0:512],
                                      scalar1=cw[:, 0:1], scalar2=None, op0=ALU.mult)
                    p.v.scalar_tensor_tensor(out=scrf[:, 0:512], in0=cb[:, 1:513],
                                             scalar=cw[:, 1:2], in1=scrf[:, 0:512],
                                             op0=ALU.mult, op1=ALU.add, _sd=1)
                    p.v.scalar_tensor_tensor(out=scrf[:, 0:512], in0=cb[:, 2:514],
                                             scalar=cw[:, 2:3], in1=scrf[:, 0:512],
                                             op0=ALU.mult, op1=ALU.add, _sd=1)
                    p.v.scalar_tensor_tensor(out=scrf[:, 0:512], in0=cb[:, 3:515],
                                             scalar=cw[:, 3:4], in1=scrf[:, 0:512],
                                             op0=ALU.mult, op1=ALU.add, _sd=1)
                    p.s.wait(p.cv)
                    p.s.wait(chg)
                    p.s.activation(out=scr[:, t5 * 512:(t5 + 1) * 512], in_=scrf[:, 0:512],
                                   func=AF.Silu, bias=cbcol[:, ec:ec + 1], scale=1.0)
                p.g.wait(p.cs)
                p.g.dma_start(out=cvbuf[b, ec * 128:(ec + 1) * 128, :],
                              in_=scr[:, 0:S]).inc(chg, 16)

            p.v.wait(p.cs)
            p.v.tensor_scalar(out=dtmix[0:32, 0:S], in0=dtmix[0:32, 0:S],
                              scalar1=dtbc[0:32, 0:1], scalar2=None, op0=ALU.add, _sd=1)
            p.s.wait(p.cv)
            p.s.activation(out=dtmix[0:32, S:2 * S], in_=dtmix[0:32, 0:S],
                           func=AF.Abs)
            p.s.activation(out=dtmix[0:32, S:2 * S], in_=dtmix[0:32, S:2 * S],
                           func=AF.Exp, scale=-1.0, _sd=1)
            p.v.wait(p.cs)
            p.v.tensor_scalar(out=dtmix[0:32, S:2 * S], in0=dtmix[0:32, S:2 * S],
                              scalar1=1.0, scalar2=None, op0=ALU.add)
            p.s.wait(p.cv)
            p.s.activation(out=dtmix[0:32, S:2 * S], in_=dtmix[0:32, S:2 * S],
                           func=AF.Ln)
            p.v.wait(p.cs)
            p.v.scalar_tensor_tensor(out=dtmix[0:32, 0:S], in0=dtmix[0:32, 0:S],
                                     scalar=0.0, in1=dtmix[0:32, S:2 * S], op0=ALU.max,
                                     op1=ALU.add, _sd=1)
            p.v.tensor_scalar(out=dtmix[0:32, S:2 * S], in0=dtmix[0:32, 0:S],
                              scalar1=acol[0:32, 0:1], scalar2=None, op0=ALU.mult, _sd=1)
            p.v.tensor_copy(out=dtmixb[:, :], in_=dtmix[:, :], _sd=1)
            dt_done = p.cv.count
            p.q.wait(chmm)
            p.q.wait(chg)
            p.q.dma_start(out=bfm[:, :], in_=cvbuf[b, 2048:2176, :]).inc(chbc, 16)
            p.q.dma_start(out=cfmS[:, :], in_=cvbuf[b, 2176:2304, :]).inc(chbc, 16)
            bc_load = chbc.count
            hb_mark = p.cg.count
            sst_done = p.cg.count
            dmt_done = p.cs.count
            ssb_rd_all = chmm.count

            for c in range(16):
                csl = slice(c * 128, (c + 1) * 128)
                wait_slot(p.t, ptrs[0])
                wait_slot(p.t, ptrs[1])
                p.t.wait(p.cv, dt_done)
                p.t.transpose(ptr[0][:, 0:32], dtmixb[0:32, csl],
                              ident[0:32, 0:32]).inc(chmm)
                p.t.transpose(ptr[1][:, 0:32],
                              dtmixb[0:32, S + c * 128: S + (c + 1) * 128],
                              ident[0:32, 0:32]).inc(chmm)
                p.v.wait(chmm)
                p.v.tensor_copy(out=dtat[:, 0:32], in_=ptr[0][:, 0:32])
                p.v.tensor_copy(out=dtat[:, 32:64], in_=ptr[1][:, 0:32])
                p.v.tensor_copy(out=dtatb[:, :], in_=dtat[:, :], _sd=1)
                ptrs[0] = (p.cv.count, 0)
                ptrs[1] = (p.cv.count, 0)
                dtt_done = p.cv.count
                slot = 0
                bank = pss[slot]
                wait_slot(p.t, psss[slot])
                p.t.wait(p.cv, dtt_done)
                p.t.matmul(bank[:, 0:32], trilS[:, :], dtatb[:, 32:64],
                           start=True, stop=False)
                hm = p.t.matmul(bank[:, 32:64], triuS[:, :], dtatb[:, 32:64],
                                start=False, stop=True)
                hm.inc(chmm)
                aslot = 0
                abank = pfp[aslot]
                wait_slot(p.t, pfs[aslot])
                hm = p.t.matmul(abank[0:32, 0:128], dtatb[:, 32:64], trilS[:, :],
                                start=True, stop=True)
                hm.inc(chmm)
                p.v.wait(chmm)
                p.v.tensor_copy(out=dtat[:, 32:64], in_=bank[:, 0:32])
                p.s.wait(chmm)
                p.s.activation(out=eal[:, :], in_=bank[:, 0:32], func=AF.Exp)
                p.s.activation(out=ebe[:, :], in_=bank[:, 32:64], func=AF.Exp)
                p.v.wait(p.cs)
                p.v.tensor_copy(out=afmb[0:32, :], in_=abank[0:32, 0:128])
                pfs[aslot] = (p.cv.count, 0)
                p.v.tensor_tensor(out=dteb[:, :], in0=dtat[:, 0:32], in1=ebe[:, :],
                                  op=ALU.mult)
                p.v.tensor_tensor(out=gcol[:, :], in0=ebe[:, :], in1=eal[:, :],
                                  op=ALU.mult)
                p.s.wait(p.cv)
                p.s.activation(out=ldtc[:, :], in_=dtat[:, 0:32], func=AF.Ln)
                p.v.wait(p.cs)
                p.v.tensor_tensor(out=lcol[:, :], in0=ldtc[:, :], in1=dtat[:, 32:64],
                                  op=ALU.subtract)
                psss[slot] = (p.cv.count, p.cs.count)
                sh_done = p.cv.count
                p.q.wait(chmm)
                p.q.dma_start(out=ldxh[:, :].rearrange("p (g w) -> p g w", w=128),
                              in_=dram3(cvbuf[b, 0:2048, :])[:, :, csl]).inc(chx, 16)
                xh_load = chx.count
                for pr in range(16):
                    slot = pr % 2
                    bank = ptr[slot]
                    wait_slot(p.t, ptrs[slot])
                    p.t.wait(chx, xh_load)
                    p.t.transpose(bank[:, :], ldxh[:, pr * 128:(pr + 1) * 128],
                                  ident[:, :]).inc(chmm)
                    p.v.wait(chmm)
                    p.v.tensor_copy(out=xhtm[:, pr * 128:(pr + 1) * 128], in_=bank[:, :])
                    ptrs[slot] = (p.cv.count, 0)
                    for hh in range(2):
                        h = pr * 2 + hh
                        dsl = slice(pr * 128 + hh * 64, pr * 128 + (hh + 1) * 64)
                        p.s.wait(p.cv)
                        p.s.activation(out=u2bf[:, dsl], in_=xhtm[:, dsl],
                                       func=AF.Identity, scale=dteb[:, h:h + 1])
                u_done = p.cv.count
                u_done_s = p.cs.count
                slot = 0
                bank = ptr[slot]
                wait_slot(p.t, ptrs[slot])
                p.t.wait(chbc, bc_load)
                p.t.transpose(bank[:, :], bfm[:, csl], ident[:, :]).inc(chmm)
                p.v.wait(chmm)
                p.v.tensor_copy(out=btm[:, :], in_=bank[:, :])
                ptrs[slot] = (p.cv.count, 0)
                btm_done = p.cv.count
                slot = 1
                gbank = pss[slot]
                wait_slot(p.t, psss[slot])
                p.t.wait(chbc, bc_load)
                p.t.matmul(gbank[:, :], bfm[:, csl], cfmS[:, csl],
                           start=True, stop=True).inc(chmm)
                gt_mm = chmm.count
                p.v.wait(chmm, gt_mm)
                p.v.tensor_copy(out=gsbb[:, :], in_=gbank[:, :])
                psss[slot] = (p.cv.count, 0)
                gsb_mark = p.cv.count
                sstm = {}
                dmtp = [dmt_done] * 4
                sstp4 = [sst_done] * 4
                t1_rd = [0, 0]
                t2_rd = [0, 0]
                for hbase in (0, 16):
                  for h in range(hbase, hbase + 16):
                    slot = 2 + (h % 2)
                    par4 = h % 4
                    psl4 = slice(par4 * 128, (par4 + 1) * 128)
                    dbank = pfp[slot]
                    wait_slot(p.t, pfs[slot])
                    p.t.wait(p.cv, sh_done)
                    p.t.matmul(dbank[:, 0:128], ebk[:, h * 128:(h + 1) * 128],
                               afmb[0:32, :], start=True, stop=True).inc(chmm)
                    dmm = chmm.count
                    p.v.wait(chmm, dmm)
                    p.v.wait(p.cs, dmtp[par4])
                    p.v.scalar_tensor_tensor(out=dmt[:, psl4], in0=dbank[:, 0:128],
                                             scalar=lcol[:, h:h + 1],
                                             in1=logtS[:, :], op0=ALU.add,
                                             op1=ALU.add, _sd=0)
                    dm_done = p.cv.count
                    pfs[slot] = (dm_done, 0)
                    p.s.wait(p.cv, dm_done)
                    p.s.wait(p.cg, sstp4[par4])
                    p.s.activation(out=expd[:, psl4], in_=dmt[:, psl4], func=AF.Exp)
                    dmtp[par4] = p.cs.count
                    p.g.wait(p.cs, p.cs.count)
                    if h == hbase:
                        p.g.wait(chmm, ssb_rd_all)
                        p.g.wait(p.cv, gsb_mark)
                    p.g.tensor_tensor(out=ssba[:, (h - hbase) * 128:(h - hbase + 1) * 128],
                                      in0=expd[:, psl4], in1=gsbb[:, :],
                                      op=ALU.mult, _sd=0)
                    sstp4[par4] = p.cg.count
                    sstm[h] = p.cg.count
                  for h in range(hbase, hbase + 16):
                    par = h % 2
                    hp = slice((h // 2) * 128 + (h % 2) * 64, (h // 2) * 128 + (h % 2 + 1) * 64)
                    p6 = slice(par * P_, (par + 1) * P_)
                    ybank = pfp[par]
                    wait_slot(p.t, pfs[par])
                    p.t.wait(p.cg, sstm[h])
                    p.t.matmul(ybank[:, 0:P_],
                               ssba[:, (h - hbase) * 128:(h - hbase + 1) * 128],
                               xhtm[:, hp], start=True, stop=True).inc(chmm)
                    yi_mm = chmm.count
                    if c > 0:
                        ibank = pss[par]
                        wait_slot(p.t, psss[par])
                        p.t.wait(p.cg, hb_mark)
                        p.t.matmul(ibank[:, 0:P_], cfmS[:, csl],
                                   hstb[:, h * P_:(h + 1) * P_],
                                   start=True, stop=True).inc(chmm)
                        yint_mm = chmm.count
                        p.s.wait(chmm, yint_mm)
                        p.s.wait(p.cg, t1_rd[par])
                        p.s.activation(out=t1sb[:, p6], in_=ibank[:, 0:P_],
                                       func=AF.Identity, scale=eal[:, h:h + 1])
                        psss[par] = (0, p.cs.count)
                        t1_done = p.cs.count
                        p.g.wait(p.cs, t1_done)
                        p.g.wait(p.cv, max(u_done, t2_rd[par]))
                        p.g.tensor_tensor(out=t2sb[:, p6], in0=xhtm[:, hp],
                                          in1=dskx[:, h * P_:(h + 1) * P_],
                                          op=ALU.mult, _sd=0)
                        p.g.tensor_tensor(out=t2sb[:, p6], in0=t2sb[:, p6],
                                          in1=t1sb[:, p6], op=ALU.add, _sd=1)
                        t1_rd[par] = p.cg.count
                        t2_done = p.cg.count
                    else:
                        p.g.wait(p.cv, max(u_done, t2_rd[par]))
                        p.g.tensor_tensor(out=t2sb[:, p6], in0=xhtm[:, hp],
                                          in1=dskx[:, h * P_:(h + 1) * P_],
                                          op=ALU.mult, _sd=0)
                        t2_done = p.cg.count
                    sbank = pfp[2 + par]
                    wait_slot(p.t, pfs[2 + par])
                    p.t.wait(p.cv, btm_done)
                    p.t.wait(p.cs, u_done_s)
                    p.t.matmul(sbank[:, 0:P_], btm[:, :], u2bf[:, hp],
                               start=True, stop=True).inc(chmm)
                    hst_mm = chmm.count
                    p.v.wait(chmm, yi_mm)
                    p.v.wait(p.cg, t2_done)
                    if h == 0:
                        p.v.wait(chg)
                    p.v.tensor_tensor(out=ytm[:, h * P_:(h + 1) * P_],
                                      in0=ybank[:, 0:P_], in1=t2sb[:, p6],
                                      op=ALU.add, _sd=0)
                    t2_rd[par] = p.cv.count
                    pfs[par] = (p.cv.count, 0)
                    p.v.wait(chmm, hst_mm)
                    if c > 0:
                        p.v.scalar_tensor_tensor(out=hstate[:, h * P_:(h + 1) * P_],
                                                 in0=hstate[:, h * P_:(h + 1) * P_],
                                                 scalar=gcol[:, h:h + 1],
                                                 in1=sbank[:, 0:P_], op0=ALU.mult,
                                                 op1=ALU.add, _sd=0)
                    else:
                        p.v.tensor_copy(out=hstate[:, h * P_:(h + 1) * P_],
                                        in_=sbank[:, 0:P_], _sd=0)
                    pfs[2 + par] = (p.cv.count, 0)
                    hs_done = p.cv.count
                    p.g.wait(p.cv, hs_done)
                    p.g.tensor_copy(out=hstb[:, h * P_:(h + 1) * P_],
                                    in_=hstate[:, h * P_:(h + 1) * P_], _sd=0)
                  ssb_rd_all = chmm.count
                hb_mark = p.cg.count
                sst_done = p.cg.count
                dmt_done = p.cs.count
                p.v.wait(p.cg)
                p.g.wait(p.cv)
                p.g.dma_start(out=ybuf[b, csl, :], in_=ytm[:, :]).inc(chg, 16)
            p.v.wait(chmm)
            p.v.memset(hstate[:, :], 0.0)

        p.g.wait(chmm)
        for k in range(ZCH):
            p.g.dma_start(out=wsb[:, k * D:(k + 1) * D],
                          in_=woutT[k * 128:(k + 1) * 128, :]).inc(chg, 16)
        wl = chg.count
        p.v.wait(chg, wl)
        for k in range(ZCH):
            p.v.tensor_scalar(out=wsb[:, k * D:(k + 1) * D], in0=wsb[:, k * D:(k + 1) * D],
                              scalar1=nwcol[:, k:k + 1], scalar2=None, op0=ALU.mult)
        wout_done = p.cv.count
        for b in range(n_b):
            for tt in range(TT):
                tsl = slice(tt * 128, (tt + 1) * 128)
                p.q.wait(p.cs)
                p.q.wait(p.cv)
                p.q.wait(chst)
                p.q.dma_start(out=ytile[:, :], in_=ybuf[b, tsl, :]).inc(chq, 16)
                p.q.dma_start(out=sztile[:, :], in_=zbuf[b, tsl, :]).inc(chq, 16)
                p.q.dma_start(out=atile[:, :], in_=abuf[b, tsl, :]).inc(chq, 16)
                p.q.dma_start(out=xtile[:, :], in_=xs[b, tsl, :]).inc(chq, 16)
                p.v.wait(chq)
                p.v.wait(chmm)
                p.v.tensor_tensor(out=ytile[:, :], in0=ytile[:, :], in1=sztile[:, :],
                                  op=ALU.mult)
                p.s.wait(p.cv)
                p.s.activation(out=scr[:, 0:DI], in_=ytile[:, :], func=AF.Square,
                               accum_out=sc[:, 8:9])
                p.v.wait(p.cs)
                p.v.tensor_scalar(out=sc[:, 9:10], in0=sc[:, 8:9], scalar1=1.0 / DI,
                                  scalar2=EPS, op0=ALU.mult, op1=ALU.add)
                p.s.wait(p.cv)
                p.s.activation(out=sc[:, 10:11], in_=sc[:, 9:10], func=AF.Sqrt)
                p.v.wait(p.cs)
                p.v.reciprocal(out=sc[:, 10:11], in_=sc[:, 10:11])
                p.v.tensor_scalar(out=ygb[:, :], in0=ytile[:, :], scalar1=sc[:, 10:11],
                                  scalar2=None, op0=ALU.mult, _sd=1)
                yg_done = p.cv.count
                for k in range(ZCH):
                    slot = k % 2
                    bank = ptr[slot]
                    wait_slot(p.t, ptrs[slot])
                    p.t.wait(p.cv, yg_done)
                    p.t.wait(p.cv, wout_done)
                    p.t.transpose(bank[:, :], ygb[:, k * 128:(k + 1) * 128],
                                  ident[:, :]).inc(chmm)
                    p.v.wait(chmm)
                    p.v.tensor_copy(out=ygfm[:, k * 128:(k + 1) * 128], in_=bank[:, :])
                    ptrs[slot] = (p.cv.count, 0)
                ygfm_done = p.cv.count
                for nn in range(2):
                    slot = nn % 2
                    bank = pfp[slot]
                    wait_slot(p.t, pfs[slot])
                    p.t.wait(p.cv, ygfm_done)
                    for k in range(ZCH):
                        hm = p.t.matmul(bank[:, :], ygfm[:, k * 128:(k + 1) * 128],
                                        wsb[:, k * D + nn * 512: k * D + (nn + 1) * 512],
                                        start=(k == 0), stop=(k == ZCH - 1))
                    hm.inc(chmm)
                    p.v.wait(chmm)
                    p.v.tensor_copy(out=rtile[:, nn * 512:(nn + 1) * 512], in_=bank[:, :])
                    pfs[slot] = (p.cv.count, 0)
                p.s.wait(p.cv)
                nmu, rcol, nmr = emit_ln_stats(p, rtile[:, :], D, sc, 0, scr)
                p.v.tensor_scalar(out=rtile[:, :], in0=rtile[:, :], scalar1=rcol,
                                  scalar2=nmr, op0=ALU.mult, op1=ALU.add, _sd=1)
                p.v.tensor_tensor(out=rtile[:, :], in0=rtile[:, :], in1=lngb[:, 0:D],
                                  op=ALU.mult, _sd=1)
                p.v.tensor_tensor(out=rtile[:, :], in0=rtile[:, :], in1=lngb[:, D:2 * D],
                                  op=ALU.add, _sd=1)
                p.v.tensor_tensor(out=rtile[:, :], in0=rtile[:, :], in1=atile[:, :],
                                  op=ALU.add, _sd=1)
                p.s.wait(p.cv)
                nmu, rcol, nmr = emit_ln_stats(p, rtile[:, :], D, sc, 8, scr)
                p.v.tensor_scalar(out=xtile[:, :], in0=xtile[:, :], scalar1=nmr,
                                  scalar2=None, op0=ALU.add, _sd=1)
                p.v.scalar_tensor_tensor(out=xtile[:, :], in0=rtile[:, :], scalar=rcol,
                                         in1=xtile[:, :], op0=ALU.mult, op1=ALU.add, _sd=1)
                p.g.wait(p.cv)
                p.g.dma_start(out=xs1[b, tsl, :], in_=xtile[:, :]).inc(chst, 16)

        p.g.wait(chst)
        p.finalize()
    return nc


def _host_l0attn_inputs(x0_full, P):
    NH = 32
    wipT = np.ascontiguousarray(P["L0_attn_in_proj_w"].T)
    woutT = np.ascontiguousarray(P["L0_attn_out_w"].T)
    ebank = np.zeros((NH, NH * 128), ml_dtypes.bfloat16)
    for h in range(NH):
        ebank[h, h * 128:(h + 1) * 128] = 1.0
    s_ = np.arange(128)[:, None]
    t_ = np.arange(128)[None, :]
    tril = (s_ <= t_).astype(ml_dtypes.bfloat16)
    trius = (s_ > t_).astype(ml_dtypes.bfloat16)
    logtri = np.where(s_ <= t_, 0.0, -1e30).astype(np.float32)
    in_maps = []
    for c in range(N_CORES):
        in_maps.append({
            "xs": np.ascontiguousarray(x0_full[c * 4:(c + 1) * 4]),
            "wipT": wipT,
            "convw": np.ascontiguousarray(P["L0_attn_conv_w"]),
            "convb": np.ascontiguousarray(P["L0_attn_conv_b"][:, None]),
            "dtb": np.ascontiguousarray(P["L0_attn_dt_bias"][:, None]),
            "alog": np.ascontiguousarray(P["L0_attn_A_log"][:, None]),
            "dskip": np.ascontiguousarray(P["L0_attn_D_skip"][None, :]).astype(ml_dtypes.bfloat16),
            "dskrep": np.ascontiguousarray(np.repeat(P["L0_attn_D_skip"], 64)[None, :]).astype(ml_dtypes.bfloat16),
            "normw": np.ascontiguousarray(P["L0_attn_norm_w"][:, None]),
            "woutT": woutT,
            "lng": np.ascontiguousarray(P["L0_attn_ln_g"][None, :]).astype(ml_dtypes.bfloat16),
            "lnb": np.ascontiguousarray(P["L0_attn_ln_b"][None, :]).astype(ml_dtypes.bfloat16),
            "ebank": ebank, "tril": tril, "trius": trius, "logtri": logtri,
            "ident": _ident128(),
        })
    return in_maps




def build_l1all(n_b=4, S=2048):
    D = 1024
    E, F = 8, 2048
    NTU = n_b * S // 512

    nc = bass.Bass("TRN2", target_bir_lowering=False)
    xs2 = nc.dram_tensor("xs2", [n_b, S, D], F32, kind="ExternalInput").ap()
    wqkv = nc.dram_tensor("wqkv", [D, 3 * D], F32, kind="ExternalInput").ap()
    proj = nc.dram_tensor("proj", [16, 64, 256], F32, kind="ExternalInput").ap()
    wout = nc.dram_tensor("wout", [D, D], F32, kind="ExternalInput").ap()
    bout = nc.dram_tensor("bout", [1, D], F32, kind="ExternalInput").ap()
    g1 = nc.dram_tensor("g1", [1, D], F32, kind="ExternalInput").ap()
    wdown = nc.dram_tensor("wdown", [D, 512], F32, kind="ExternalInput").ap()
    wqkvc = nc.dram_tensor("wqkvc", [512, 3 * 512], F32, kind="ExternalInput").ap()
    wup = nc.dram_tensor("wup", [512, D], F32, kind="ExternalInput").ap()
    cosr = nc.dram_tensor("cosr", [S, 256], BF16, kind="ExternalInput").ap()
    sinr = nc.dram_tensor("sinr", [S, 256], BF16, kind="ExternalInput").ap()
    cmask = nc.dram_tensor("cmask", [128, 4 * 512], F32, kind="ExternalInput").ap()
    g2 = nc.dram_tensor("g2", [1, D], F32, kind="ExternalInput").ap()
    rw = nc.dram_tensor("rw", [D, E], F32, kind="ExternalInput").ap()
    w1 = nc.dram_tensor("w1", [E, D, F], F32, kind="ExternalInput").ap()
    w3 = nc.dram_tensor("w3", [E, D, F], F32, kind="ExternalInput").ap()
    w2 = nc.dram_tensor("w2", [E, F, D], F32, kind="ExternalInput").ap()
    ident_in = nc.dram_tensor("ident", [128, 128], BF16, kind="ExternalInput").ap()
    xs4 = nc.dram_tensor("xs4", [n_b, S, D], F32, kind="ExternalOutput").ap()

    x3buf = nc.dram_tensor("x3buf", [n_b, S, D], F32, kind="Internal").ap()
    xpbuf = nc.dram_tensor("xpbuf", [n_b, S, D], F32, kind="Internal").ap()
    tfm = nc.dram_tensor("tfm", [n_b, D, S], BF16, kind="Internal").ap()
    hidb = nc.dram_tensor("hidb", [E, NTU, F, 512], BF16, kind="Internal").ap()
    ybuf = nc.dram_tensor("ybuf", [n_b * S, D], F32, kind="Internal").ap()

    es = contextlib.ExitStack()
    with es:
        p = Prog(nc, es)
        chs = _mk_chans(p)

        sub = contextlib.ExitStack()
        p.alloc_es, p.prefix = sub, "a_"
        emit_l1attn(nc, p, chs, xs2, wqkv, proj, wout, bout, ident_in, x3buf, n_b, S)
        p.barrier("tqvsg")
        sub.close()

        sub = contextlib.ExitStack()
        p.alloc_es, p.prefix = sub, "b_"
        emit_l1ffn_a(nc, p, chs, x3buf, g1, wdown, wqkvc, wup, cosr, sinr, cmask,
                     ident_in, xpbuf, n_b, S)
        p.barrier("tqvsg")
        sub.close()

        sub = contextlib.ExitStack()
        p.alloc_es, p.prefix = sub, "c_"
        emit_l1ffn_b(nc, p, chs, x3buf, xpbuf, g2, rw, w1, w3, w2, ident_in, xs4,
                     tfm, hidb, ybuf, n_b, S)
        p.g.wait(chs["chst"])
        p.finalize()
        sub.close()
    return nc


def _host_l1all_inputs(x2_full, P):
    wqkvc = np.ascontiguousarray(
        np.concatenate([P["L1_ffn_wq"], P["L1_ffn_wk"], P["L1_ffn_wv"]], axis=1))
    cosr, sinr = _rope_tables()
    cosr = cosr.astype(ml_dtypes.bfloat16)
    sinr = sinr.astype(ml_dtypes.bfloat16)
    in_maps = []
    for c in range(N_CORES):
        in_maps.append({
            "xs2": np.ascontiguousarray(x2_full[c * 4:(c + 1) * 4]),
            "wqkv": np.ascontiguousarray(P["L1_attn_w_qkv"]),
            "proj": np.ascontiguousarray(P["L1_attn_proj_matrix"]),
            "wout": np.ascontiguousarray(P["L1_attn_w_out"]),
            "bout": np.ascontiguousarray(P["L1_attn_b_out"][None, :]).astype(np.float32),
            "g1": np.ascontiguousarray(P["L1_ffn_g1"][None, :]),
            "wdown": np.ascontiguousarray(P["L1_ffn_w_down"]),
            "wqkvc": wqkvc,
            "wup": np.ascontiguousarray(P["L1_ffn_w_up"]),
            "cosr": cosr, "sinr": sinr, "cmask": _cmask_phases(),
            "g2": np.ascontiguousarray(P["L1_ffn_g2"][None, :]),
            "rw": np.ascontiguousarray(P["L1_ffn_router_w"]),
            "w1": np.ascontiguousarray(P["L1_ffn_w1"]),
            "w3": np.ascontiguousarray(P["L1_ffn_w3"]),
            "w2": np.ascontiguousarray(P["L1_ffn_w2"]),
            "ident": _ident128(),
        })
    return in_maps



def _ident128():
    return np.eye(128, dtype=ml_dtypes.bfloat16)


def _host_l0ffn_inputs(x1_full, P, n_steps=32):
    wtxT = np.ascontiguousarray(
        np.concatenate([P["L0_ffn_wf"], P["L0_ffn_wi"], P["L0_ffn_wc"]], axis=0).T)
    utT = np.ascontiguousarray(
        np.concatenate([P["L0_ffn_uf"], P["L0_ffn_ui"], P["L0_ffn_uc"]], axis=0).T)
    b3 = np.ascontiguousarray(
        np.concatenate([P["L0_ffn_bf"], P["L0_ffn_bi"], P["L0_ffn_bc"]])[None, :]).astype(np.float32)
    wot = np.ascontiguousarray(P["L0_ffn_wo"].T)
    uot = np.ascontiguousarray(P["L0_ffn_uo"].T)
    bo = np.ascontiguousarray(P["L0_ffn_bo"][None, :]).astype(np.float32)
    in_c = np.ascontiguousarray(P["L0_ffn_in_c"][None, :]).astype(np.float32)
    ident = _ident128()
    in_maps = []
    for c in range(N_CORES):
        xsc = np.ascontiguousarray(x1_full[:n_steps, c * 256:(c + 1) * 256, :])
        in_maps.append({
            "xs": xsc, "wtx": wtxT, "ut": utT, "b3": b3, "wot": wot, "uot": uot,
            "bo": bo, "in_c": in_c, "ident": ident,
        })
    return in_maps


def run_l0ffn(x1_full, P, n_steps=32):
    nc = build_l0ffn(n_steps=n_steps)
    in_maps = _host_l0ffn_inputs(x1_full, P, n_steps)
    res = run_bass_kernel_spmd(nc, in_maps, core_ids=list(range(N_CORES)))
    x2 = np.empty((n_steps, 2048, 1024), np.float32)
    for c in range(N_CORES):
        x2[:, c * 256:(c + 1) * 256, :] = res.results[c]["xs2"]
    return x2


_NC_CACHE = {}


def _get_nc(name, builder):
    if name not in _NC_CACHE:
        _NC_CACHE[name] = builder()
    return _NC_CACHE[name]


def kernel(**inputs):
    P = {k: np.ascontiguousarray(np.asarray(v, dtype=np.float32)) for k, v in inputs.items()}
    x0 = P["L0_attn_x"]

    nc1 = _get_nc("l0attn", build_l0attn)
    res = run_bass_kernel_spmd(nc1, _host_l0attn_inputs(x0, P), core_ids=list(range(N_CORES)))
    x1 = np.empty((32, 2048, 1024), np.float32)
    for c in range(N_CORES):
        x1[c * 4:(c + 1) * 4] = res.results[c]["xs1"]

    x2 = run_l0ffn(x1, P)

    nc3 = _get_nc("l1all", build_l1all)
    res = run_bass_kernel_spmd(nc3, _host_l1all_inputs(x2, P), core_ids=list(range(N_CORES)))
    x4 = np.empty((32, 2048, 1024), np.float32)
    for c in range(N_CORES):
        x4[c * 4:(c + 1) * 4] = res.results[c]["xs4"]
    return x4
